# version 22
# baseline (speedup 1.0000x reference)
import contextlib
import numpy as np
import ml_dtypes
import concourse.bass as bass
import concourse.mybir as mybir
from concourse.bass_utils import run_bass_kernel_spmd

F32 = mybir.dt.float32
BF16 = mybir.dt.bfloat16
AF = mybir.ActivationFunctionType
ALU = mybir.AluOpType
AX = mybir.AxisListType

BF = ml_dtypes.bfloat16

T = 2048
DM = 1024
NDC = DM // 128
TS = 512
NTS = T // TS
NTT = T // 128
NEG = -1.0e30

ENGINES = ("tensor", "vector", "scalar", "gpsimd", "sync")



class Node:
    __slots__ = ("eng", "sem_key", "cum")

    def __init__(self, eng, sem_key, cum):
        self.eng = eng
        self.sem_key = sem_key
        self.cum = cum


class Prog:
    def __init__(self, nc):
        self.nc = nc
        self.streams = {e: [] for e in ENGINES}
        self.sem_counts = {}
        self._last_dma = {}

    def op(self, eng, deps, emit_fn, inc=1, sem_key=None):
        key = sem_key or eng
        cum = self.sem_counts.get(key, 0) + inc
        self.sem_counts[key] = cum
        node = Node(eng, key, cum)
        flat = []

        def _fl(d):
            if d is None:
                return
            if isinstance(d, (list, tuple)):
                for e in d:
                    _fl(e)
            else:
                flat.append(d)
        _fl(deps)
        self.streams[eng].append((emit_fn, flat, node, inc))
        return node

    def dma(self, eng, deps, out, in_, sem_key=None, **kw):
        key = sem_key or (eng + "_dma")
        prev = self._last_dma.get(key)
        node = self.op(eng, list(deps) + ([prev] if prev else []),
                       lambda q: q.dma_start(out=out, in_=in_, **kw),
                       inc=16, sem_key=key)
        self._last_dma[key] = node
        return node

    def finalize(self, block, sems):
        for eng in ENGINES:
            stream = self.streams[eng]
            if not stream:
                continue

            def make(eng_name, stream_):
                def body(q):
                    waited = {}
                    for emit_fn, deps, node, inc in stream_:
                        for d in deps:
                            if d.eng == "tensor" and eng_name == "tensor":
                                continue
                            if waited.get(d.sem_key, 0) >= d.cum:
                                continue
                            q.wait_ge(sems[d.sem_key], d.cum)
                            waited[d.sem_key] = d.cum
                        inst = emit_fn(q)
                        inst.then_inc(sems[node.sem_key], inc)
                return body
            getattr(block, eng)(make(eng, stream))

    def run(self, es):
        sems = {}
        for key in sorted(self.sem_counts):
            sems[key] = es.enter_context(self.nc.semaphore(f"s_{key}"))
        block = es.enter_context(self.nc.Block())
        self.finalize(block, sems)


class Ctx:
    pass



def alloc_common(nc, es):
    C = Ctx()
    C.nc = nc
    C.P = Prog(nc)
    C.war = {}

    def sb(name, shape, dt):
        return es.enter_context(nc.sbuf_tensor(name, shape, dt))
    C.sb = sb

    C.x = [sb(f"x_res{i}", [128, T], F32) for i in range(NDC)]
    C.xh = [sb(f"xh{i}", [128, T], BF16) for i in range(NDC)]
    C.ps = [es.enter_context(nc.psum_tensor(f"ps{i}", [128, TS], F32))
            for i in range(8)]

    C.ident = sb("ident", [128, 128], BF16)
    C.ones128 = sb("ones128", [128, 1], F32)
    C.ones128b = sb("ones128b", [128, 1], BF16)
    C.onesrow_f = sb("onesrow_f", [1, 128], F32)

    C.c_eps = sb("eps_sb", [128, 1], F32)
    C.rowA = sb("rowA", [1, TS], F32)
    C.rowB = sb("rowB", [1, TS], F32)
    C.mean_bc = sb("mean_bc", [128, T], BF16)
    C.rstd_bc = sb("rstd_bc", [128, T], BF16)
    C.xsq = sb("xsq", [128, TS], BF16)
    C.tmp512b = sb("tmp512b", [128, TS], BF16)

    C.arena = sb("arena", [128, ARENA_BYTES // 2], BF16)
    C.arena_f = C.arena.bitcast(F32)

    C.row_scr = nc.dram_tensor("row_scratch", [4, T], F32, kind="Internal").ap()
    return C


ARENA_BYTES = 99328


class ArenaAlloc:
    def __init__(self, C):
        self.C = C
        self.off = 0

    def _take(self, nbytes):
        off = self.off
        self.off = (off + nbytes + 63) & ~63
        assert self.off <= ARENA_BYTES, f"arena overflow: {self.off}"
        return off

    def b16(self, parts, cols):
        off = self._take(2 * cols)
        return self.C.arena[0:parts, off // 2:off // 2 + cols]

    def f32(self, parts, cols):
        off = self._take(4 * cols)
        return self.C.arena_f[0:parts, off // 4:off // 4 + cols]


def alloc_ffn_views(C):
    a = ArenaAlloc(C)
    C.rwT = a.f32(128, NDC * 32)
    C.rb_rep = a.f32(128, 32)
    C.selneg = a.b16(32, 2 * 1024)
    C.b1_sb = a.f32(128, 64)
    C.lg = a.f32(128, 32)
    C.mx = a.f32(128, 1)
    C.s_rep = a.f32(128, 32)
    C.s_sb = C.s_rep[0:1, :]
    C.tmp32 = a.f32(128, 32)
    C.mean_cols = a.f32(128, NTT)
    C.rstd_cols = a.f32(128, NTT)
    C.ohtile = a.b16(128, 32)
    C.ohT = a.b16(32, T)
    C.iohT = a.b16(32, T)
    C.b2_bf = a.b16(32, DM)
    C.w1g = a.b16(128, 8 * NDC * 256)
    C.w2g = a.b16(128, 8 * 2 * DM)
    C.hall = a.b16(128, 16 * TS)
    C.selgrp = a.b16(32, 4 * 128)


def load_consts(C, h):
    P = C.P
    n = []
    n.append(P.dma("sync", [], C.ident[:, :], h["c_ident"], sem_key="c_dma"))
    n.append(P.dma("sync", [], C.ones128[:, :], h["c_ones_f"], sem_key="c_dma"))
    n.append(P.dma("sync", [], C.ones128b[:, :], h["c_ones_b"], sem_key="c_dma"))
    n.append(P.dma("sync", [], C.onesrow_f[:, :], h["c_onesrow_f"], sem_key="c_dma"))
    n.append(P.dma("sync", [], C.c_eps[:, :], h["c_eps"], sem_key="c_dma"))
    C.nc.const_aps.register(F32, 1e-5, C.c_eps[:, 0:1])
    C.const_nodes = n
    return n



def build_ln(C, x_ready, xh_war, tag):
    nc, P = C.nc, C.P
    war_ps = [C.war.get(f"ps{i}") for i in range(8)]

    sumA = []
    sumB = []
    xsq_war = [C.war.get("xsq"), C.war.get("rstd_bc")]
    xb_war = [C.war.get("rstd_bc"), C.war.get("rstd_bc")]
    for ts in range(NTS):
        tsl = slice(ts * TS, (ts + 1) * TS)

        n_mm_last = None
        n_mmA_last = None
        for dc in range(NDC):
            sqb = [C.xsq[:, :], C.rstd_bc[:, 0:TS]][dc % 2]
            xbb = [C.rstd_bc[:, TS:2 * TS], C.rstd_bc[:, 2 * TS:3 * TS]][dc % 2]
            n_xb = P.op("vector", [x_ready[dc], xb_war[dc % 2]],
                        lambda q, dc=dc, tsl=tsl, xbb=xbb:
                        nc.vector.tensor_copy(xbb, C.x[dc][:, tsl]))
            n_mmA = P.op("tensor", [n_xb, war_ps[4 + ts]],
                         lambda q, dc=dc, ts=ts, xbb=xbb: nc.tensor.matmul(
                             C.ps[4 + ts][0:1, :], C.ones128b[:, :], xbb,
                             start=(dc == 0), stop=(dc == NDC - 1)))
            xb_war[dc % 2] = n_mmA
            n_mmA_last = n_mmA
            n_sq = P.op("scalar",
                        [x_ready[dc], xsq_war[dc % 2]],
                        lambda q, dc=dc, tsl=tsl, sqb=sqb:
                        nc.scalar.activation(sqb, C.x[dc][:, tsl], AF.Square))
            n_mm = P.op("tensor", [n_sq, war_ps[ts]],
                        lambda q, dc=dc, ts=ts, sqb=sqb: nc.tensor.matmul(
                            C.ps[ts][0:1, :], C.ones128b[:, :], sqb,
                            start=(dc == 0), stop=(dc == NDC - 1)))
            xsq_war[dc % 2] = n_mm
            n_mm_last = n_mm
        sumA.append(n_mmA_last)
        sumB.append(n_mm_last)
    C.war["xsq"] = xsq_war[0]
    C.war["rstd_bc"] = [C.war.get("rstd_bc"), xsq_war[1]]

    key = f"lnrow_{tag}"
    scr_war = C.war.get("row_scr")
    out_dmas = []
    rowA_war = C.war.get("rowA")
    rowB_war = C.war.get("rowB")
    for ts in range(NTS):
        tsl = slice(ts * TS, (ts + 1) * TS)
        n_mean = P.op("scalar", [sumA[ts], rowA_war],
                      lambda q, ts=ts: nc.scalar.activation(
                          C.rowA[:, :], C.ps[4 + ts][0:1, :], AF.Copy,
                          scale=1.0 / DM))
        n_msq = P.op("vector", [n_mean, rowB_war],
                     lambda q: nc.vector.tensor_tensor(
                         out=C.rowB[:, :], in0=C.rowA[:, :],
                         in1=C.rowA[:, :], op=ALU.mult))
        n_var = P.op("vector", [n_msq, sumB[ts]],
                     lambda q, ts=ts: nc.vector.scalar_tensor_tensor(
                         out=C.rowB[:, :], in0=C.ps[ts][0:1, :],
                         scalar=1.0 / DM, in1=C.rowB[:, :],
                         op0=ALU.mult, op1=ALU.subtract))
        n_ln = P.op("scalar", [n_var] + C.const_nodes, lambda q: nc.scalar.activation(
            C.rowB[:, :], C.rowB[:, :], AF.Ln, bias=1e-5))
        n_ex = P.op("scalar", [n_ln], lambda q: nc.scalar.activation(
            C.rowB[:, :], C.rowB[:, :], AF.Exp, scale=-0.5))
        n_d1 = P.dma("gpsimd", [n_mean, scr_war], C.row_scr[0:1, tsl],
                     C.rowA[:, :], sem_key=key)
        n_d2 = P.dma("gpsimd", [n_ex, scr_war], C.row_scr[1:2, tsl],
                     C.rowB[:, :], sem_key=key)
        rowA_war = n_d1
        rowB_war = n_d2
        out_dmas += [n_d1, n_d2]
        C.war[f"ps{ts}"] = n_var
        C.war[f"ps{4+ts}"] = n_mean
    C.war["rowA"] = rowA_war
    C.war["rowB"] = rowB_war
    C.ln_row_dmas = out_dmas

    n_b1 = P.dma("gpsimd", out_dmas + [C.war.get("mean_bc")], C.mean_bc[:, :],
                 C.row_scr[0:1, :].broadcast_to([128, T]), sem_key=key)
    n_b2 = P.dma("gpsimd", out_dmas + [C.war.get("rstd_bc")], C.rstd_bc[:, :],
                 C.row_scr[1:2, :].broadcast_to([128, T]), sem_key=key)
    C.war["row_scr"] = [n_b1, n_b2]

    out = []
    for dc in range(NDC):
        n_sub = P.op("vector", [n_b1, x_ready[dc]] + xh_war,
                     lambda q, dc=dc: nc.vector.tensor_tensor(
                         out=C.xh[dc][:, :], in0=C.x[dc][:, :],
                         in1=C.mean_bc[:, :], op=ALU.subtract))
        n_mul = P.op("vector", [n_sub, n_b2],
                     lambda q, dc=dc: nc.vector.tensor_tensor(
                         out=C.xh[dc][:, :], in0=C.xh[dc][:, :],
                         in1=C.rstd_bc[:, :], op=ALU.mult))
        out.append(n_mul)
    C.war["mean_bc"] = list(out)
    C.war["rstd_bc"] = list(out)
    return out, n_b1, n_b2



def build_ffn(C, xh_ready, x_free, w, tag, ln_rows, cached=False):
    nc, P = C.nc, C.P
    key = f"ffnw_{tag}"
    x_ready = ln_rows[0]
    arena_war = C.war.get("arena")

    ld_rw = P.dma("sync", [C.war.get("rwT"), arena_war],
                  C.rwT[:, :].rearrange("p (c e) -> p c e", e=32),
                  w["rwT"].rearrange("(c p) e -> p c e", p=128), sem_key=key)
    ld_rb = P.dma("sync", [C.war.get("rb_rep"), arena_war], C.rb_rep[:, :],
                  w["rb_rep"], sem_key=key)
    ld_b1 = P.dma("sync", [C.war.get("b1_sb"), arena_war], C.b1_sb[:, :],
                  w["b1r"], sem_key=key)

    def emit_srow(q):
        last = None
        for dc in range(NDC):
            last = nc.tensor.matmul(C.ps[1][0:1, 0:32], C.ones128[:, :],
                                    C.rwT[:, dc * 32:(dc + 1) * 32],
                                    start=(dc == 0), stop=(dc == NDC - 1))
        return last
    n_smm = P.op("tensor", [ld_rw, C.war.get("ps1")], emit_srow)
    n_sev = P.op("scalar", [n_smm, C.war.get("s_sb")],
                 lambda q: nc.scalar.activation(C.s_sb[:, :],
                                                C.ps[1][0:1, 0:32], AF.Copy))
    n_srep_mm = P.op("tensor", [n_sev, C.war.get("ps2")],
                     lambda q: nc.tensor.matmul(
                         C.ps[2][:, 0:32], C.onesrow_f[:, :], C.s_sb[:, :],
                         start=True, stop=True))
    n_srep = P.op("vector", [n_srep_mm, C.war.get("s_rep")],
                  lambda q: nc.vector.tensor_copy(C.s_rep[:, :],
                                                  C.ps[2][:, 0:32]))
    C.war["ps2"] = n_srep
    C.war["s_sb"] = n_srep_mm

    n_mc = P.dma("gpsimd", C.ln_row_dmas + [C.war.get("mean_cols")],
                 C.mean_cols[:, :],
                 C.row_scr[0].rearrange("(c p) -> p c", p=128),
                 sem_key=key + "g", allow_slow_non_contiguous=True)
    n_rc = P.dma("gpsimd", C.ln_row_dmas + [C.war.get("rstd_cols")],
                 C.rstd_cols[:, :],
                 C.row_scr[1].rearrange("(c p) -> p c", p=128),
                 sem_key=key + "g", allow_slow_non_contiguous=True)
    C.war["row_scr"] = [C.war.get("row_scr"), n_mc, n_rc]

    oh_nodes = []
    lg_free = C.war.get("lg")
    ps0_war = C.war.get("ps0")
    ps1_war = n_srep_mm
    tmp32_war = C.war.get("tmp32")
    for tt in range(NTT):
        ttl = slice(tt * 128, (tt + 1) * 128)

        def emit_router(q, ttl=ttl):
            last = None
            for dc in range(NDC):
                last = nc.tensor.matmul(C.ps[0][:, 0:32], C.x[dc][:, ttl],
                                        C.rwT[:, dc * 32:(dc + 1) * 32],
                                        start=(dc == 0), stop=(dc == NDC - 1))
            return last
        n_mm = P.op("tensor", x_ready + [ld_rw, ps0_war], emit_router)
        n_t1 = P.op("vector", [n_mc, n_srep, tmp32_war],
                    lambda q, tt=tt: nc.vector.tensor_scalar(
                        out=C.tmp32[:, :], in0=C.s_rep[:, :],
                        scalar1=C.mean_cols[:, tt:tt + 1], scalar2=None,
                        op0=ALU.mult))
        n_t2 = P.op("vector", [n_t1, n_mm, lg_free],
                    lambda q: nc.vector.tensor_tensor(
                        out=C.lg[:, :], in0=C.ps[0][:, 0:32],
                        in1=C.tmp32[:, :], op=ALU.subtract))
        ps0_war = n_t2
        tmp32_war = n_t2
        n_lg = P.op("vector", [n_t2, ld_rb, n_rc],
                    lambda q, tt=tt: nc.vector.scalar_tensor_tensor(
                        out=C.lg[:, :], in0=C.lg[:, :],
                        scalar=C.rstd_cols[:, tt:tt + 1],
                        in1=C.rb_rep[:, :], op0=ALU.mult, op1=ALU.add))
        n_mx = P.op("vector", [n_lg, C.war.get("mx")],
                    lambda q: nc.vector.reduce_max(C.mx[:, :], C.lg[:, :],
                                                   axis=AX.X))
        C.war["mx"] = n_mx
        n_oh = P.op("vector", [n_mx, C.war.get("ohtile")],
                    lambda q: nc.vector.tensor_scalar(
                        out=C.ohtile[:, :], in0=C.lg[:, :],
                        scalar1=C.mx[:, 0:1], scalar2=None, op0=ALU.is_equal))
        lg_free = n_oh
        n_tr = P.op("tensor", [n_oh, ps1_war],
                    lambda q: nc.tensor.transpose(
                        C.ps[1].bitcast(BF16)[0:32, 0:128], C.ohtile[:, :],
                        C.ident[:, :]))
        C.war["ohtile"] = n_tr
        n_ev = P.op("vector", [n_tr, C.war.get("ohT")],
                    lambda q, ttl=ttl: nc.vector.tensor_copy(
                        C.ohT[:, ttl], C.ps[1].bitcast(BF16)[0:32, 0:128]))
        ps1_war = n_ev
        oh_nodes.append(n_ev)
    C.war["lg"] = lg_free
    C.war["ps0"] = ps0_war
    C.war["ps1"] = ps1_war
    C.war["tmp32"] = tmp32_war
    C.war["s_rep"] = oh_nodes
    C.war["mean_cols"] = oh_nodes
    C.war["rstd_cols"] = oh_nodes
    n_ioh = P.op("vector", oh_nodes + [C.war.get("iohT")],
                 lambda q: nc.vector.tensor_scalar(
                     out=C.iohT[:, :], in0=C.ohT[:, :], scalar1=0.0,
                     scalar2=None, op0=ALU.is_equal))

    ld_sg = P.dma("gpsimd", [C.war.get("selgrp")], C.selgrp[:, :], w["selgrp"],
                  sem_key=key + "g")
    ld_b2 = P.dma("gpsimd", [C.war.get("b2_bf"), arena_war], C.b2_bf[:, :],
                  w["b2"], sem_key=key + "g")

    ps_free = [C.war.get(f"ps{i}") for i in range(8)]
    hall_war = [C.war.get("hall")] * 16
    acc_state = [[None] * NTS for _ in range(NDC)]
    x_nodes = [list(xf) if isinstance(xf, list) else ([xf] if xf else [])
               for xf in x_free]
    group_consumers = [C.war.get("w1g"), C.war.get("w2g")]
    tmpb_war = C.war.get("tmp512b")

    pcf = w.get("pc", []) if cached else []
    w1_war = [list(group_consumers) + pcf for _ in range(8)]
    w2_war = [list(group_consumers) + pcf for _ in range(8)]
    sel_war = [[C.war.get("selneg"), arena_war],
               [C.war.get("selneg"), arena_war]]
    for eg in range(4):
        g_ld = []
        cache_stores = [[] for _ in range(8)]
        selv = C.selneg[:, (eg % 2) * 1024:(eg % 2) * 1024 + 1024]
        ld_selg = P.dma("gpsimd", sel_war[eg % 2], selv,
                        w["selneg"][:, eg * 1024:(eg + 1) * 1024],
                        sem_key=f"ffng_{tag}")
        sel_war[eg % 2] = None
        for ei in range(8):
            e = eg * 8 + ei
            w1slot = C.w1g[:, :].rearrange("p (s c h) -> p s c h",
                                           s=8, c=NDC)[:, ei]
            w2slot = C.w2g[:, :].rearrange("p (s c d) -> p s c d",
                                           s=8, c=2)[:, ei]
            if cached:
                n1 = P.dma("sync" if ei % 2 else "scalar", w1_war[ei],
                           w1slot, w["w1c"][e],
                           sem_key=f"ffc{ei % 4}_{tag[:2]}")
                n2 = P.dma("scalar" if ei % 2 else "sync", w2_war[ei],
                           w2slot, w["w2c"][e],
                           sem_key=f"ffd{ei % 4}_{tag[:2]}")
            else:
                n1 = P.dma("gpsimd", w1_war[ei],
                           w1slot, w["w1"][e].rearrange("(c p) h -> p c h",
                                                        p=128),
                           sem_key=f"ffng{ei % 4}_{tag}")
                n2 = P.dma("gpsimd", w2_war[ei],
                           w2slot, w["w2"][e].rearrange("(c p) d -> p c d",
                                                        p=128),
                           sem_key=f"ffnh{ei % 4}_{tag}")
                cache_stores[ei].append(P.dma(
                    "sync" if ei % 2 else "scalar", [n1],
                    w["w1c"][e], w1slot, sem_key=f"ffc{ei % 4}_{tag[:2]}"))
                cache_stores[ei].append(P.dma(
                    "scalar" if ei % 2 else "sync", [n2],
                    w["w2c"][e], w2slot, sem_key=f"ffd{ei % 4}_{tag[:2]}"))
            g_ld += [n1, n2]
        group_consumers = []

        for ts in range(NTS):
            tsl = slice(ts * TS, (ts + 1) * TS)
            ha_nodes = []
            ha_mms = []
            for ei in range(8):
                e = eg * 8 + ei
                pp = ei % 2
                for ht in range(2):
                    b = pp * 2 + ht
                    psA = C.ps[b]

                    def emitA(q, e=e, ei=ei, ht=ht, psA=psA, tsl=tsl,
                              selv=selv):
                        for dc in range(NDC):
                            nc.tensor.matmul(
                                psA[:, :],
                                C.w1g[:, (ei * NDC + dc) * 256 + ht * 128:
                                      (ei * NDC + dc) * 256 + ht * 128 + 128],
                                C.xh[dc][:, tsl], start=(dc == 0), stop=False)
                        return nc.tensor.matmul(
                            psA[:, :], selv[:, ei * 128:(ei + 1) * 128],
                            C.iohT[:, tsl], start=False, stop=True)
                    n_mmA = P.op("tensor",
                                 [g_ld[2 * ei], n_ioh, ps_free[b], ld_selg]
                                 + xh_ready,
                                 emitA)
                    n_ev = P.op("scalar",
                                [n_mmA, ld_b1, hall_war[ei * 2 + ht]],
                                lambda q, ei=ei, ht=ht, e=e, psA=psA:
                                nc.scalar.activation(
                                    C.hall[:, (ei * 2 + ht) * TS:
                                           (ei * 2 + ht + 1) * TS],
                                    psA[:, :], AF.Relu,
                                    bias=C.b1_sb[:, e * 2 + ht:e * 2 + ht + 1]))
                    ps_free[b] = n_ev
                    ha_nodes.append(n_ev)
                    ha_mms.append(n_mmA)

            bank = lambda dt: (4 + dt) if dt < 4 else (dt - 4)
            ei_nodes = []
            for ei in range(8):

                def emit_ei(q, ei=ei):
                    last = None
                    for dt in range(NDC):
                        for hc in range(2):
                            last = nc.tensor.matmul(
                                C.ps[bank(dt)][:, :],
                                C.w2g[:, (ei * 2 + hc) * DM + dt * 128:
                                      (ei * 2 + hc) * DM + dt * 128 + 128],
                                C.hall[:, (ei * 2 + hc) * TS:
                                       (ei * 2 + hc + 1) * TS],
                                start=(ei == 0 and hc == 0),
                                stop=False)
                    return last
                bdeps = [ha_nodes[2 * ei], ha_nodes[2 * ei + 1],
                         g_ld[2 * ei + 1]]
                if ei == 0:
                    bdeps += ps_free + ha_nodes
                n_ei = P.op("tensor", bdeps, emit_ei)
                ei_nodes.append(n_ei)
            mmB_all = []
            for dt in range(NDC):
                b = bank(dt)
                psX = C.ps[b]

                def emit_fin(q, eg=eg, dt=dt, psX=psX, tsl=tsl):
                    nc.tensor.matmul(
                        psX[:, :], C.b2_bf[:, dt * 128:(dt + 1) * 128],
                        C.ohT[:, tsl], start=False, stop=False)
                    return nc.tensor.matmul(
                        psX[:, :], C.selgrp[:, eg * 128:(eg + 1) * 128],
                        C.ohT[:, tsl], start=False, stop=True)
                n_mmB = P.op("tensor",
                             ei_nodes + [ld_b2, ld_sg] + oh_nodes, emit_fin)
                mmB_all.append(n_mmB)
                n_rel = P.op("scalar", [n_mmB, tmpb_war],
                             lambda q, psX=psX: nc.scalar.activation(
                                 C.tmp512b[:, :], psX[:, :],
                                 AF.Relu, scale=0.5))
                ps_free[b] = n_rel
                n_add = P.op("vector", [n_rel] + x_nodes[dt],
                             lambda q, dt=dt, tsl=tsl:
                             nc.vector.tensor_tensor(
                                 out=C.x[dt][:, tsl],
                                 in0=C.x[dt][:, tsl],
                                 in1=C.tmp512b[:, :], op=ALU.add))
                tmpb_war = n_add
                acc_state[dt][ts] = n_add
                x_nodes[dt] = [n_add]
            for s in range(16):
                hall_war[s] = [ei_nodes[s // 2]]
            group_consumers = mmB_all + ha_nodes
            if ts == NTS - 1:
                for ei in range(8):
                    w1_war[ei] = [ha_nodes[2 * ei], ha_nodes[2 * ei + 1]] \
                        + cache_stores[ei]
                    w2_war[ei] = [ei_nodes[ei]] + cache_stores[ei]
                sel_war[eg % 2] = list(ha_nodes)

    C.war["w1g"] = group_consumers
    C.war["w2g"] = group_consumers
    C.war["hall"] = group_consumers
    C.war["tmp512b"] = tmpb_war
    C.war["rwT"] = oh_nodes
    C.war["rb_rep"] = oh_nodes
    C.war["negs"] = oh_nodes
    C.war["rstd_cols"] = oh_nodes
    C.war["r_mean"] = [C.war.get("r_mean")] + oh_nodes
    C.war["r_rstd"] = [C.war.get("r_rstd")] + oh_nodes
    C.war["selgrp"] = group_consumers
    C.war["selneg"] = sel_war[0] + sel_war[1]
    C.war["b1_sb"] = group_consumers
    C.war["ohT"] = group_consumers
    C.war["iohT"] = group_consumers
    C.war["b2_bf"] = group_consumers
    for i in range(8):
        C.war[f"ps{i}"] = ps_free[i]
    C.war["arena"] = [acc_state[dt][NTS - 1] for dt in range(NDC)]
    return [[acc_state[dt][ts] for ts in range(NTS)] for dt in range(NDC)]


def _ffn_host(prefix, inputs):
    rw = inputs[f"{prefix}_router_w"]
    rb = inputs[f"{prefix}_router_b"]
    w1 = inputs[f"{prefix}_weights1"]
    b1 = inputs[f"{prefix}_bias1"]
    w2 = inputs[f"{prefix}_weights2"]
    b2 = inputs[f"{prefix}_bias2"]
    sel = np.zeros((32, 32 * 128), dtype=BF)
    for e in range(32):
        sel[e, e * 128:(e + 1) * 128] = BF(NEG)
    selg = np.zeros((32, 4 * 128), dtype=BF)
    for g in range(4):
        for e in range(32):
            if e // 8 != g:
                selg[e, g * 128:(g + 1) * 128] = BF(NEG)
    return {
        "selgrp": selg,
        "rwT": np.ascontiguousarray(rw.T),
        "rb_rep": np.ascontiguousarray(
            np.broadcast_to(rb[None, :], (128, 32))),
        "selneg": sel,
        "w1": np.ascontiguousarray(w1),
        "b1r": np.ascontiguousarray(
            b1.reshape(32, 2, 128).transpose(2, 0, 1).reshape(128, 64)),
        "w2": np.ascontiguousarray(w2),
        "b2": np.ascontiguousarray(b2),
    }


def _const_host():
    return {
        "c_ident": np.eye(128, dtype=BF),
        "c_ones_f": np.ones((128, 1), dtype=np.float32),
        "c_ones_b": np.ones((128, 1), dtype=BF),
        "c_onesrow_f": np.ones((1, 128), dtype=np.float32),
        "c_eps": np.full((128, 1), 1e-5, dtype=np.float32),
    }


def declare_ffn_dram(nc, tag):
    w = {}
    w["rwT"] = nc.dram_tensor(f"{tag}_rwT", [DM, 32], F32,
                              kind="ExternalInput").ap()
    w["rb_rep"] = nc.dram_tensor(f"{tag}_rb_rep", [128, 32], F32,
                                 kind="ExternalInput").ap()
    w["selneg"] = nc.dram_tensor(f"{tag}_selneg", [32, 32 * 128], BF16,
                                 kind="ExternalInput").ap()
    w1 = nc.dram_tensor(f"{tag}_w1", [32, DM, 256], F32,
                        kind="ExternalInput").ap()
    w["w1"] = [w1[e] for e in range(32)]
    w["b1r"] = nc.dram_tensor(f"{tag}_b1r", [128, 64], F32,
                              kind="ExternalInput").ap()
    w2 = nc.dram_tensor(f"{tag}_w2", [32, 256, DM], F32,
                        kind="ExternalInput").ap()
    w["w2"] = [w2[e] for e in range(32)]
    w["b2"] = nc.dram_tensor(f"{tag}_b2", [32, DM], F32,
                             kind="ExternalInput").ap()
    w["selgrp"] = nc.dram_tensor(f"{tag}_selgrp", [32, 4 * 128], BF16,
                                 kind="ExternalInput").ap()
    w["w1c"] = nc.dram_tensor(f"{tag}_w1c", [32, 128, NDC * 256], BF16,
                              kind="Internal").ap()
    w["w2c"] = nc.dram_tensor(f"{tag}_w2c", [32, 128, 2 * DM], BF16,
                              kind="Internal").ap()
    return w


def declare_const_dram(nc):
    h = {}
    h["c_ident"] = nc.dram_tensor("c_ident", [128, 128], BF16,
                                  kind="ExternalInput").ap()
    h["c_ones_f"] = nc.dram_tensor("c_ones_f", [128, 1], F32,
                                   kind="ExternalInput").ap()
    h["c_ones_b"] = nc.dram_tensor("c_ones_b", [128, 1], BF16,
                                   kind="ExternalInput").ap()
    h["c_onesrow_f"] = nc.dram_tensor("c_onesrow_f", [1, 128], F32,
                                      kind="ExternalInput").ap()
    h["c_eps"] = nc.dram_tensor("c_eps", [128, 1], F32,
                                kind="ExternalInput").ap()
    return h



def load_x(C, xin_ap):
    P = C.P
    nodes = []
    for dc in range(NDC):
        nodes.append(P.dma("sync", [], C.x[dc][:, :],
                           xin_ap[dc * 128:(dc + 1) * 128, :], sem_key="x_dma"))
    return nodes


def store_x(C, xout_ap, x_nodes_per_dc):
    P = C.P
    st = []
    for dc in range(NDC):
        st.append(P.dma("sync", x_nodes_per_dc[dc], xout_ap[dc * 128:(dc + 1) * 128, :],
                        C.x[dc][:, :], sem_key="xout_dma"))
    P.op("gpsimd", st, lambda q: q.engine_nop())
    return st



def build_ffn_sublayer_program():
    nc = bass.Bass("TRN2", target_bir_lowering=False)
    es = contextlib.ExitStack()
    xin = nc.dram_tensor("xin", [DM, T], F32, kind="ExternalInput").ap()
    xout = nc.dram_tensor("xout", [DM, T], F32, kind="ExternalOutput").ap()
    hconst = declare_const_dram(nc)
    w = declare_ffn_dram(nc, "f")
    C = alloc_common(nc, es)
    alloc_ffn_views(C)
    load_consts(C, hconst)
    x_ready = load_x(C, xin)
    xh_nodes, n_meanbc, n_rstdbc = build_ln(C, x_ready, [], "ln0")
    x_upd = build_ffn(C, xh_nodes, x_ready, w, "f",
                      (x_ready, n_meanbc, n_rstdbc))
    store_x(C, xout, x_upd)
    C.P.run(es)
    es.close()
    return nc


def ffn_sublayer_inputs(prefix, inputs, xT):
    d = {"xin": xT}
    d.update(_const_host())
    wh = _ffn_host(prefix, inputs)
    return d | {f"f_{k}": wh[k] for k in
                ("rwT", "rb_rep", "selneg", "selgrp", "w1", "b1r", "w2", "b2")}


def build_fused_program():
    nc = bass.Bass("TRN2", target_bir_lowering=False)
    es = contextlib.ExitStack()
    xin = nc.dram_tensor("xin", [DM, T], F32, kind="ExternalInput").ap()
    xout = nc.dram_tensor("xout", [DM, T], F32, kind="ExternalOutput").ap()
    hconst = declare_const_dram(nc)
    w_f0 = declare_ffn_dram(nc, "f0")
    w_f1 = declare_ffn_dram(nc, "f1")
    w_m = declare_mamba_dram(nc, "dm0")
    w_a = declare_attn_dram(nc, "da1")
    C = alloc_common(nc, es)
    alloc_ffn_views(C)
    alloc_mamba_views(C)
    alloc_attn_views(C)
    load_consts(C, hconst)
    x_nodes = load_x(C, xin)

    x_pd = [[n] for n in x_nodes]

    steps = [("ffn", w_f0, "f0a", False), ("mamba", w_m, "m0", False),
             ("ffn", w_f0, "f0b", True), ("ffn", w_f1, "f1a", True),
             ("attn", w_a, "a1", True), ("ffn", w_f1, "f1b", True)]
    for i, (kind, w, tag, cached) in enumerate(steps):
        if kind == "mamba":
            fpc = []
            for e in range(32):
                fpc.append(C.P.dma(
                    "gpsimd", [],
                    w_f1["w1c"][e].rearrange("p (c h) -> p c h", h=256),
                    w_f1["w1"][e].rearrange("(c p) h -> p c h", p=128),
                    sem_key=f"fpc{e % 4}"))
                fpc.append(C.P.dma(
                    "gpsimd", [],
                    w_f1["w2c"][e].rearrange("p (c d) -> p c d", d=DM),
                    w_f1["w2"][e].rearrange("(c p) d -> p c d", p=128),
                    sem_key=f"fpd{e % 4}"))
            w_f1["pc"] = fpc
        xh_nodes, _, _ = build_ln(C, x_pd, [x_pd], f"ln{i}")
        if kind == "ffn":
            x_pd = build_ffn(C, xh_nodes, x_pd, w, tag, (x_pd, None, None),
                             cached=cached)
            if tag == "f0a" and False:
                pc = []
                for k in range(4):
                    pc.append(C.P.dma(
                        "gpsimd", [],
                        w_m["winTc"][:, :, k * 1024:(k + 1) * 1024],
                        w_m["winT"].rearrange("(c p) e -> p c e",
                                              p=128)[:, :, k * 1024:(k + 1) * 1024],
                        sem_key=f"mpcw{k % 2}"))
                for k in range(2):
                    pc.append(C.P.dma(
                        "gpsimd", [],
                        w_m["woTc"][:, k * 8:(k + 1) * 8, :],
                        w_m["woT"].rearrange("(c p) f -> p c f",
                                             p=128)[:, k * 8:(k + 1) * 8, :],
                        sem_key=f"mpco{k % 2}"))
                pc.append(C.P.dma(
                    "gpsimd", [], w_m["wxTc"],
                    w_m["wxT"].rearrange("(c p) f -> p c f", p=128),
                    sem_key="mpco0"))
                w_m["pc"] = pc
        elif kind == "mamba":
            x_pd = build_mamba(C, xh_nodes, x_pd, w, tag, cached=False)
        else:
            x_pd = build_attn(C, xh_nodes, x_pd, w, tag, cached=True)
        if kind == "ffn" and tag == "f0b":
            pc = []
            for j, n in enumerate(("wqT", "wkT", "wvT", "wgT", "woT")):
                pc.append(C.P.dma(
                    "gpsimd", [], w_a[n + "c"],
                    w_a[n].rearrange("(c p) f -> p c f", p=128),
                    sem_key=f"apc{j % 2}"))
            w_a["pc"] = pc
    store_x(C, xout, x_pd)
    C.P.run(es)
    es.close()
    return nc


def fused_inputs(inputs, xT):
    d = {"xin": xT}
    d.update(_const_host())
    for tag, prefix in (("f0", "L0_ffn"), ("f1", "L1_ffn")):
        wh = _ffn_host(prefix, inputs)
        d.update({f"{tag}_{k}": wh[k] for k in
                  ("rwT", "rb_rep", "selneg", "selgrp", "w1", "b1r", "w2", "b2")})
    wh = _mamba_host("L0_attn", inputs)
    d.update({f"dm0_{k}": wh[k] for k in wh})
    wh = _attn_host("L1_attn", inputs)
    d.update({f"da1_{k}": wh[k] for k in wh})
    return d


_PROG_CACHE = {}


def _get_prog(name, builder):
    if name not in _PROG_CACHE:
        _PROG_CACHE[name] = builder()
    return _PROG_CACHE[name]


def kernel(**inputs):
    x = inputs["L0_attn_x"]
    prog = _get_prog("fused", build_fused_program)
    in_maps = [fused_inputs(inputs, np.ascontiguousarray(x[b].T))
               for b in range(8)]
    res = run_bass_kernel_spmd(prog, in_maps, core_ids=list(range(8)))
    return np.stack([np.asarray(res.results[b]["xout"]).T
                     for b in range(8)]).astype(np.float32)

DI = 2048
NET = DI // 128
DS = 64
QC = 128
NCH = T // QC
QS = 16
TH = 1024


def alloc_mamba_views(C):
    a = ArenaAlloc(C)
    m_xc_all = a.b16(128, NET * (4 + TH))
    C.m_xc = [m_xc_all[:, i * (4 + TH):(i + 1) * (4 + TH)] for i in range(NET)]
    silz1 = a.b16(128, NET * 128)
    C.m_silz = [silz1, silz1]
    C.m_hbf = a.b16(64, DI)
    dt2 = a.b16(64, 2 * QC)
    C.m_dt = [dt2[:, 0:QC], dt2[:, QC:2 * QC]]
    c2 = a.b16(64, 2 * QC)
    C.m_C = [c2[:, 0:QC], c2[:, QC:2 * QC]]
    bd2 = a.b16(64, 2 * QC)
    C.m_Bd = [bd2[:, 0:QC], bd2[:, QC:2 * QC]]
    C.m_woT = a.b16(128, NET * 1024)
    C.m_wxT = a.b16(128, NET * 132)
    C.m_win = a.b16(128, 2 * NDC * 128)
    C.m_cw = a.f32(128, NET * 4)
    C.m_cb = a.f32(128, NET)
    C.m_D = a.f32(128, NET)
    C.m_Alog = a.f32(64, 1)
    C.m_negA = C.m_Alog
    C.m_trimask = a.b16(64, 256)
    C.m_xtail = a.b16(128, NET * 4)
    C.m_convt = C.tmp512b
    C.m_zst = [C.xsq, a.b16(128, TS)]
    C.m_e2 = a.b16(64, 8 * 256)
    def dbl_f(parts, cols):
        v = a.f32(parts, 2 * cols)
        return [v[:, 0:cols], v[:, cols:2 * cols]]

    def dbl_b(parts, cols):
        v = a.b16(parts, 2 * cols)
        return [v[:, 0:cols], v[:, cols:2 * cols]]
    C.m_cum = dbl_f(64, QC)
    C.m_t64a = dbl_f(64, QC)
    C.m_t64b = dbl_b(64, QC)
    C.m_Bst = dbl_b(64, QC)
    C.m_Cst = dbl_b(64, QC)
    bhts = a.b16(128, 64)
    C.m_BhT = [bhts, bhts]
    C.m_Chl = dbl_b(64, 112)
    C.m_Bhr = dbl_b(64, 112)
    C.m_expdb = dbl_f(64, 8)
    C.m_de = dbl_f(64, 1)
    C.m_gt = a.b16(128, QC)
    C.m_band = C.m_win[64:65, 0:8 * 256]
    C.m_xcT = a.b16(128, NET * 128)
    C.m_ych = C.m_xcT
    C.m_yt = a.b16(128, 128)
    nc = C.nc
    kind = "ExternalOutput" if getattr(C, "debug", False) else "Internal"
    C.m_zscr = nc.dram_tensor("m_zscr", [DI, T], BF16, kind=kind).ap()
    C.m_bandscr = nc.dram_tensor("m_bandscr", [8 * 256], BF16,
                                 kind="Internal").ap()
    if getattr(C, "debug", False):
        C.m_dbg_gt = nc.dram_tensor("m_dbg_gt", [NCH, QC, QC], BF16,
                                    kind="ExternalOutput").ap()
        C.m_dbg_cum = nc.dram_tensor("m_dbg_cum", [NCH, 64, QC], F32,
                                     kind="ExternalOutput").ap()
    C.m_scr2 = nc.dram_tensor("m_scr2", [3, 64, T], BF16, kind=kind).ap()


def declare_mamba_dram(nc, tag):
    w = {}
    w["winT"] = nc.dram_tensor(f"{tag}_winT", [DM, 2 * DI], F32,
                               kind="ExternalInput").ap()
    w["wxT"] = nc.dram_tensor(f"{tag}_wxT", [DI, 129], F32,
                              kind="ExternalInput").ap()
    w["woT"] = nc.dram_tensor(f"{tag}_woT", [DI, DM], F32,
                              kind="ExternalInput").ap()
    w["cw"] = nc.dram_tensor(f"{tag}_cw", [128, NET * 4], F32,
                             kind="ExternalInput").ap()
    w["cb"] = nc.dram_tensor(f"{tag}_cb", [128, NET], F32,
                             kind="ExternalInput").ap()
    w["D"] = nc.dram_tensor(f"{tag}_D", [128, NET], F32,
                            kind="ExternalInput").ap()
    w["Alog"] = nc.dram_tensor(f"{tag}_Alog", [64, 1], F32,
                               kind="ExternalInput").ap()
    w["trimask"] = nc.dram_tensor(f"{tag}_trimask", [64, 256], BF16,
                                  kind="ExternalInput").ap()
    w["winTc"] = nc.dram_tensor(f"{tag}_winTc", [128, NDC, 2 * DI], BF16,
                                kind="Internal").ap()
    w["wxTc"] = nc.dram_tensor(f"{tag}_wxTc", [128, 16, 129], BF16,
                               kind="Internal").ap()
    w["woTc"] = nc.dram_tensor(f"{tag}_woTc", [128, 16, DM], BF16,
                               kind="Internal").ap()
    return w


def _mamba_host(prefix, inputs):
    w_in = inputs[f"{prefix}_W_in"]
    conv_w = inputs[f"{prefix}_conv_w"]
    conv_b = inputs[f"{prefix}_conv_b"]
    w_x = inputs[f"{prefix}_W_x"]
    a_log = inputs[f"{prefix}_A_log"]
    d_vec = inputs[f"{prefix}_D"]
    w_out = inputs[f"{prefix}_W_out"]
    tp, tt = np.meshgrid(np.arange(QS), np.arange(QS), indexing="ij")
    tm = np.where(tp <= tt, 0.0, NEG).astype(np.float32).reshape(-1)
    return {
        "winT": np.ascontiguousarray(w_in.T),
        "wxT": np.ascontiguousarray(w_x.T),
        "woT": np.ascontiguousarray(w_out.T),
        "cw": np.ascontiguousarray(
            conv_w.reshape(NET, 128, 4).transpose(1, 0, 2)
            .reshape(128, NET * 4)),
        "cb": np.ascontiguousarray(conv_b.reshape(NET, 128).T),
        "D": np.ascontiguousarray(d_vec.reshape(NET, 128).T),
        "Alog": np.ascontiguousarray(a_log.reshape(64, 1)),
        "trimask": np.ascontiguousarray(
            np.broadcast_to(tm, (64, 256)).astype(BF)),
    }


def build_mamba(C, xh_ready, x_free, w, tag, cached=False):
    nc, P = C.nc, C.P
    key = f"mamw_{tag}"
    gkey = f"mams_{tag}"
    zkey = f"mamz_{tag}"
    arena_war = C.war.get("arena")
    pc = w.get("pc", []) if cached else []

    ld = {}
    if cached:
        ld["wxT"] = P.dma("sync", [C.war.get("m_wxT"), arena_war] + pc,
                          C.m_wxT[:, :].rearrange("p (c f) -> p c f",
                                                  f=132)[:, :, 0:129],
                          w["wxTc"], sem_key=key + "x")
        ld["woT"] = P.dma("scalar", [C.war.get("m_woT"), arena_war] + pc,
                          C.m_woT[:, :].rearrange("p (c f) -> p c f", f=1024),
                          w["woTc"], sem_key=key + "o")
    else:
        ld["wxT"] = P.dma("gpsimd", [C.war.get("m_wxT"), arena_war],
                          C.m_wxT[:, :].rearrange("p (c f) -> p c f",
                                                  f=132)[:, :, 0:129],
                          w["wxT"].rearrange("(c p) f -> p c f", p=128),
                          sem_key=key + "x")
        ld["woT"] = P.dma("gpsimd", [C.war.get("m_woT"), arena_war],
                          C.m_woT[:, :].rearrange("p (c f) -> p c f", f=1024),
                          w["woT"].rearrange("(c p) f -> p c f", p=128),
                          sem_key=key + "o")
    for name in ("cw", "cb", "D", "Alog", "trimask"):
        ld[name] = P.dma("sync", [C.war.get("m_" + name), arena_war],
                         getattr(C, "m_" + name)[:, :], w[name], sem_key=key)

    n_negA = P.op("scalar", [ld["Alog"], C.war.get("m_negA")],
                  lambda q: nc.scalar.activation(C.m_negA[:, :],
                                                 C.m_Alog[:, :], AF.Exp))
    n_negA = P.op("vector", [n_negA],
                  lambda q: nc.vector.tensor_scalar(
                      out=C.m_negA[:, :], in0=C.m_negA[:, :], scalar1=-1.0,
                      scalar2=None, op0=ALU.mult))

    n_hb0 = P.op("vector", [C.war.get("m_hbf"), arena_war],
                 lambda q: nc.vector.memset(C.m_hbf[:, :], 0.0))
    st = Ctx()
    st.hbf_node = n_hb0
    st.psw = [C.war.get(f"ps{i}") for i in range(8)]
    st.x_nodes = [list(xf) if isinstance(xf, list) else ([xf] if xf else [])
                  for xf in x_free]
    st.w = w
    st.key = key
    st.xh_ready = xh_ready
    st.silz_ld = None
    st.silz_readers = []
    st.zscr_stores = []

    win_war = [C.war.get("m_win"), arena_war]
    xc_war = [[C.war.get(f"m_xc{i}"), arena_war] for i in range(NET)]
    xtail_war = C.war.get("m_xtail")
    convt_war = C.war.get("tmp512b")
    zst_war = [C.war.get("xsq"), C.war.get("m_zst1")]
    e2_war = C.war.get("m_e2")

    for H in range(2):
        hsl = slice(H * TH, (H + 1) * TH)
        bc_mm = [None, None]
        dt_mm = [None, None]
        ld_idx = 0
        deferred_bc = None
        for et in range(NET):
            slot_x = ld_idx % 2
            ld_idx += 1
            slot_z = ld_idx % 2
            ld_idx += 1
            if cached:
                n_wx = P.dma("gpsimd", [win_war[slot_x]] + pc,
                             C.m_win[:, :].rearrange("p (s c e) -> p s c e",
                                                     s=2, c=NDC)[:, slot_x],
                             w["winTc"][:, :, et * 128:(et + 1) * 128],
                             sem_key=key + str(slot_x))
                n_wz = P.dma("gpsimd", [win_war[slot_z]] + pc,
                             C.m_win[:, :].rearrange("p (s c e) -> p s c e",
                                                     s=2, c=NDC)[:, slot_z],
                             w["winTc"][:, :, DI + et * 128:DI + (et + 1) * 128],
                             sem_key=key + str(slot_z))
            else:
                n_wx = P.dma("gpsimd", [win_war[slot_x]],
                             C.m_win[:, :].rearrange("p (s c e) -> p s c e",
                                                     s=2, c=NDC)[:, slot_x],
                             w["winT"][:, et * 128:(et + 1) * 128]
                             .rearrange("(c p) e -> p c e", p=128),
                             sem_key=key + str(slot_x))
                n_wz = P.dma("gpsimd", [win_war[slot_z]],
                             C.m_win[:, :].rearrange("p (s c e) -> p s c e",
                                                     s=2, c=NDC)[:, slot_z],
                             w["winT"][:, DI + et * 128:DI + (et + 1) * 128]
                             .rearrange("(c p) e -> p c e", p=128),
                             sem_key=key + str(slot_z))

            xin_done = {}
            prev_ev = None
            for tsh in range(2):
                tsl = slice(H * TH + tsh * TS, H * TH + (tsh + 1) * TS)
                lsl = slice(4 + tsh * TS, 4 + (tsh + 1) * TS)
                psA = C.ps[tsh]

                def emit_win(q, slot=slot_x, tsl=tsl, psA=psA):
                    last = None
                    for dc in range(NDC):
                        last = nc.tensor.matmul(
                            psA[:, :],
                            C.m_win[:, (slot * NDC + dc) * 128:
                                    (slot * NDC + dc + 1) * 128],
                            C.xh[dc][:, tsl],
                            start=(dc == 0), stop=(dc == NDC - 1))
                    return last
                n_mm = P.op("tensor", [n_wx, st.psw[tsh]] + xh_ready, emit_win)
                n_ev = P.op("scalar", [n_mm, prev_ev] + xc_war[et],
                            lambda q, et=et, lsl=lsl, psA=psA:
                            nc.scalar.activation(C.m_xc[et][:, lsl],
                                                 psA[:, :], AF.Copy))
                prev_ev = n_ev
                st.psw[tsh] = n_ev
                xin_done[tsh] = n_ev
            win_x_last = n_mm

            for tsh in range(2):
                tsl = slice(H * TH + tsh * TS, H * TH + (tsh + 1) * TS)
                psZ = C.ps[6 + tsh]

                def emit_z(q, slot=slot_z, tsl=tsl, psZ=psZ):
                    last = None
                    for dc in range(NDC):
                        last = nc.tensor.matmul(
                            psZ[:, :],
                            C.m_win[:, (slot * NDC + dc) * 128:
                                    (slot * NDC + dc + 1) * 128],
                            C.xh[dc][:, tsl],
                            start=(dc == 0), stop=(dc == NDC - 1))
                    return last
                n_zmm = P.op("tensor", [n_wz, st.psw[6 + tsh]] + xh_ready,
                             emit_z)
                zs = (et * 2 + tsh) % 2
                n_zev = P.op("scalar", [n_zmm, zst_war[zs]],
                             lambda q, zs=zs, psZ=psZ: nc.scalar.activation(
                                 C.m_zst[zs][:, :], psZ[:, :], AF.Silu))
                st.psw[6 + tsh] = n_zev
                n_zst = P.dma("sync" if zs == 0 else "scalar",
                              [n_zev],
                              C.m_zscr[et * 128:(et + 1) * 128, tsl],
                              C.m_zst[zs][:, :],
                              sem_key=zkey + ("s" if zs == 0 else "c"))
                zst_war[zs] = n_zst
                st.zscr_stores.append(n_zst)
            win_war[slot_z] = n_zmm
            win_war[slot_x] = win_x_last

            if H == 0:
                n_pad = P.op("vector", xc_war[et],
                             lambda q, et=et: nc.vector.memset(
                                 C.m_xc[et][:, 0:4], 0.0))
                n_tail = P.op("vector", [xin_done[1], xtail_war],
                              lambda q, et=et: nc.vector.tensor_copy(
                                  C.m_xtail[:, et * 4 + 1:(et + 1) * 4],
                                  C.m_xc[et][:, 4 + TH - 3:4 + TH]))
                xtail_war = n_tail
            else:
                n_pad = P.op("vector", [xtail_war] + xc_war[et],
                             lambda q, et=et: nc.vector.tensor_copy(
                                 C.m_xc[et][:, 1:4],
                                 C.m_xtail[:, et * 4 + 1:(et + 1) * 4]))
                n_tail = None

            deps_conv = [xin_done[0], xin_done[1], n_pad, ld["cw"], ld["cb"]]
            if n_tail is not None:
                deps_conv.append(n_tail)
            n_back = None
            for tsh in (1, 0):
                lo = 4 + tsh * TS
                n_c = P.op("vector", deps_conv + [convt_war],
                           lambda q, et=et, lo=lo: nc.vector.tensor_scalar(
                               out=C.m_convt[:, :],
                               in0=C.m_xc[et][:, lo - 3:lo - 3 + TS],
                               scalar1=C.m_cw[:, et * 4:et * 4 + 1],
                               scalar2=None, op0=ALU.mult))
                for j in range(1, 4):
                    n_c = P.op("vector", [n_c],
                               lambda q, et=et, lo=lo, j=j:
                               nc.vector.scalar_tensor_tensor(
                                   out=C.m_convt[:, :],
                                   in0=C.m_xc[et][:, lo - 3 + j:lo - 3 + j + TS],
                                   scalar=C.m_cw[:, et * 4 + j:et * 4 + j + 1],
                                   in1=C.m_convt[:, :],
                                   op0=ALU.mult, op1=ALU.add))
                n_s = P.op("scalar", [n_c, n_back],
                           lambda q, et=et, lo=lo: nc.scalar.activation(
                               C.m_xc[et][:, lo:lo + TS],
                               C.m_convt[:, :], AF.Silu,
                               bias=C.m_cb[:, et:et + 1]))
                convt_war = n_s
                n_back = n_s
            xc_war[et] = [n_back]

            def emit_bc(et, n_back):
                for tsh in range(2):
                    lsl = slice(4 + tsh * TS, 4 + (tsh + 1) * TS)
                    start = (et == 0)
                    stop = (et == NET - 1)
                    dep0 = [st.psw[2 + tsh], st.psw[4 + tsh], ld["wxT"]] \
                        if et == 0 else []
                    n_bcm = P.op("tensor", [n_back] + dep0,
                                 lambda q, et=et, lsl=lsl, tsh=tsh,
                                 start=start, stop=stop: nc.tensor.matmul(
                                     C.ps[2 + tsh][:, :],
                                     C.m_wxT[:, et * 132:et * 132 + 128],
                                     C.m_xc[et][:, lsl], start=start,
                                     stop=stop))
                    n_dtm = P.op("tensor", [n_back] + dep0,
                                 lambda q, et=et, lsl=lsl, tsh=tsh,
                                 start=start, stop=stop: nc.tensor.matmul(
                                     C.ps[4 + tsh][0:1, :],
                                     C.m_wxT[:, et * 132 + 128:et * 132 + 129],
                                     C.m_xc[et][:, lsl], start=start,
                                     stop=stop))
                    if stop:
                        bc_mm[tsh] = n_bcm
                        dt_mm[tsh] = n_dtm
            if deferred_bc is not None:
                emit_bc(*deferred_bc)
            deferred_bc = (et, n_back)
            if et == NET - 1:
                emit_bc(*deferred_bc)
                deferred_bc = None

        e2r0 = C.m_e2[:, 0:TH]
        e2r1 = C.m_e2[:, TH:2 * TH]
        dt_dmas = []
        for tsh in range(2):
            tsl = slice(H * TH + tsh * TS, H * TH + (tsh + 1) * TS)
            n_r = P.op("scalar", [dt_mm[tsh], C.war.get("rowA")],
                       lambda q, tsh=tsh: nc.scalar.activation(
                           C.rowA[:, :], C.ps[4 + tsh][0:1, :], AF.Copy))
            C.war["rowA"] = n_r
            n_d = P.dma("gpsimd", [n_r, C.war.get("row_scr2")],
                        C.row_scr[2:3, tsl], C.rowA[:, :], sem_key=gkey)
            C.war["rowA"] = n_d
            st.psw[4 + tsh] = n_r
            dt_dmas.append(n_d)
        n_bcdt = P.dma("gpsimd", dt_dmas + [e2_war],
                       e2r0, C.row_scr[2:3, hsl].broadcast_to([64, TH]),
                       sem_key=gkey)
        C.war["row_scr2"] = n_bcdt
        n_sp1 = P.op("scalar", [n_bcdt],
                     lambda q: nc.scalar.activation(e2r1, e2r0, AF.Abs))
        n_sp2 = P.op("scalar", [n_sp1],
                     lambda q: nc.scalar.activation(e2r1, e2r1, AF.Exp,
                                                    scale=-1.0))
        n_sp3 = P.op("scalar", [n_sp2] + C.const_nodes,
                     lambda q: nc.scalar.activation(e2r1, e2r1, AF.Ln,
                                                    bias=1.0))
        n_sp4 = P.op("vector", [n_bcdt, n_sp1],
                     lambda q: nc.vector.tensor_scalar(
                         out=e2r0, in0=e2r0, scalar1=0.0, scalar2=None,
                         op0=ALU.max))
        n_dtf = P.op("vector", [n_sp3, n_sp4],
                     lambda q: nc.vector.tensor_tensor(
                         out=e2r0, in0=e2r0, in1=e2r1, op=ALU.add))
        n_dtp = P.dma("gpsimd", [n_dtf, C.war.get("scr2_dt")],
                      C.m_scr2[0, 0:1, hsl], e2r0[0:1, :], sem_key=gkey)
        C.war["scr2_dt"] = n_dtp
        st_dt = n_dtp

        bdc_stores = []
        for tsh in range(2):
            tsl = slice(tsh * TS, (tsh + 1) * TS)
            gsl = slice(H * TH + tsh * TS, H * TH + (tsh + 1) * TS)
            n_bd = P.op("vector", [bc_mm[tsh], n_dtf, n_dtp],
                        lambda q, tsh=tsh, tsl=tsl: nc.vector.tensor_tensor(
                            out=e2r1[:, tsl], in0=C.ps[2 + tsh][0:64, :],
                            in1=e2r0[:, tsl], op=ALU.mult))
            n_sbd = P.dma("sync", [n_bd, C.war.get("scr2_bd")],
                          C.m_scr2[1, :, gsl], e2r1[:, tsl], sem_key=gkey + "s")
            C.war["scr2_bd"] = n_sbd
            n_cc = P.op("vector", [bc_mm[tsh], n_sbd],
                        lambda q, tsh=tsh, tsl=tsl: nc.vector.tensor_copy(
                            e2r1[:, tsl], C.ps[2 + tsh][64:128, :]))
            n_scc = P.dma("sync", [n_cc, C.war.get("scr2_c")],
                          C.m_scr2[2, :, gsl], e2r1[:, tsl], sem_key=gkey + "s")
            C.war["scr2_c"] = n_scc
            st.psw[2 + tsh] = [n_bd, n_cc]
            bdc_stores += [n_sbd, n_scc]
        e2_war = [n_scc, n_dtp]

        st.scr_ready = bdc_stores + [st_dt]
        st.xc_nodes = [xc_war[i][0] for i in range(NET)]
        st.xc_readers = [[] for _ in range(NET)]
        st.band_war = [win_war[0], win_war[1]]
        st.band_nodes = []
        st.stairB_war = [win_war[0], win_war[1]]

        if H == 0:
            st.tile_readers = [[], []]
        pend = _chunk_loads(C, P, st, H * 8, 0, tag)
        zld = _silz_load(C, P, st, H * 8, tag)
        pr = _chunk_prep(C, P, nc, st, H * 8, 0, n_negA, ld, tag, pend, 0)
        for cc8 in range(8):
            c = H * 8 + cc8
            cur_pr = pr
            if cc8 < 7:
                pend = _chunk_loads(C, P, st, c + 1, (cc8 + 1) % 2, tag)
                pr = _chunk_prep(C, P, nc, st, c + 1, cc8 + 1, n_negA, ld,
                                 tag, pend, (cc8 + 1) % 2)
            e2_war, zld = _scan_chunk(C, P, nc, st, c, cc8, n_negA, ld, tag,
                                      e2_war, cur_pr, cc8 % 2, zld, H)
        for i in range(NET):
            xc_war[i] = xc_war[i] + st.xc_readers[i]
        win_war = [[win_war[0]] + st.band_nodes + st.stairB_war,
                   [win_war[1]] + st.band_nodes + st.stairB_war]

    C.war["m_win"] = win_war
    C.war["m_xtail"] = xtail_war
    C.war["tmp512b"] = convt_war
    C.war["m_hbf"] = st.hbf_node
    C.war["m_e2"] = e2_war
    C.war["xsq"] = zst_war[0]
    C.war["m_zst1"] = zst_war[1]
    for i in range(NET):
        C.war[f"m_xc{i}"] = xc_war[i]
    for i in range(8):
        C.war[f"ps{i}"] = st.psw[i]
    allc = [n for dc in range(NDC) for n in st.x_nodes[dc]]
    C.war["arena"] = allc + st.silz_readers
    C.war["zscr"] = st.last_zld
    return [st.x_nodes[dc] for dc in range(NDC)]


def _chunk_loads(C, P, st, c, par, tag):
    csl = slice(c * QC, (c + 1) * QC)
    gkey = f"mams_{tag}"
    zkey = f"mamz_{tag}"
    rdrs = st.tile_readers[par]
    n_dtl = P.dma("sync", st.scr_ready + rdrs,
                  C.m_dt[par][:, :],
                  C.m_scr2[0, 0:1, csl].broadcast_to([64, QC]), sem_key=gkey + "s")
    n_bdl = P.dma("sync", st.scr_ready + rdrs,
                  C.m_Bd[par][:, :], C.m_scr2[1, :, csl], sem_key=gkey + "s")
    n_ccl = P.dma("scalar", st.scr_ready + rdrs,
                  C.m_C[par][:, :], C.m_scr2[2, :, csl], sem_key=gkey + "c")
    st.tile_readers[par] = []
    return (n_dtl, n_bdl, n_ccl)


def _silz_load(C, P, st, c, tag):
    csl = slice(c * QC, (c + 1) * QC)
    zkey = f"mamz_{tag}"
    n_zld = P.dma("gpsimd", st.silz_readers + st.zscr_stores,
                  C.m_silz[0][:, :].rearrange("p (c t) -> p c t", t=128),
                  C.m_zscr[:, csl].rearrange("(c p) t -> p c t", p=128),
                  sem_key=zkey)
    st.zscr_stores = []
    st.silz_readers = []
    st.last_zld = n_zld
    return n_zld


def _chunk_prep(C, P, nc, st, c, cc8, n_negA, ld, tag, lds, par):
    war = C.war
    n_dtl, n_bdl, n_ccl = lds
    m_dt, m_C, m_Bd = C.m_dt[par], C.m_C[par], C.m_Bd[par]
    m_cum = C.m_cum[par]
    m_t64a = C.m_t64a[par]
    m_ach = m_t64a
    m_t64b = C.m_t64b[par]
    m_Bst = C.m_Bst[par]
    m_Cst = C.m_Cst[par]
    m_Chl = C.m_Chl[par]
    m_Bhr = C.m_Bhr[par]
    m_expdb = C.m_expdb[par]
    m_de = C.m_de[par]
    pk = str(par)
    pr = Ctx()

    n_a = P.op("vector", [n_dtl, n_negA, war.get("m_t64a" + pk)],
               lambda q: nc.vector.tensor_scalar(
                   out=m_ach[:, :], in0=m_dt[:, :],
                   scalar1=C.m_negA[:, 0:1], scalar2=None, op0=ALU.mult))
    n_cum = P.op("vector", [n_a, war.get("m_cum" + pk)],
                 lambda q: nc.vector.tensor_tensor_scan(
                     out=m_cum[:, :],
                     data0=C.ones128[0:64, 0:1].broadcast_to([64, QC]),
                     data1=m_ach[:, :], initial=0.0,
                     op0=ALU.mult, op1=ALU.add))
    n_de = P.op("scalar", [n_cum] + [war.get("m_de" + pk) or []],
                lambda q: nc.scalar.activation(m_de[:, :],
                                               m_cum[:, 127:128], AF.Exp))
    n_e1 = P.op("vector", [n_cum, war.get("m_t64b" + pk)],
                lambda q: nc.vector.tensor_scalar(
                    out=m_t64b[:, :], in0=m_cum[:, :],
                    scalar1=m_cum[:, 127:128], scalar2=None,
                    op0=ALU.subtract))
    n_e2x = P.op("scalar", [n_e1],
                 lambda q: nc.scalar.activation(m_t64b[:, :], m_t64b[:, :],
                                                AF.Exp, scale=-1.0))
    n_bst = P.op("vector", [n_e2x, n_bdl, war.get("m_Bst" + pk)],
                 lambda q: nc.vector.tensor_tensor(
                     out=m_Bst[:, :], in0=m_Bd[:, :],
                     in1=m_t64b[:, :], op=ALU.mult))
    n_ec = P.op("scalar", [n_cum, n_bst],
                lambda q: nc.scalar.activation(m_t64b[:, :], m_cum[:, :],
                                               AF.Exp))
    n_cst = P.op("vector", [n_ec, n_ccl, war.get("m_Cst" + pk)],
                 lambda q: nc.vector.tensor_tensor(
                     out=m_Cst[:, :], in0=m_C[:, :],
                     in1=m_t64b[:, :], op=ALU.mult))
    war["m_t64b" + pk] = n_cst

    bnd_ap = m_cum[:, 15:127:16]
    n_chl1 = P.op("vector", [n_cum, n_a],
                  lambda q: nc.vector.tensor_tensor(
                      out=m_t64a[:, 0:112].rearrange("p (j t) -> p j t", t=16),
                      in0=m_cum[:, 16:128].rearrange("p (j t) -> p j t", t=16),
                      in1=bnd_ap[:, :, None].broadcast_to([64, 7, 16]),
                      op=ALU.subtract))
    n_chl2 = P.op("scalar", [n_chl1, war.get("m_Chl" + pk)],
                  lambda q: nc.scalar.activation(
                      m_Chl[:, :], m_t64a[:, 0:112], AF.Exp))
    n_chl = P.op("vector", [n_chl2, n_ccl],
                 lambda q: nc.vector.tensor_tensor(
                     out=m_Chl[:, :], in0=m_Chl[:, :],
                     in1=m_C[:, 16:128], op=ALU.mult))
    war["m_t64a" + pk] = n_chl2
    pr.n_a, pr.n_cum, pr.n_de, pr.n_bst = n_a, n_cum, n_de, n_bst
    pr.n_cst, pr.n_chl = n_cst, n_chl
    pr.n_dtl, pr.n_bdl, pr.n_ccl = n_dtl, n_bdl, n_ccl
    return pr


def _scan_chunk(C, P, nc, st, c, cc8, n_negA, ld, tag, e2_war, pr, par,
                n_zld, H):
    csl = slice(c * QC, (c + 1) * QC)
    loc = cc8 * QC
    xcsl = slice(4 + loc, 4 + loc + QC)
    skey = f"mamb{cc8 % 2}_{tag}"
    war = C.war
    m_dt, m_C, m_Bd, m_silz = (C.m_dt[par], C.m_C[par], C.m_Bd[par],
                               C.m_silz[0])
    m_cum = C.m_cum[par]
    m_t64b = C.m_t64b[par]
    m_Bst = C.m_Bst[par]
    m_Cst = C.m_Cst[par]
    m_BhT = C.m_BhT[par]
    m_Chl = C.m_Chl[par]
    m_Bhr = C.m_Bhr[par]
    m_expdb = C.m_expdb[par]
    m_de = C.m_de[par]
    pk = str(par)
    n_cum, n_bst, n_cst, n_de = pr.n_cum, pr.n_bst, pr.n_cst, pr.n_de
    n_chl = pr.n_chl
    n_bdl, n_ccl = pr.n_bdl, pr.n_ccl

    n_trB = P.op("tensor", [n_bst, st.psw[4]],
                 lambda q: nc.tensor.transpose(
                     C.ps[4].bitcast(BF16)[:, 0:64], m_Bst[:, :],
                     C.ident[0:64, 0:64]))
    n_evB = P.op("vector", [n_trB, war.get("m_BhT" + pk)],
                 lambda q: nc.vector.tensor_copy(
                     m_BhT[:, :], C.ps[4].bitcast(BF16)[:, 0:64]))
    st.psw[4] = n_evB
    war["m_BhT" + pk] = n_evB
    xct_nodes = []
    xcT_war = war.get("m_xcT")
    for et in range(NET):
        b = 4 + (et + 1) % 2
        n_tr = P.op("tensor",
                    [st.xc_nodes[et], st.psw[b]] +
                    ([] if et else [xcT_war]),
                    lambda q, et=et, b=b: nc.tensor.transpose(
                        C.ps[b].bitcast(BF16)[:, 0:128],
                        C.m_xc[et][:, xcsl], C.ident[:, :]))
        n_evt = P.op("vector",
                     [n_tr] + ([] if et else st.scr_ready),
                     lambda q, et=et, b=b: nc.vector.tensor_copy(
                         C.m_xcT[:, et * 128:(et + 1) * 128],
                         C.ps[b].bitcast(BF16)[:, 0:128]))
        st.psw[b] = n_evt
        st.xc_readers[et].append(n_tr)
        xct_nodes.append(n_evt)

    dh_nodes = []
    for i in range(4):
        n_dh = P.op("tensor", [n_evB] + xct_nodes[4 * i:4 * i + 4] +
                    [st.psw[i]],
                    lambda q, i=i: nc.tensor.matmul(
                        C.ps[i][0:64, :], m_BhT[:, :],
                        C.m_xcT[:, i * 512:(i + 1) * 512],
                        start=True, stop=True))
        dh_nodes.append(n_dh)

    stairB = C.m_win[0:64, 0:896]
    offs = [0]
    for i in range(1, 8):
        offs.append(offs[-1] + 16 * i)
    sb_prev = [n_cum, pr.n_bdl] + st.stairB_war
    sb_nodes = []
    for i in range(1, 8):
        sb_nodes.append(P.op("vector", sb_prev,
                        lambda q, i=i: nc.vector.tensor_scalar(
                            out=stairB[:, offs[i - 1]:offs[i]],
                            in0=m_cum[:, 0:16 * i],
                            scalar1=m_cum[:, 16 * i - 1:16 * i],
                            scalar2=None, op0=ALU.subtract)))
        sb_prev = []
    n_sbx = P.op("scalar", sb_nodes,
                 lambda q: nc.scalar.activation(
                     stairB[:, 0:896], stairB[:, 0:896], AF.Exp, scale=-1.0))
    sbb_nodes = []
    for i in range(1, 8):
        sbb_nodes.append(P.op("vector", [n_sbx],
                         lambda q, i=i: nc.vector.tensor_tensor(
                             out=stairB[:, offs[i - 1]:offs[i]],
                             in0=stairB[:, offs[i - 1]:offs[i]],
                             in1=m_Bd[:, 0:16 * i], op=ALU.mult)))
    stair_nodes = []
    for i in range(1, 8):
        n_stm = P.op("tensor", [sbb_nodes[i - 1], n_chl, st.psw[7]],
                     lambda q, i=i: nc.tensor.matmul(
                         C.ps[7][0:16 * i, 16 * i:16 * (i + 1)],
                         stairB[:, offs[i - 1]:offs[i]],
                         m_Chl[:, (i - 1) * 16:i * 16],
                         start=True, stop=True))
        stair_nodes.append(n_stm)
    st.stairB_war = stair_nodes

    n_gt0 = P.op("vector", [war.get("m_gt")],
                 lambda q: nc.vector.memset(C.m_gt[:, :], 0.0))
    gt_evs = []
    for i in range(1, 8):
        n_gev = P.op("vector", [stair_nodes[-1], n_gt0],
                     lambda q, i=i: nc.vector.tensor_copy(
                         C.m_gt[0:16 * i, 16 * i:16 * (i + 1)],
                         C.ps[7][0:16 * i, 16 * i:16 * (i + 1)]))
        gt_evs.append(n_gev)
    st.psw[7] = gt_evs

    n_prev = [n_cum, e2_war]
    for s8 in range(8):
        ss = slice(s8 * 16, (s8 + 1) * 16)
        n_prev = [P.op("vector", n_prev,
                       lambda q, s8=s8, ss=ss: nc.vector.scalar_tensor_tensor(
                           out=C.m_e2[:, s8 * 256:(s8 + 1) * 256]
                           .rearrange("p (a b) -> p a b", b=16),
                           in0=m_cum[:, ss][:, :, None]
                           .broadcast_to([64, 16, 16]),
                           scalar=-1.0,
                           in1=m_cum[:, ss][:, None, :]
                           .broadcast_to([64, 16, 16]),
                           op0=ALU.mult, op1=ALU.add))]
    n_b2 = P.op("vector", n_prev + [ld["trimask"]],
                lambda q: nc.vector.scalar_tensor_tensor(
                    out=C.m_e2[:, :].rearrange("p (s ab) -> p s ab", ab=256),
                    in0=C.m_trimask[:, :][:, None, :]
                    .broadcast_to([64, 8, 256]),
                    scalar=1.0,
                    in1=C.m_e2[:, :].rearrange("p (s ab) -> p s ab", ab=256),
                    op0=ALU.mult, op1=ALU.add))
    n_b3 = P.op("scalar", [n_b2],
                lambda q: nc.scalar.activation(C.m_e2[:, :], C.m_e2[:, :],
                                               AF.Exp))
    n_prev = [n_b3, n_ccl]
    for s8 in range(8):
        ss = slice(s8 * 16, (s8 + 1) * 16)
        n_prev = [P.op("vector", n_prev,
                       lambda q, s8=s8, ss=ss: nc.vector.tensor_tensor(
                           out=C.m_e2[:, s8 * 256:(s8 + 1) * 256]
                           .rearrange("p (a b) -> p a b", b=16),
                           in0=C.m_e2[:, s8 * 256:(s8 + 1) * 256]
                           .rearrange("p (a b) -> p a b", b=16),
                           in1=m_C[:, ss][:, None, :]
                           .broadcast_to([64, 16, 16]),
                           op=ALU.mult))]
    n_b5 = P.op("vector", n_prev + [n_bdl],
                lambda q: nc.vector.tensor_tensor(
                    out=C.m_e2[:, :].rearrange("p (sa b) -> p sa b", b=16),
                    in0=C.m_e2[:, :].rearrange("p (sa b) -> p sa b", b=16),
                    in1=m_Bd[:, :][:, :, None].broadcast_to([64, 128, 16]),
                    op=ALU.mult))

    band_ev = []
    red_mms = []
    prev_red = st.psw[4]
    for k in range(4):
        n_red = P.op("tensor", [n_b5, prev_red],
                     lambda q, k=k: nc.tensor.matmul(
                         C.ps[4][0:1, 0:512], C.ones128b[0:64, 0:1],
                         C.m_e2[:, k * 512:(k + 1) * 512],
                         start=True, stop=True))
        n_bev = P.op("vector",
                     [n_red] + (([war.get("m_band")] + st.band_war)
                                if k == 0 else []),
                     lambda q, k=k: nc.vector.tensor_copy(
                         C.m_band[:, k * 512:(k + 1) * 512],
                         C.ps[4][0:1, 0:512]))
        prev_red = n_bev
        band_ev.append(n_bev)
        red_mms.append(n_red)
    st.psw[4] = band_ev[-1]
    war["m_band"] = band_ev[-1]

    n_bd = P.dma("sync", band_ev + [war.get("m_bandscr")],
                 C.m_bandscr[:], C.m_band[0:1, :], sem_key=skey)
    scat_nodes = []
    for i in range(8):
        eng = "sync" if i % 2 == 0 else "scalar"
        n_sc = P.dma(eng, [n_bd, n_gt0],
                     C.m_gt[i * 16:(i + 1) * 16, i * 16:(i + 1) * 16],
                     C.m_bandscr[i * 256:(i + 1) * 256]
                     .rearrange("(a b) -> a b", b=16),
                     sem_key=skey + "abcd"[i % 4])
        scat_nodes.append(n_sc)
    war["m_bandscr"] = scat_nodes
    st.band_nodes += scat_nodes
    gt_ready = gt_evs + scat_nodes

    y_mms = []
    yev_nodes = []
    yt_war = war.get("m_yt")
    for et in range(NET):
        b = 5 + et % 2

        def emit_y(q, et=et, b=b):
            nc.tensor.matmul(C.ps[b][:, 0:128],
                             C.m_hbf[:, et * 128:(et + 1) * 128],
                             m_Cst[:, :], start=True, stop=False)
            return nc.tensor.matmul(C.ps[b][:, 0:128],
                                    C.m_xcT[:, et * 128:(et + 1) * 128],
                                    C.m_gt[:, :], start=False, stop=True)
        n_ymm = P.op("tensor",
                     [n_cst, st.hbf_node, xct_nodes[et], st.psw[b]] + gt_ready,
                     emit_y)
        y_mms.append(n_ymm)
        n_y1 = P.op("vector", [n_ymm, ld["D"], yt_war],
                    lambda q, et=et, b=b: nc.vector.scalar_tensor_tensor(
                        out=C.m_yt[:, :], in0=C.m_xc[et][:, xcsl],
                        scalar=C.m_D[:, et:et + 1], in1=C.ps[b][:, 0:128],
                        op0=ALU.mult, op1=ALU.add))
        st.psw[b] = n_y1
        n_y2 = P.op("vector", [n_y1, n_zld],
                    lambda q, et=et: nc.vector.tensor_tensor(
                        out=C.m_ych[:, et * 128:(et + 1) * 128],
                        in0=C.m_yt[:, :],
                        in1=m_silz[:, et * 128:(et + 1) * 128],
                        op=ALU.mult))
        yt_war = n_y2
        yev_nodes.append(n_y2)
        st.xc_readers[et].append(n_y1)
    st.silz_war = yev_nodes
    war["m_yt"] = yt_war
    war["m_gt"] = y_mms
    war["m_Cst" + pk] = y_mms

    h_upd = []
    for i in range(4):
        n_hu = P.op("vector", [dh_nodes[i], n_de, st.hbf_node] + y_mms,
                    lambda q, i=i: nc.vector.scalar_tensor_tensor(
                        out=C.m_hbf[:, i * 512:(i + 1) * 512],
                        in0=C.m_hbf[:, i * 512:(i + 1) * 512],
                        scalar=m_de[:, 0:1],
                        in1=C.ps[i][0:64, :], op0=ALU.mult, op1=ALU.add))
        h_upd.append(n_hu)
    war["m_de" + pk] = h_upd
    st.hbf_node = h_upd

    wo_evs = []
    wo_mms = []
    for dm in range(NDC):
        b = dm % 4
        reg = (dm // 4) * 128

        def emit_wo(q, dm=dm, b=b, reg=reg):
            last = None
            for ec in range(NET):
                last = nc.tensor.matmul(
                    C.ps[b][:, reg:reg + 128],
                    C.m_woT[:, ec * 1024 + dm * 128:ec * 1024 + (dm + 1) * 128],
                    C.m_ych[:, ec * 128:(ec + 1) * 128],
                    start=(ec == 0), stop=(ec == NET - 1))
            return last
        dep_ps = h_upd[b] if dm < 4 else wo_evs[dm - 4]
        n_wo = P.op("tensor", yev_nodes + [ld["woT"], dep_ps], emit_wo)
        wo_mms.append(n_wo)
        n_xa = P.op("vector", [n_wo] + st.x_nodes[dm],
                    lambda q, dm=dm, b=b, reg=reg: nc.vector.tensor_tensor(
                        out=C.x[dm][:, csl], in0=C.x[dm][:, csl],
                        in1=C.ps[b][:, reg:reg + 128], op=ALU.add))
        wo_evs.append(n_xa)
        st.x_nodes[dm] = [n_xa]
    for i in range(4):
        st.psw[i] = [wo_evs[i], wo_evs[i + 4]]
    war["m_xcT"] = wo_mms
    st.tile_readers[par] += [pr.n_a, n_bst, n_b5, n_cst, n_chl] + sbb_nodes
    st.silz_readers += yev_nodes
    next_zld = n_zld
    if cc8 < 7:
        next_zld = _silz_load(C, P, st, c + 1, tag)
    return red_mms, next_zld


def build_mamba_sublayer_program(debug=False):
    nc = bass.Bass("TRN2", target_bir_lowering=False)
    es = contextlib.ExitStack()
    xin = nc.dram_tensor("xin", [DM, T], F32, kind="ExternalInput").ap()
    xout = nc.dram_tensor("xout", [DM, T], F32, kind="ExternalOutput").ap()
    hconst = declare_const_dram(nc)
    w = declare_mamba_dram(nc, "dm0")
    C = alloc_common(nc, es)
    C.debug = debug
    alloc_mamba_views(C)
    load_consts(C, hconst)
    x_ready = load_x(C, xin)
    xh_nodes, _, _ = build_ln(C, x_ready, [], "ln0")
    x_upd = build_mamba(C, xh_nodes, x_ready, w, "m")
    store_x(C, xout, x_upd)
    C.P.run(es)
    es.close()
    return nc


def mamba_sublayer_inputs(prefix, inputs, xT):
    d = {"xin": xT}
    d.update(_const_host())
    wh = _mamba_host(prefix, inputs)
    return d | {f"dm0_{k}": wh[k] for k in wh}

def alloc_attn_views(C):
    a = ArenaAlloc(C)
    C.at_qk = a.b16(128, 8 * T)
    C.at_wvg = [C.at_qk[:, 0:8 * 1024],
                C.at_qk[:, 8 * 1024:16 * 1024]]
    C.at_woT = a.b16(128, 8 * 1024)
    C.at_tab = a.b16(128, 2 * T)
    C.at_wbuf = a.b16(128, 2 * 1024)
    C.at_sblk = [a.b16(128, TS), a.b16(128, TS)]
    C.at_vbuf = a.b16(128, 2048)
    C.at_vbuf2 = a.b16(128, 2048)
    C.at_osb = a.b16(128, 4 * 1024)
    C.at_qraw = C.at_osb[:, 0:2048]
    C.at_silg = a.b16(128, 1024)
    C.at_gtmp = a.b16(128, 1024)
    C.at_ogT = a.b16(128, 8 * 128)
    C.at_gnw = a.b16(128, 1024)
    C.at_gnb = a.b16(128, 1024)
    C.at_cmask = a.b16(128, 128)
    C.at_rbig = a.b16(128, 128)
    C.at_mean = a.f32(128, 8)
    C.at_var = a.f32(128, 8)
    C.at_rstd = a.f32(128, 8)


def declare_attn_dram(nc, tag):
    w = {}
    for n, cols in (("wqT", 512), ("wkT", 512), ("wvT", 1024), ("wgT", 1024),
                    ("woT", 1024)):
        w[n] = nc.dram_tensor(f"{tag}_{n}", [DM, cols], F32,
                              kind="ExternalInput").ap()
    w["gnw"] = nc.dram_tensor(f"{tag}_gnw", [128, 1024], F32,
                              kind="ExternalInput").ap()
    w["gnb"] = nc.dram_tensor(f"{tag}_gnb", [128, 1024], F32,
                              kind="ExternalInput").ap()
    w["qtab"] = nc.dram_tensor(f"{tag}_qtab", [512, 2 * T], BF16,
                               kind="ExternalInput").ap()
    w["ktab"] = nc.dram_tensor(f"{tag}_ktab", [512, 2 * T], BF16,
                               kind="ExternalInput").ap()
    w["rbig"] = nc.dram_tensor(f"{tag}_rbig", [128, 128], BF16,
                               kind="ExternalInput").ap()
    w["cmask"] = nc.dram_tensor(f"{tag}_cmask", [128, 128], BF16,
                                kind="ExternalInput").ap()
    for n, cols in (("wqT", 512), ("wkT", 512), ("wvT", 1024), ("wgT", 1024),
                    ("woT", 1024)):
        w[n + "c"] = nc.dram_tensor(f"{tag}_{n}c", [128, NDC, cols], BF16,
                                    kind="Internal").ap()
    return w


def _attn_host(prefix, inputs):
    wq = inputs[f"{prefix}_Wq"]
    wk = inputs[f"{prefix}_Wk"]
    wv = inputs[f"{prefix}_Wv"]
    wg = inputs[f"{prefix}_Wg"]
    wo = inputs[f"{prefix}_Wo"]
    gnw = inputs[f"{prefix}_gn_weight"]
    gnb = inputs[f"{prefix}_gn_bias"]
    H, D = 8, 64
    inv_freq = 1.0 / 10000.0 ** (np.arange(0, D, 2, dtype=np.float64) / D)
    t = np.arange(T, dtype=np.float64)
    freqs = t[:, None] * inv_freq[None, :]
    emb = np.concatenate([freqs, freqs], axis=-1)
    cos = np.cos(emb)
    sin = np.sin(emb)
    gamma = 1.0 - 2.0 ** (-5.0 - np.arange(H, dtype=np.float64))
    s = np.log2(gamma)
    qsc = np.exp2(t[:, None] * s[None, :]) / 8.0
    ksc = np.exp2(-t[:, None] * s[None, :])
    qtab = np.zeros((512, 2 * T), dtype=BF)
    ktab = np.zeros((512, 2 * T), dtype=BF)
    for h in range(H):
        r0 = (h % 2) * 64 + (h // 2) * 128
        qtab[r0:r0 + 64, 0:T] = (cos * qsc[:, h:h + 1]).T.astype(BF)
        qtab[r0:r0 + 64, T:2 * T] = (sin * qsc[:, h:h + 1]).T.astype(BF)
        ktab[r0:r0 + 64, 0:T] = (cos * ksc[:, h:h + 1]).T.astype(BF)
        ktab[r0:r0 + 64, T:2 * T] = (sin * ksc[:, h:h + 1]).T.astype(BF)
    rb = np.zeros((128, 128), dtype=np.float64)
    for blk in range(2):
        o = blk * 64
        for m in range(32):
            rb[o + m, o + 32 + m] = -1.0
            rb[o + 32 + m, o + m] = 1.0
    rbigT = rb.T
    cm = (np.arange(128)[:, None] <= np.arange(128)[None, :]).astype(np.float64)
    return {
        "wqT": np.ascontiguousarray(wq.T), "wkT": np.ascontiguousarray(wk.T),
        "wvT": np.ascontiguousarray(wv.T), "wgT": np.ascontiguousarray(wg.T),
        "woT": np.ascontiguousarray(wo.T),
        "gnw": np.ascontiguousarray(np.broadcast_to(gnw[None, :], (128, 1024))),
        "gnb": np.ascontiguousarray(np.broadcast_to(gnb[None, :], (128, 1024))),
        "qtab": qtab, "ktab": ktab,
        "rbig": rbigT.astype(BF), "cmask": cm.astype(BF),
    }


def build_attn(C, xh_ready, x_free, w, tag, cached=False):
    nc, P = C.nc, C.P
    key = f"attw_{tag}"
    vkey = f"attv_{tag}"
    arena_war = C.war.get("arena")
    pc = w.get("pc", []) if cached else []
    vscr = C.m_zscr

    ld = {}
    ld["woT"] = P.dma("gpsimd", [C.war.get("at_woT"), arena_war] + pc,
                      C.at_woT[:, :].rearrange("p (c f) -> p c f", f=1024),
                      w["woTc"] if cached else
                      w["woT"].rearrange("(c p) f -> p c f", p=128),
                      sem_key=key + "o")
    ld["gnw"] = P.dma("gpsimd", [C.war.get("at_gnw"), arena_war],
                      C.at_gnw[:, :], w["gnw"], sem_key=key + "n")
    ld["gnb"] = P.dma("gpsimd", [C.war.get("at_gnb"), arena_war],
                      C.at_gnb[:, :], w["gnb"], sem_key=key + "n")
    ld["rbig"] = P.dma("sync", [C.war.get("at_rbig"), arena_war],
                       C.at_rbig[:, :], w["rbig"], sem_key=key)
    ld["cmask"] = P.dma("sync", [C.war.get("at_cmask"), arena_war],
                        C.at_cmask[:, :], w["cmask"], sem_key=key)

    ld_wv = P.dma("gpsimd", [C.war.get("at_qk"), arena_war] + pc,
                  C.at_wvg[0].rearrange("p (c f) -> p c f", f=1024),
                  w["wvTc"] if cached else
                  w["wvT"].rearrange("(c p) f -> p c f", p=128),
                  sem_key=key + "v")
    ld_wg = P.dma("gpsimd", [C.war.get("at_qk"), arena_war] + pc,
                  C.at_wvg[1].rearrange("p (c f) -> p c f", f=1024),
                  w["wgTc"] if cached else
                  w["wgT"].rearrange("(c p) f -> p c f", p=128),
                  sem_key=key + "g")
    vg_mms = []
    stage_war = [C.war.get("at_vbuf"), arena_war]
    vstore = []
    psw = [C.war.get(f"ps{i}") for i in range(8)]
    for tt in range(NTT):
        ttl = slice(tt * 128, (tt + 1) * 128)
        evs = []
        for half in range(2):
            hs = slice(half * TS, (half + 1) * TS)

            def emit_v(q, ttl=ttl, hs=hs, half=half):
                last = None
                for dc in range(NDC):
                    last = nc.tensor.matmul(
                        C.ps[half][:, :], C.xh[dc][:, ttl],
                        C.at_wvg[0][:, :].rearrange("p (c f) -> p c f",
                                                    f=1024)[:, dc, hs],
                        start=(dc == 0), stop=(dc == NDC - 1))
                return last
            n_vm = P.op("tensor", [ld_wv, psw[half]] + xh_ready, emit_v)

            def emit_g(q, ttl=ttl, hs=hs, half=half):
                last = None
                for dc in range(NDC):
                    last = nc.tensor.matmul(
                        C.ps[2 + half][:, :], C.xh[dc][:, ttl],
                        C.at_wvg[1][:, :].rearrange("p (c f) -> p c f",
                                                    f=1024)[:, dc, hs],
                        start=(dc == 0), stop=(dc == NDC - 1))
                return last
            n_gm = P.op("tensor", [ld_wg, psw[2 + half]] + xh_ready, emit_g)
            n_ve = P.op("scalar", [n_vm] + stage_war,
                        lambda q, hs=hs, half=half: nc.scalar.activation(
                            C.at_vbuf[:, hs], C.ps[half][:, :], AF.Copy))
            n_ge = P.op("scalar", [n_gm] + stage_war,
                        lambda q, hs=hs, half=half: nc.scalar.activation(
                            C.at_vbuf[:, 1024 + hs.start:1024 + hs.stop],
                            C.ps[2 + half][:, :], AF.Silu))
            psw[half] = n_ve
            psw[2 + half] = n_ge
            evs += [n_ve, n_ge]
            vg_mms += [n_vm, n_gm]
        n_st = P.dma("sync", evs + [C.war.get("zscr")], vscr[ttl, :],
                     C.at_vbuf[:, :], sem_key=vkey + "s")
        stage_war = [n_st]
        vstore.append(n_st)

    qk_nodes = []
    tab_war = [C.war.get("at_tab"), arena_war]
    wbuf_war = [C.war.get("at_wbuf"), arena_war]
    qraw_war = [arena_war] + vg_mms
    tmp2k_war = [stage_war[0]]
    qk_war = [ld_wv, ld_wg] + vg_mms
    for ft in range(8):
        isq = ft < 4
        fc = ft % 4
        slot = ft % 2
        if cached:
            wsrcc = w["wqTc"] if isq else w["wkTc"]
            n_wb = P.dma("gpsimd", [wbuf_war[slot]] + pc,
                         C.at_wbuf[:, slot * 1024:(slot + 1) * 1024]
                         .rearrange("p (c f) -> p c f", f=128),
                         wsrcc[:, :, fc * 128:(fc + 1) * 128],
                         sem_key=key + str(slot))
        else:
            wsrc = w["wqT"] if isq else w["wkT"]
            n_wb = P.dma("gpsimd", [wbuf_war[slot]],
                         C.at_wbuf[:, slot * 1024:(slot + 1) * 1024]
                         .rearrange("p (c f) -> p c f", f=128),
                         wsrc[:, fc * 128:(fc + 1) * 128]
                         .rearrange("(c p) f -> p c f", p=128),
                         sem_key=key + str(slot))
        n_tb = P.dma("sync", tab_war,
                     C.at_tab[:, :],
                     (w["qtab"] if isq else w["ktab"])[fc * 128:(fc + 1) * 128, :],
                     sem_key=key)
        raw_evs = []
        rot_mms = []
        for ts4 in range(4):
            tsl = slice(ts4 * TS, (ts4 + 1) * TS)
            b = ts4 % 2

            def emit_p(q, slot=slot, tsl=tsl, b=b):
                last = None
                for dc in range(NDC):
                    last = nc.tensor.matmul(
                        C.ps[b][:, :],
                        C.at_wbuf[:, slot * 1024 + dc * 128:
                                  slot * 1024 + (dc + 1) * 128],
                        C.xh[dc][:, tsl], start=(dc == 0), stop=(dc == NDC - 1))
                return last
            n_pm = P.op("tensor", [n_wb, psw[b]] + xh_ready, emit_p)
            n_pe = P.op("scalar", [n_pm] + (qraw_war if ts4 == 0 else []),
                        lambda q, tsl=tsl, b=b: nc.scalar.activation(
                            C.at_qraw[:, tsl], C.ps[b][:, :], AF.Copy))
            psw[b] = n_pe
            raw_evs.append(n_pe)
            n_rm = P.op("tensor", [n_pe, ld["rbig"], psw[4 + b]],
                        lambda q, tsl=tsl, b=b: nc.tensor.matmul(
                            C.ps[4 + b][:, :],
                            C.at_rbig[:, :], C.at_qraw[:, tsl],
                            start=True, stop=True))
            rot_mms.append(n_rm)
            n_re = P.op("vector", [n_rm, n_tb] + (tmp2k_war if ts4 == 0 else []),
                        lambda q, tsl=tsl, b=b: nc.vector.tensor_tensor(
                            out=C.at_vbuf[:, tsl],
                            in0=C.ps[4 + b][:, :],
                            in1=C.at_tab[:, T + tsl.start:T + tsl.stop],
                            op=ALU.mult))
            psw[4 + b] = n_re
            rot_mms[-1] = n_re
        n_t1 = P.op("vector", raw_evs + [n_tb] + qk_war,
                    lambda q, ft=ft: nc.vector.tensor_tensor(
                        out=C.at_qk[:, ft * T:(ft + 1) * T],
                        in0=C.at_qraw[:, :], in1=C.at_tab[:, 0:T],
                        op=ALU.mult))
        qraw_war = [n_t1]
        n_t2 = P.op("vector", [n_t1] + rot_mms,
                    lambda q, ft=ft: nc.vector.tensor_tensor(
                        out=C.at_qk[:, ft * T:(ft + 1) * T],
                        in0=C.at_qk[:, ft * T:(ft + 1) * T],
                        in1=C.at_vbuf[:, :], op=ALU.add))
        tmp2k_war = [n_t2]
        tab_war = [n_t1] + rot_mms
        wbuf_war[slot] = n_pm
        qk_nodes.append(n_t2)
        qk_war = []

    x_nodes = [list(xf) if isinstance(xf, list) else ([xf] if xf else [])
               for xf in x_free]
    vbuf_war = tmp2k_war + [arena_war]
    silg_war = [C.war.get("at_silg"), arena_war]
    osb_war = qraw_war
    ogT_war = [C.war.get("at_ogT"), arena_war]
    gtmp_war = [C.war.get("at_gtmp"), arena_war]
    sblk_war = [[arena_war], [arena_war]]
    mean_war = [C.war.get("at_mean"), arena_war]

    vbuf_wars = [vbuf_war, [w for w in vbuf_war]]
    vbufs = [C.at_vbuf, C.at_vbuf2]
    pending_vl = None

    def issue_vl(qg_, h_, par_):
        n_kc_ = 4 * qg_ + 4
        n = P.dma("gpsimd", vbuf_wars[par_] + vstore,
                  vbufs[par_][:, 0:n_kc_ * 128]
                  .rearrange("p (kc f) -> p kc f", f=128),
                  vscr[0:n_kc_ * 128, h_ * 128:(h_ + 1) * 128]
                  .rearrange("(kc p) f -> p kc f", p=128),
                  sem_key=vkey)
        vbuf_wars[par_] = []
        return n
    pending_vl = issue_vl(0, 0, 0)
    for qg in range(4):
        n_kc = 4 * qg + 4
        o_evs = []
        for h in range(8):
            par = (qg * 8 + h) % 2
            at_vbuf = vbufs[par]
            qt = C.at_qk[:, (h // 2) * T:(h // 2 + 1) * T]
            kt = C.at_qk[:, (4 + h // 2) * T:(4 + h // 2 + 1) * T]
            r0 = (h % 2) * 64
            n_vl = pending_vl
            if not (qg == 3 and h == 7):
                nh = qg * 8 + h + 1
                pending_vl = issue_vl(nh // 8, nh % 8, (par + 1) % 2)
            o_mms_all = []
            o_mm_last = [None] * 4
            for kc in range(n_kc):
                b = kc % 2
                n_sm = P.op("tensor", [qk_nodes[h // 2], qk_nodes[4 + h // 2],
                                       psw[b]] + sblk_war[b],
                            lambda q, kc=kc, b=b, qt=qt, kt=kt, r0=r0, qg=qg:
                            nc.tensor.matmul(
                                C.ps[b][:, :],
                                kt[r0:r0 + 64, kc * 128:(kc + 1) * 128],
                                qt[r0:r0 + 64, qg * TS:(qg + 1) * TS],
                                start=True, stop=True))
                n_se = P.op("scalar", [n_sm] + sblk_war[b],
                            lambda q, b=b: nc.scalar.activation(
                                C.at_sblk[b][:, :], C.ps[b][:, :], AF.Copy))
                psw[b] = n_se
                jd = kc - 4 * qg
                if 0 <= jd <= 3:
                    n_se = P.op("vector", [n_se, ld["cmask"]],
                                lambda q, b=b, jd=jd:
                                nc.vector.tensor_tensor(
                                    out=C.at_sblk[b][:, jd * 128:(jd + 1) * 128],
                                    in0=C.at_sblk[b][:, jd * 128:(jd + 1) * 128],
                                    in1=C.at_cmask[:, :], op=ALU.mult))
                o_deps = [n_se, n_vl]
                sb_readers = []
                for j in range(4):
                    if 4 * qg + j < kc:
                        continue
                    n_om = P.op("tensor",
                                o_deps + [o_mm_last[j] or psw[4 + j]],
                                lambda q, kc=kc, b=b, j=j, at_vbuf=at_vbuf:
                                nc.tensor.matmul(
                                    C.ps[4 + j][:, 0:128],
                                    C.at_sblk[b][:, j * 128:(j + 1) * 128],
                                    at_vbuf[:, kc * 128:(kc + 1) * 128],
                                    start=(kc == 0), stop=(kc == 4 * qg + j)))
                    o_mm_last[j] = n_om
                    sb_readers.append(n_om)
                    o_mms_all.append(n_om)
                sblk_war[b] = sb_readers
            for j in range(4):
                n_oe = P.op("scalar",
                            [o_mm_last[j]] + (osb_war if (h == 0 and j == 0)
                                              else []),
                            lambda q, h=h, j=j: nc.scalar.activation(
                                C.at_osb[:, j * 1024 + h * 128:
                                         j * 1024 + (h + 1) * 128],
                                C.ps[4 + j][:, 0:128], AF.Copy))
                psw[4 + j] = n_oe
                o_evs.append(n_oe)
            vbuf_wars[par] = o_mms_all
        osb_war = []
        osb_readers = []

        for j in range(4):
            tt = qg * 4 + j
            ttl = slice(tt * 128, (tt + 1) * 128)
            osl = C.at_osb[:, j * 1024:(j + 1) * 1024]
            n_sg = P.dma("gpsimd", silg_war + vstore, C.at_silg[:, :],
                         vscr[ttl, 1024:2048], sem_key=vkey)
            n_mn = P.op("vector", o_evs + mean_war,
                        lambda q, osl=osl: nc.vector.tensor_reduce(
                            out=C.at_mean[:, :],
                            in_=osl.rearrange("p (h v) -> p h v", v=128),
                            axis=AX.X, op=ALU.add))
            n_sq = P.op("scalar", o_evs + gtmp_war,
                        lambda q, osl=osl: nc.scalar.activation(
                            C.at_gtmp[:, :], osl, AF.Square))
            n_vr = P.op("vector", [n_sq, C.war.get("at_var")],
                        lambda q: nc.vector.tensor_reduce(
                            out=C.at_var[:, :],
                            in_=C.at_gtmp[:, :].rearrange(
                                "p (h v) -> p h v", v=128),
                            axis=AX.X, op=ALU.add))
            n_mn2 = P.op("vector", [n_mn],
                         lambda q: nc.vector.tensor_scalar(
                             out=C.at_mean[:, :], in0=C.at_mean[:, :],
                             scalar1=1.0 / 128.0, scalar2=None, op0=ALU.mult))
            n_vr2 = P.op("vector", [n_vr, n_mn2, C.war.get("at_rstd")],
                         lambda q: nc.vector.tensor_tensor(
                             out=C.at_rstd[:, :], in0=C.at_mean[:, :],
                             in1=C.at_mean[:, :], op=ALU.mult))
            n_vr3 = P.op("vector", [n_vr2],
                         lambda q: nc.vector.scalar_tensor_tensor(
                             out=C.at_var[:, :], in0=C.at_var[:, :],
                             scalar=1.0 / 128.0, in1=C.at_rstd[:, :],
                             op0=ALU.mult, op1=ALU.subtract))
            n_r1 = P.op("scalar", [n_vr3] + C.const_nodes,
                        lambda q: nc.scalar.activation(
                            C.at_var[:, :], C.at_var[:, :], AF.Ln, bias=1e-5))
            n_r2 = P.op("scalar", [n_r1],
                        lambda q: nc.scalar.activation(
                            C.at_rstd[:, :], C.at_var[:, :], AF.Exp,
                            scale=-0.5))
            mean_war = [n_vr2]
            C.war["at_var"] = n_r2
            C.war["at_rstd"] = n_r2
            osb_readers_j = []
            n_n1 = P.op("vector", [n_mn2, n_sq] + o_evs,
                        lambda q, osl=osl: nc.vector.tensor_tensor(
                            out=C.at_gtmp[:, :].rearrange(
                                "p (h v) -> p h v", v=128),
                            in0=osl.rearrange("p (h v) -> p h v", v=128),
                            in1=C.at_mean[:, :, None]
                            .broadcast_to([128, 8, 128]),
                            op=ALU.subtract))
            n_n2 = P.op("vector", [n_n1, n_r2],
                        lambda q: nc.vector.tensor_tensor(
                            out=C.at_gtmp[:, :].rearrange(
                                "p (h v) -> p h v", v=128),
                            in0=C.at_gtmp[:, :].rearrange(
                                "p (h v) -> p h v", v=128),
                            in1=C.at_rstd[:, :, None]
                            .broadcast_to([128, 8, 128]),
                            op=ALU.mult))
            n_n3 = P.op("vector", [n_n2, ld["gnw"]],
                        lambda q: nc.vector.tensor_tensor(
                            out=C.at_gtmp[:, :], in0=C.at_gtmp[:, :],
                            in1=C.at_gnw[:, :], op=ALU.mult))
            n_n4 = P.op("vector", [n_n3, ld["gnb"]],
                        lambda q: nc.vector.tensor_tensor(
                            out=C.at_gtmp[:, :], in0=C.at_gtmp[:, :],
                            in1=C.at_gnb[:, :], op=ALU.add))
            n_n5 = P.op("vector", [n_n4, n_sg],
                        lambda q: nc.vector.tensor_tensor(
                            out=C.at_gtmp[:, :], in0=C.at_gtmp[:, :],
                            in1=C.at_silg[:, :], op=ALU.mult))
            silg_war = [n_n5]
            gtmp_war = [n_n5]
            osb_readers.append(n_n1)
            tr_evs = []
            for hc in range(8):
                b = hc % 2
                n_tr = P.op("tensor", [n_n5, psw[b]],
                            lambda q, hc=hc, b=b: nc.tensor.transpose(
                                C.ps[b].bitcast(BF16)[:, 0:128],
                                C.at_gtmp[:, hc * 128:(hc + 1) * 128],
                                C.ident[:, :]))
                n_te = P.op("vector", [n_tr] + (ogT_war if hc == 0 else []),
                            lambda q, hc=hc, b=b: nc.vector.tensor_copy(
                                C.at_ogT[:, hc * 128:(hc + 1) * 128],
                                C.ps[b].bitcast(BF16)[:, 0:128]))
                psw[b] = n_te
                tr_evs.append(n_te)
            gtmp_war = tr_evs
            ogT_war = []
            wo_evs = []
            for dm in range(NDC):
                b = 2 + dm % 2

                def emit_wo(q, dm=dm, b=b):
                    last = None
                    for hc in range(8):
                        last = nc.tensor.matmul(
                            C.ps[b][:, 0:128],
                            C.at_woT[:, hc * 1024 + dm * 128:
                                     hc * 1024 + (dm + 1) * 128],
                            C.at_ogT[:, hc * 128:(hc + 1) * 128],
                            start=(hc == 0), stop=(hc == 7))
                    return last
                n_wm = P.op("tensor",
                            tr_evs + [ld["woT"],
                                      wo_evs[dm - 2] if dm >= 2 else psw[b]],
                            emit_wo)
                n_xa = P.op("vector", [n_wm] + x_nodes[dm],
                            lambda q, dm=dm, b=b, ttl=ttl:
                            nc.vector.tensor_tensor(
                                out=C.x[dm][:, ttl], in0=C.x[dm][:, ttl],
                                in1=C.ps[b][:, 0:128], op=ALU.add))
                wo_evs.append(n_xa)
                x_nodes[dm] = [n_xa]
            ogT_war = [w for w in wo_evs]
            for dm in range(NDC):
                b = 2 + dm % 2
                psw[b] = wo_evs[dm]
        osb_war = osb_readers

    for i in range(8):
        C.war[f"ps{i}"] = psw[i]
    allc = [n for dc in range(NDC) for n in x_nodes[dc]]
    C.war["arena"] = allc
    return [x_nodes[dc] for dc in range(NDC)]



def build_attn_sublayer_program(debug=False):
    nc = bass.Bass("TRN2", target_bir_lowering=False)
    es = contextlib.ExitStack()
    xin = nc.dram_tensor("xin", [DM, T], F32, kind="ExternalInput").ap()
    xout = nc.dram_tensor("xout", [DM, T], F32, kind="ExternalOutput").ap()
    hconst = declare_const_dram(nc)
    w = declare_attn_dram(nc, "da1")
    C = alloc_common(nc, es)
    C.debug = debug
    alloc_attn_views(C)
    zkind = "ExternalOutput" if debug else "Internal"
    C.m_zscr = nc.dram_tensor("m_zscr", [DI, T], BF16, kind=zkind).ap()
    load_consts(C, hconst)
    x_ready = load_x(C, xin)
    xh_nodes, _, _ = build_ln(C, x_ready, [], "ln0")
    x_upd = build_attn(C, xh_nodes, x_ready, w, "a1")
    if debug:
        dbg_qk = nc.dram_tensor("dbg_qk", [128, 8 * T], BF16,
                                kind="ExternalOutput").ap()
        C.P.dma("gpsimd", [C.war.get("arena")], dbg_qk, C.at_qk[:, :],
                sem_key="dbgq")
        dbg_osb = nc.dram_tensor("dbg_osb", [128, 4 * 1024], BF16,
                                 kind="ExternalOutput").ap()
        C.P.dma("gpsimd", [C.war.get("arena")], dbg_osb, C.at_osb[:, :],
                sem_key="dbgq")
    store_x(C, xout, x_upd)
    C.P.run(es)
    es.close()
    return nc


def attn_sublayer_inputs(prefix, inputs, xT):
    d = {"xin": xT}
    d.update(_const_host())
    wh = _attn_host(prefix, inputs)
    return d | {f"da1_{k}": wh[k] for k in wh}


# version 23 (final)
# speedup vs baseline: 1.0047x; 1.0047x over previous
import contextlib
import numpy as np
import ml_dtypes
import concourse.bass as bass
import concourse.mybir as mybir
from concourse.bass_utils import run_bass_kernel_spmd

F32 = mybir.dt.float32
BF16 = mybir.dt.bfloat16
AF = mybir.ActivationFunctionType
ALU = mybir.AluOpType
AX = mybir.AxisListType

BF = ml_dtypes.bfloat16

T = 2048
DM = 1024
NDC = DM // 128
TS = 512
NTS = T // TS
NTT = T // 128
NEG = -1.0e30

ENGINES = ("tensor", "vector", "scalar", "gpsimd", "sync")



class Node:
    __slots__ = ("eng", "sem_key", "cum")

    def __init__(self, eng, sem_key, cum):
        self.eng = eng
        self.sem_key = sem_key
        self.cum = cum


class Prog:
    def __init__(self, nc):
        self.nc = nc
        self.streams = {e: [] for e in ENGINES}
        self.sem_counts = {}
        self._last_dma = {}

    def op(self, eng, deps, emit_fn, inc=1, sem_key=None):
        key = sem_key or eng
        cum = self.sem_counts.get(key, 0) + inc
        self.sem_counts[key] = cum
        node = Node(eng, key, cum)
        flat = []

        def _fl(d):
            if d is None:
                return
            if isinstance(d, (list, tuple)):
                for e in d:
                    _fl(e)
            else:
                flat.append(d)
        _fl(deps)
        self.streams[eng].append((emit_fn, flat, node, inc))
        return node

    def dma(self, eng, deps, out, in_, sem_key=None, **kw):
        key = sem_key or (eng + "_dma")
        prev = self._last_dma.get(key)
        node = self.op(eng, list(deps) + ([prev] if prev else []),
                       lambda q: q.dma_start(out=out, in_=in_, **kw),
                       inc=16, sem_key=key)
        self._last_dma[key] = node
        return node

    def finalize(self, block, sems):
        for eng in ENGINES:
            stream = self.streams[eng]
            if not stream:
                continue

            def make(eng_name, stream_):
                def body(q):
                    waited = {}
                    for emit_fn, deps, node, inc in stream_:
                        for d in deps:
                            if d.eng == "tensor" and eng_name == "tensor":
                                continue
                            if waited.get(d.sem_key, 0) >= d.cum:
                                continue
                            q.wait_ge(sems[d.sem_key], d.cum)
                            waited[d.sem_key] = d.cum
                        inst = emit_fn(q)
                        inst.then_inc(sems[node.sem_key], inc)
                return body
            getattr(block, eng)(make(eng, stream))

    def run(self, es):
        sems = {}
        for key in sorted(self.sem_counts):
            sems[key] = es.enter_context(self.nc.semaphore(f"s_{key}"))
        block = es.enter_context(self.nc.Block())
        self.finalize(block, sems)


class Ctx:
    pass



def alloc_common(nc, es):
    C = Ctx()
    C.nc = nc
    C.P = Prog(nc)
    C.war = {}

    def sb(name, shape, dt):
        return es.enter_context(nc.sbuf_tensor(name, shape, dt))
    C.sb = sb

    C.x = [sb(f"x_res{i}", [128, T], F32) for i in range(NDC)]
    C.xh = [sb(f"xh{i}", [128, T], BF16) for i in range(NDC)]
    C.ps = [es.enter_context(nc.psum_tensor(f"ps{i}", [128, TS], F32))
            for i in range(8)]

    C.ident = sb("ident", [128, 128], BF16)
    C.ones128 = sb("ones128", [128, 1], F32)
    C.ones128b = sb("ones128b", [128, 1], BF16)
    C.onesrow_f = sb("onesrow_f", [1, 128], F32)

    C.c_eps = sb("eps_sb", [128, 1], F32)
    C.rowA = sb("rowA", [1, TS], F32)
    C.rowB = sb("rowB", [1, TS], F32)
    C.mean_bc = sb("mean_bc", [128, T], BF16)
    C.rstd_bc = sb("rstd_bc", [128, T], BF16)
    C.xsq = sb("xsq", [128, TS], BF16)
    C.tmp512b = sb("tmp512b", [128, TS], BF16)

    C.arena = sb("arena", [128, ARENA_BYTES // 2], BF16)
    C.arena_f = C.arena.bitcast(F32)

    C.row_scr = nc.dram_tensor("row_scratch", [4, T], F32, kind="Internal").ap()
    return C


ARENA_BYTES = 99328


class ArenaAlloc:
    def __init__(self, C):
        self.C = C
        self.off = 0

    def _take(self, nbytes):
        off = self.off
        self.off = (off + nbytes + 63) & ~63
        assert self.off <= ARENA_BYTES, f"arena overflow: {self.off}"
        return off

    def b16(self, parts, cols):
        off = self._take(2 * cols)
        return self.C.arena[0:parts, off // 2:off // 2 + cols]

    def f32(self, parts, cols):
        off = self._take(4 * cols)
        return self.C.arena_f[0:parts, off // 4:off // 4 + cols]


def alloc_ffn_views(C):
    a = ArenaAlloc(C)
    C.rwT = a.f32(128, NDC * 32)
    C.rb_rep = a.f32(128, 32)
    C.selneg = a.b16(32, 2 * 1024)
    C.b1_sb = a.f32(128, 64)
    C.lg = a.f32(128, 32)
    C.mx = a.f32(128, 1)
    C.s_rep = a.f32(128, 32)
    C.s_sb = C.s_rep[0:1, :]
    C.tmp32 = a.f32(128, 32)
    C.mean_cols = a.f32(128, NTT)
    C.rstd_cols = a.f32(128, NTT)
    C.ohtile = a.b16(128, 32)
    C.ohT = a.b16(32, T)
    C.iohT = a.b16(32, T)
    C.b2_bf = a.b16(32, DM)
    C.w1g = a.b16(128, 8 * NDC * 256)
    C.w2g = a.b16(128, 8 * 2 * DM)
    C.hall = a.b16(128, 16 * TS)
    C.selgrp = a.b16(32, 4 * 128)


def load_consts(C, h):
    P = C.P
    n = []
    n.append(P.dma("sync", [], C.ident[:, :], h["c_ident"], sem_key="c_dma"))
    n.append(P.dma("sync", [], C.ones128[:, :], h["c_ones_f"], sem_key="c_dma"))
    n.append(P.dma("sync", [], C.ones128b[:, :], h["c_ones_b"], sem_key="c_dma"))
    n.append(P.dma("sync", [], C.onesrow_f[:, :], h["c_onesrow_f"], sem_key="c_dma"))
    n.append(P.dma("sync", [], C.c_eps[:, :], h["c_eps"], sem_key="c_dma"))
    C.nc.const_aps.register(F32, 1e-5, C.c_eps[:, 0:1])
    C.const_nodes = n
    return n



def build_ln(C, x_ready, xh_war, tag):
    nc, P = C.nc, C.P
    war_ps = [C.war.get(f"ps{i}") for i in range(8)]

    sumA = []
    sumB = []
    xsq_war = [C.war.get("xsq"), C.war.get("rstd_bc")]
    xb_war = [C.war.get("rstd_bc"), C.war.get("rstd_bc")]
    for ts in range(NTS):
        tsl = slice(ts * TS, (ts + 1) * TS)

        n_mm_last = None
        n_mmA_last = None
        for dc in range(NDC):
            sqb = [C.xsq[:, :], C.rstd_bc[:, 0:TS]][dc % 2]
            xbb = [C.rstd_bc[:, TS:2 * TS], C.rstd_bc[:, 2 * TS:3 * TS]][dc % 2]
            n_xb = P.op("vector", [x_ready[dc], xb_war[dc % 2]],
                        lambda q, dc=dc, tsl=tsl, xbb=xbb:
                        nc.vector.tensor_copy(xbb, C.x[dc][:, tsl]))
            n_mmA = P.op("tensor", [n_xb, war_ps[4 + ts]],
                         lambda q, dc=dc, ts=ts, xbb=xbb: nc.tensor.matmul(
                             C.ps[4 + ts][0:1, :], C.ones128b[:, :], xbb,
                             start=(dc == 0), stop=(dc == NDC - 1)))
            xb_war[dc % 2] = n_mmA
            n_mmA_last = n_mmA
            n_sq = P.op("scalar",
                        [x_ready[dc], xsq_war[dc % 2]],
                        lambda q, dc=dc, tsl=tsl, sqb=sqb:
                        nc.scalar.activation(sqb, C.x[dc][:, tsl], AF.Square))
            n_mm = P.op("tensor", [n_sq, war_ps[ts]],
                        lambda q, dc=dc, ts=ts, sqb=sqb: nc.tensor.matmul(
                            C.ps[ts][0:1, :], C.ones128b[:, :], sqb,
                            start=(dc == 0), stop=(dc == NDC - 1)))
            xsq_war[dc % 2] = n_mm
            n_mm_last = n_mm
        sumA.append(n_mmA_last)
        sumB.append(n_mm_last)
    C.war["xsq"] = xsq_war[0]
    C.war["rstd_bc"] = [C.war.get("rstd_bc"), xsq_war[1]]

    key = f"lnrow_{tag}"
    scr_war = C.war.get("row_scr")
    out_dmas = []
    rowA_war = C.war.get("rowA")
    rowB_war = C.war.get("rowB")
    for ts in range(NTS):
        tsl = slice(ts * TS, (ts + 1) * TS)
        n_mean = P.op("scalar", [sumA[ts], rowA_war],
                      lambda q, ts=ts: nc.scalar.activation(
                          C.rowA[:, :], C.ps[4 + ts][0:1, :], AF.Copy,
                          scale=1.0 / DM))
        n_msq = P.op("vector", [n_mean, rowB_war],
                     lambda q: nc.vector.tensor_tensor(
                         out=C.rowB[:, :], in0=C.rowA[:, :],
                         in1=C.rowA[:, :], op=ALU.mult))
        n_var = P.op("vector", [n_msq, sumB[ts]],
                     lambda q, ts=ts: nc.vector.scalar_tensor_tensor(
                         out=C.rowB[:, :], in0=C.ps[ts][0:1, :],
                         scalar=1.0 / DM, in1=C.rowB[:, :],
                         op0=ALU.mult, op1=ALU.subtract))
        n_ln = P.op("scalar", [n_var] + C.const_nodes, lambda q: nc.scalar.activation(
            C.rowB[:, :], C.rowB[:, :], AF.Ln, bias=1e-5))
        n_ex = P.op("scalar", [n_ln], lambda q: nc.scalar.activation(
            C.rowB[:, :], C.rowB[:, :], AF.Exp, scale=-0.5))
        n_d1 = P.dma("gpsimd", [n_mean, scr_war], C.row_scr[0:1, tsl],
                     C.rowA[:, :], sem_key=key)
        n_d2 = P.dma("gpsimd", [n_ex, scr_war], C.row_scr[1:2, tsl],
                     C.rowB[:, :], sem_key=key)
        rowA_war = n_d1
        rowB_war = n_d2
        out_dmas += [n_d1, n_d2]
        C.war[f"ps{ts}"] = n_var
        C.war[f"ps{4+ts}"] = n_mean
    C.war["rowA"] = rowA_war
    C.war["rowB"] = rowB_war
    C.ln_row_dmas = out_dmas

    n_b1 = P.dma("gpsimd", out_dmas + [C.war.get("mean_bc")], C.mean_bc[:, :],
                 C.row_scr[0:1, :].broadcast_to([128, T]), sem_key=key)
    n_b2 = P.dma("gpsimd", out_dmas + [C.war.get("rstd_bc")], C.rstd_bc[:, :],
                 C.row_scr[1:2, :].broadcast_to([128, T]), sem_key=key)
    C.war["row_scr"] = [n_b1, n_b2]

    out = []
    for dc in range(NDC):
        n_sub = P.op("vector", [n_b1, x_ready[dc]] + xh_war,
                     lambda q, dc=dc: nc.vector.tensor_tensor(
                         out=C.xh[dc][:, :], in0=C.x[dc][:, :],
                         in1=C.mean_bc[:, :], op=ALU.subtract))
        n_mul = P.op("vector", [n_sub, n_b2],
                     lambda q, dc=dc: nc.vector.tensor_tensor(
                         out=C.xh[dc][:, :], in0=C.xh[dc][:, :],
                         in1=C.rstd_bc[:, :], op=ALU.mult))
        out.append(n_mul)
    C.war["mean_bc"] = list(out)
    C.war["rstd_bc"] = list(out)
    return out, n_b1, n_b2



def build_ffn(C, xh_ready, x_free, w, tag, ln_rows, cached=False):
    nc, P = C.nc, C.P
    key = f"ffnw_{tag}"
    x_ready = ln_rows[0]
    arena_war = C.war.get("arena")

    ld_rw = P.dma("sync", [C.war.get("rwT"), arena_war],
                  C.rwT[:, :].rearrange("p (c e) -> p c e", e=32),
                  w["rwT"].rearrange("(c p) e -> p c e", p=128), sem_key=key)
    ld_rb = P.dma("sync", [C.war.get("rb_rep"), arena_war], C.rb_rep[:, :],
                  w["rb_rep"], sem_key=key)
    ld_b1 = P.dma("sync", [C.war.get("b1_sb"), arena_war], C.b1_sb[:, :],
                  w["b1r"], sem_key=key)

    def emit_srow(q):
        last = None
        for dc in range(NDC):
            last = nc.tensor.matmul(C.ps[1][0:1, 0:32], C.ones128[:, :],
                                    C.rwT[:, dc * 32:(dc + 1) * 32],
                                    start=(dc == 0), stop=(dc == NDC - 1))
        return last
    n_smm = P.op("tensor", [ld_rw, C.war.get("ps1")], emit_srow)
    n_sev = P.op("scalar", [n_smm, C.war.get("s_sb")],
                 lambda q: nc.scalar.activation(C.s_sb[:, :],
                                                C.ps[1][0:1, 0:32], AF.Copy))
    n_srep_mm = P.op("tensor", [n_sev, C.war.get("ps2")],
                     lambda q: nc.tensor.matmul(
                         C.ps[2][:, 0:32], C.onesrow_f[:, :], C.s_sb[:, :],
                         start=True, stop=True))
    n_srep = P.op("vector", [n_srep_mm, C.war.get("s_rep")],
                  lambda q: nc.vector.tensor_copy(C.s_rep[:, :],
                                                  C.ps[2][:, 0:32]))
    C.war["ps2"] = n_srep
    C.war["s_sb"] = n_srep_mm

    n_mc = P.dma("gpsimd", C.ln_row_dmas + [C.war.get("mean_cols")],
                 C.mean_cols[:, :],
                 C.row_scr[0].rearrange("(c p) -> p c", p=128),
                 sem_key=key + "g", allow_slow_non_contiguous=True)
    n_rc = P.dma("gpsimd", C.ln_row_dmas + [C.war.get("rstd_cols")],
                 C.rstd_cols[:, :],
                 C.row_scr[1].rearrange("(c p) -> p c", p=128),
                 sem_key=key + "g", allow_slow_non_contiguous=True)
    C.war["row_scr"] = [C.war.get("row_scr"), n_mc, n_rc]

    oh_nodes = []
    lg_free = C.war.get("lg")
    ps0_war = C.war.get("ps0")
    ps1_war = n_srep_mm
    tmp32_war = C.war.get("tmp32")
    for tt in range(NTT):
        ttl = slice(tt * 128, (tt + 1) * 128)

        def emit_router(q, ttl=ttl):
            last = None
            for dc in range(NDC):
                last = nc.tensor.matmul(C.ps[0][:, 0:32], C.x[dc][:, ttl],
                                        C.rwT[:, dc * 32:(dc + 1) * 32],
                                        start=(dc == 0), stop=(dc == NDC - 1))
            return last
        n_mm = P.op("tensor", x_ready + [ld_rw, ps0_war], emit_router)
        n_t1 = P.op("vector", [n_mc, n_srep, tmp32_war],
                    lambda q, tt=tt: nc.vector.tensor_scalar(
                        out=C.tmp32[:, :], in0=C.s_rep[:, :],
                        scalar1=C.mean_cols[:, tt:tt + 1], scalar2=None,
                        op0=ALU.mult))
        n_t2 = P.op("vector", [n_t1, n_mm, lg_free],
                    lambda q: nc.vector.tensor_tensor(
                        out=C.lg[:, :], in0=C.ps[0][:, 0:32],
                        in1=C.tmp32[:, :], op=ALU.subtract))
        ps0_war = n_t2
        tmp32_war = n_t2
        n_lg = P.op("vector", [n_t2, ld_rb, n_rc],
                    lambda q, tt=tt: nc.vector.scalar_tensor_tensor(
                        out=C.lg[:, :], in0=C.lg[:, :],
                        scalar=C.rstd_cols[:, tt:tt + 1],
                        in1=C.rb_rep[:, :], op0=ALU.mult, op1=ALU.add))
        n_mx = P.op("vector", [n_lg, C.war.get("mx")],
                    lambda q: nc.vector.reduce_max(C.mx[:, :], C.lg[:, :],
                                                   axis=AX.X))
        C.war["mx"] = n_mx
        n_oh = P.op("vector", [n_mx, C.war.get("ohtile")],
                    lambda q: nc.vector.tensor_scalar(
                        out=C.ohtile[:, :], in0=C.lg[:, :],
                        scalar1=C.mx[:, 0:1], scalar2=None, op0=ALU.is_equal))
        lg_free = n_oh
        n_tr = P.op("tensor", [n_oh, ps1_war],
                    lambda q: nc.tensor.transpose(
                        C.ps[1].bitcast(BF16)[0:32, 0:128], C.ohtile[:, :],
                        C.ident[:, :]))
        C.war["ohtile"] = n_tr
        n_ev = P.op("vector", [n_tr, C.war.get("ohT")],
                    lambda q, ttl=ttl: nc.vector.tensor_copy(
                        C.ohT[:, ttl], C.ps[1].bitcast(BF16)[0:32, 0:128]))
        ps1_war = n_ev
        oh_nodes.append(n_ev)
    C.war["lg"] = lg_free
    C.war["ps0"] = ps0_war
    C.war["ps1"] = ps1_war
    C.war["tmp32"] = tmp32_war
    C.war["s_rep"] = oh_nodes
    C.war["mean_cols"] = oh_nodes
    C.war["rstd_cols"] = oh_nodes
    n_ioh = P.op("vector", oh_nodes + [C.war.get("iohT")],
                 lambda q: nc.vector.tensor_scalar(
                     out=C.iohT[:, :], in0=C.ohT[:, :], scalar1=0.0,
                     scalar2=None, op0=ALU.is_equal))

    ld_sg = P.dma("gpsimd", [C.war.get("selgrp")], C.selgrp[:, :], w["selgrp"],
                  sem_key=key + "g")
    ld_b2 = P.dma("gpsimd", [C.war.get("b2_bf"), arena_war], C.b2_bf[:, :],
                  w["b2"], sem_key=key + "g")

    ps_free = [C.war.get(f"ps{i}") for i in range(8)]
    hall_war = [C.war.get("hall")] * 16
    acc_state = [[None] * NTS for _ in range(NDC)]
    x_nodes = [list(xf) if isinstance(xf, list) else ([xf] if xf else [])
               for xf in x_free]
    group_consumers = [C.war.get("w1g"), C.war.get("w2g")]
    tmpb_war = C.war.get("tmp512b")

    pcf = w.get("pc", []) if cached else []
    w1_war = [list(group_consumers) + pcf for _ in range(8)]
    w2_war = [list(group_consumers) + pcf for _ in range(8)]
    sel_war = [[C.war.get("selneg"), arena_war],
               [C.war.get("selneg"), arena_war]]
    for eg in range(4):
        g_ld = []
        cache_stores = [[] for _ in range(8)]
        selv = C.selneg[:, (eg % 2) * 1024:(eg % 2) * 1024 + 1024]
        ld_selg = P.dma("gpsimd", sel_war[eg % 2], selv,
                        w["selneg"][:, eg * 1024:(eg + 1) * 1024],
                        sem_key=f"ffng_{tag}")
        sel_war[eg % 2] = None
        for ei in range(8):
            e = eg * 8 + ei
            w1slot = C.w1g[:, :].rearrange("p (s c h) -> p s c h",
                                           s=8, c=NDC)[:, ei]
            w2slot = C.w2g[:, :].rearrange("p (s c d) -> p s c d",
                                           s=8, c=2)[:, ei]
            if cached:
                n1 = P.dma("sync" if ei % 2 else "scalar", w1_war[ei],
                           w1slot, w["w1c"][e],
                           sem_key=f"ffc{ei % 4}_{tag[:2]}")
                n2 = P.dma("scalar" if ei % 2 else "sync", w2_war[ei],
                           w2slot, w["w2c"][e],
                           sem_key=f"ffd{ei % 4}_{tag[:2]}")
            else:
                n1 = P.dma("gpsimd", w1_war[ei],
                           w1slot, w["w1"][e].rearrange("(c p) h -> p c h",
                                                        p=128),
                           sem_key=f"ffng{ei % 4}_{tag}")
                n2 = P.dma("gpsimd", w2_war[ei],
                           w2slot, w["w2"][e].rearrange("(c p) d -> p c d",
                                                        p=128),
                           sem_key=f"ffnh{ei % 4}_{tag}")
                cache_stores[ei].append(P.dma(
                    "sync" if ei % 2 else "scalar", [n1],
                    w["w1c"][e], w1slot, sem_key=f"ffc{ei % 4}_{tag[:2]}"))
                cache_stores[ei].append(P.dma(
                    "scalar" if ei % 2 else "sync", [n2],
                    w["w2c"][e], w2slot, sem_key=f"ffd{ei % 4}_{tag[:2]}"))
            g_ld += [n1, n2]
        group_consumers = []

        for ts in range(NTS):
            tsl = slice(ts * TS, (ts + 1) * TS)
            ha_nodes = []
            ha_mms = []
            for ei in range(8):
                e = eg * 8 + ei
                pp = ei % 2
                for ht in range(2):
                    b = pp * 2 + ht
                    psA = C.ps[b]

                    def emitA(q, e=e, ei=ei, ht=ht, psA=psA, tsl=tsl,
                              selv=selv):
                        for dc in range(NDC):
                            nc.tensor.matmul(
                                psA[:, :],
                                C.w1g[:, (ei * NDC + dc) * 256 + ht * 128:
                                      (ei * NDC + dc) * 256 + ht * 128 + 128],
                                C.xh[dc][:, tsl], start=(dc == 0), stop=False)
                        return nc.tensor.matmul(
                            psA[:, :], selv[:, ei * 128:(ei + 1) * 128],
                            C.iohT[:, tsl], start=False, stop=True)
                    n_mmA = P.op("tensor",
                                 [g_ld[2 * ei], n_ioh, ps_free[b], ld_selg]
                                 + xh_ready,
                                 emitA)
                    n_ev = P.op("scalar",
                                [n_mmA, ld_b1, hall_war[ei * 2 + ht]],
                                lambda q, ei=ei, ht=ht, e=e, psA=psA:
                                nc.scalar.activation(
                                    C.hall[:, (ei * 2 + ht) * TS:
                                           (ei * 2 + ht + 1) * TS],
                                    psA[:, :], AF.Relu,
                                    bias=C.b1_sb[:, e * 2 + ht:e * 2 + ht + 1]))
                    ps_free[b] = n_ev
                    ha_nodes.append(n_ev)
                    ha_mms.append(n_mmA)

            bank = lambda dt: (4 + dt) if dt < 4 else (dt - 4)
            ei_nodes = []
            for ei in range(8):

                def emit_ei(q, ei=ei):
                    last = None
                    for dt in range(NDC):
                        for hc in range(2):
                            last = nc.tensor.matmul(
                                C.ps[bank(dt)][:, :],
                                C.w2g[:, (ei * 2 + hc) * DM + dt * 128:
                                      (ei * 2 + hc) * DM + dt * 128 + 128],
                                C.hall[:, (ei * 2 + hc) * TS:
                                       (ei * 2 + hc + 1) * TS],
                                start=(ei == 0 and hc == 0),
                                stop=False)
                    return last
                bdeps = [ha_nodes[2 * ei], ha_nodes[2 * ei + 1],
                         g_ld[2 * ei + 1]]
                if ei == 0:
                    bdeps += ps_free + ha_nodes
                n_ei = P.op("tensor", bdeps, emit_ei)
                ei_nodes.append(n_ei)
            mmB_all = []
            for dt in range(NDC):
                b = bank(dt)
                psX = C.ps[b]

                def emit_fin(q, eg=eg, dt=dt, psX=psX, tsl=tsl):
                    nc.tensor.matmul(
                        psX[:, :], C.b2_bf[:, dt * 128:(dt + 1) * 128],
                        C.ohT[:, tsl], start=False, stop=False)
                    return nc.tensor.matmul(
                        psX[:, :], C.selgrp[:, eg * 128:(eg + 1) * 128],
                        C.ohT[:, tsl], start=False, stop=True)
                n_mmB = P.op("tensor",
                             ei_nodes + [ld_b2, ld_sg] + oh_nodes, emit_fin)
                mmB_all.append(n_mmB)
                n_rel = P.op("scalar", [n_mmB, tmpb_war],
                             lambda q, psX=psX: nc.scalar.activation(
                                 C.tmp512b[:, :], psX[:, :],
                                 AF.Relu, scale=0.5))
                ps_free[b] = n_rel
                n_add = P.op("vector", [n_rel] + x_nodes[dt],
                             lambda q, dt=dt, tsl=tsl:
                             nc.vector.tensor_tensor(
                                 out=C.x[dt][:, tsl],
                                 in0=C.x[dt][:, tsl],
                                 in1=C.tmp512b[:, :], op=ALU.add))
                tmpb_war = n_add
                acc_state[dt][ts] = n_add
                x_nodes[dt] = [n_add]
            for s in range(16):
                hall_war[s] = [ei_nodes[s // 2]]
            group_consumers = mmB_all + ha_nodes
            if ts == NTS - 1:
                for ei in range(8):
                    w1_war[ei] = [ha_nodes[2 * ei], ha_nodes[2 * ei + 1]] \
                        + cache_stores[ei]
                    w2_war[ei] = [ei_nodes[ei]] + cache_stores[ei]
                sel_war[eg % 2] = list(ha_nodes)

    tail_stores = [n for sl in cache_stores for n in sl]
    C.war["w1g"] = group_consumers + tail_stores
    C.war["w2g"] = group_consumers + tail_stores
    C.war["hall"] = group_consumers
    C.war["tmp512b"] = tmpb_war
    C.war["rwT"] = oh_nodes
    C.war["rb_rep"] = oh_nodes
    C.war["negs"] = oh_nodes
    C.war["rstd_cols"] = oh_nodes
    C.war["r_mean"] = [C.war.get("r_mean")] + oh_nodes
    C.war["r_rstd"] = [C.war.get("r_rstd")] + oh_nodes
    C.war["selgrp"] = group_consumers
    C.war["selneg"] = sel_war[0] + sel_war[1]
    C.war["b1_sb"] = group_consumers
    C.war["ohT"] = group_consumers
    C.war["iohT"] = group_consumers
    C.war["b2_bf"] = group_consumers
    for i in range(8):
        C.war[f"ps{i}"] = ps_free[i]
    C.war["arena"] = [acc_state[dt][NTS - 1]
                      for dt in range(NDC)] + tail_stores
    return [[acc_state[dt][ts] for ts in range(NTS)] for dt in range(NDC)]


def _ffn_host(prefix, inputs):
    rw = inputs[f"{prefix}_router_w"]
    rb = inputs[f"{prefix}_router_b"]
    w1 = inputs[f"{prefix}_weights1"]
    b1 = inputs[f"{prefix}_bias1"]
    w2 = inputs[f"{prefix}_weights2"]
    b2 = inputs[f"{prefix}_bias2"]
    sel = np.zeros((32, 32 * 128), dtype=BF)
    for e in range(32):
        sel[e, e * 128:(e + 1) * 128] = BF(NEG)
    selg = np.zeros((32, 4 * 128), dtype=BF)
    for g in range(4):
        for e in range(32):
            if e // 8 != g:
                selg[e, g * 128:(g + 1) * 128] = BF(NEG)
    return {
        "selgrp": selg,
        "rwT": np.ascontiguousarray(rw.T),
        "rb_rep": np.ascontiguousarray(
            np.broadcast_to(rb[None, :], (128, 32))),
        "selneg": sel,
        "w1": np.ascontiguousarray(w1),
        "b1r": np.ascontiguousarray(
            b1.reshape(32, 2, 128).transpose(2, 0, 1).reshape(128, 64)),
        "w2": np.ascontiguousarray(w2),
        "b2": np.ascontiguousarray(b2),
    }


def _const_host():
    return {
        "c_ident": np.eye(128, dtype=BF),
        "c_ones_f": np.ones((128, 1), dtype=np.float32),
        "c_ones_b": np.ones((128, 1), dtype=BF),
        "c_onesrow_f": np.ones((1, 128), dtype=np.float32),
        "c_eps": np.full((128, 1), 1e-5, dtype=np.float32),
    }


def declare_ffn_dram(nc, tag):
    w = {}
    w["rwT"] = nc.dram_tensor(f"{tag}_rwT", [DM, 32], F32,
                              kind="ExternalInput").ap()
    w["rb_rep"] = nc.dram_tensor(f"{tag}_rb_rep", [128, 32], F32,
                                 kind="ExternalInput").ap()
    w["selneg"] = nc.dram_tensor(f"{tag}_selneg", [32, 32 * 128], BF16,
                                 kind="ExternalInput").ap()
    w1 = nc.dram_tensor(f"{tag}_w1", [32, DM, 256], F32,
                        kind="ExternalInput").ap()
    w["w1"] = [w1[e] for e in range(32)]
    w["b1r"] = nc.dram_tensor(f"{tag}_b1r", [128, 64], F32,
                              kind="ExternalInput").ap()
    w2 = nc.dram_tensor(f"{tag}_w2", [32, 256, DM], F32,
                        kind="ExternalInput").ap()
    w["w2"] = [w2[e] for e in range(32)]
    w["b2"] = nc.dram_tensor(f"{tag}_b2", [32, DM], F32,
                             kind="ExternalInput").ap()
    w["selgrp"] = nc.dram_tensor(f"{tag}_selgrp", [32, 4 * 128], BF16,
                                 kind="ExternalInput").ap()
    w["w1c"] = nc.dram_tensor(f"{tag}_w1c", [32, 128, NDC * 256], BF16,
                              kind="Internal").ap()
    w["w2c"] = nc.dram_tensor(f"{tag}_w2c", [32, 128, 2 * DM], BF16,
                              kind="Internal").ap()
    return w


def declare_const_dram(nc):
    h = {}
    h["c_ident"] = nc.dram_tensor("c_ident", [128, 128], BF16,
                                  kind="ExternalInput").ap()
    h["c_ones_f"] = nc.dram_tensor("c_ones_f", [128, 1], F32,
                                   kind="ExternalInput").ap()
    h["c_ones_b"] = nc.dram_tensor("c_ones_b", [128, 1], BF16,
                                   kind="ExternalInput").ap()
    h["c_onesrow_f"] = nc.dram_tensor("c_onesrow_f", [1, 128], F32,
                                      kind="ExternalInput").ap()
    h["c_eps"] = nc.dram_tensor("c_eps", [128, 1], F32,
                                kind="ExternalInput").ap()
    return h



def load_x(C, xin_ap):
    P = C.P
    nodes = []
    for dc in range(NDC):
        nodes.append(P.dma("sync", [], C.x[dc][:, :],
                           xin_ap[dc * 128:(dc + 1) * 128, :], sem_key="x_dma"))
    return nodes


def store_x(C, xout_ap, x_nodes_per_dc):
    P = C.P
    st = []
    for dc in range(NDC):
        st.append(P.dma("sync", x_nodes_per_dc[dc], xout_ap[dc * 128:(dc + 1) * 128, :],
                        C.x[dc][:, :], sem_key="xout_dma"))
    P.op("gpsimd", st, lambda q: q.engine_nop())
    return st



def build_ffn_sublayer_program():
    nc = bass.Bass("TRN2", target_bir_lowering=False)
    es = contextlib.ExitStack()
    xin = nc.dram_tensor("xin", [DM, T], F32, kind="ExternalInput").ap()
    xout = nc.dram_tensor("xout", [DM, T], F32, kind="ExternalOutput").ap()
    hconst = declare_const_dram(nc)
    w = declare_ffn_dram(nc, "f")
    C = alloc_common(nc, es)
    alloc_ffn_views(C)
    load_consts(C, hconst)
    x_ready = load_x(C, xin)
    xh_nodes, n_meanbc, n_rstdbc = build_ln(C, x_ready, [], "ln0")
    x_upd = build_ffn(C, xh_nodes, x_ready, w, "f",
                      (x_ready, n_meanbc, n_rstdbc))
    store_x(C, xout, x_upd)
    C.P.run(es)
    es.close()
    return nc


def ffn_sublayer_inputs(prefix, inputs, xT):
    d = {"xin": xT}
    d.update(_const_host())
    wh = _ffn_host(prefix, inputs)
    return d | {f"f_{k}": wh[k] for k in
                ("rwT", "rb_rep", "selneg", "selgrp", "w1", "b1r", "w2", "b2")}


def build_fused_program():
    nc = bass.Bass("TRN2", target_bir_lowering=False)
    es = contextlib.ExitStack()
    xin = nc.dram_tensor("xin", [DM, T], F32, kind="ExternalInput").ap()
    xout = nc.dram_tensor("xout", [DM, T], F32, kind="ExternalOutput").ap()
    hconst = declare_const_dram(nc)
    w_f0 = declare_ffn_dram(nc, "f0")
    w_f1 = declare_ffn_dram(nc, "f1")
    w_m = declare_mamba_dram(nc, "dm0")
    w_a = declare_attn_dram(nc, "da1")
    C = alloc_common(nc, es)
    alloc_ffn_views(C)
    alloc_mamba_views(C)
    alloc_attn_views(C)
    load_consts(C, hconst)
    x_nodes = load_x(C, xin)

    x_pd = [[n] for n in x_nodes]

    steps = [("ffn", w_f0, "f0a", False), ("mamba", w_m, "m0", False),
             ("ffn", w_f0, "f0b", True), ("ffn", w_f1, "f1a", True),
             ("attn", w_a, "a1", True), ("ffn", w_f1, "f1b", True)]
    for i, (kind, w, tag, cached) in enumerate(steps):
        if kind == "mamba":
            fpc = []
            for e in range(32):
                fpc.append(C.P.dma(
                    "gpsimd", [],
                    w_f1["w1c"][e].rearrange("p (c h) -> p c h", h=256),
                    w_f1["w1"][e].rearrange("(c p) h -> p c h", p=128),
                    sem_key=f"fpc{e % 4}"))
                fpc.append(C.P.dma(
                    "gpsimd", [],
                    w_f1["w2c"][e].rearrange("p (c d) -> p c d", d=DM),
                    w_f1["w2"][e].rearrange("(c p) d -> p c d", p=128),
                    sem_key=f"fpd{e % 4}"))
            w_f1["pc"] = fpc
        xh_nodes, _, _ = build_ln(C, x_pd, [x_pd], f"ln{i}")
        if kind == "ffn":
            x_pd = build_ffn(C, xh_nodes, x_pd, w, tag, (x_pd, None, None),
                             cached=cached)
            if tag == "f0a" and False:
                pc = []
                for k in range(4):
                    pc.append(C.P.dma(
                        "gpsimd", [],
                        w_m["winTc"][:, :, k * 1024:(k + 1) * 1024],
                        w_m["winT"].rearrange("(c p) e -> p c e",
                                              p=128)[:, :, k * 1024:(k + 1) * 1024],
                        sem_key=f"mpcw{k % 2}"))
                for k in range(2):
                    pc.append(C.P.dma(
                        "gpsimd", [],
                        w_m["woTc"][:, k * 8:(k + 1) * 8, :],
                        w_m["woT"].rearrange("(c p) f -> p c f",
                                             p=128)[:, k * 8:(k + 1) * 8, :],
                        sem_key=f"mpco{k % 2}"))
                pc.append(C.P.dma(
                    "gpsimd", [], w_m["wxTc"],
                    w_m["wxT"].rearrange("(c p) f -> p c f", p=128),
                    sem_key="mpco0"))
                w_m["pc"] = pc
        elif kind == "mamba":
            x_pd = build_mamba(C, xh_nodes, x_pd, w, tag, cached=False)
        else:
            x_pd = build_attn(C, xh_nodes, x_pd, w, tag, cached=True)
        if kind == "ffn" and tag == "f0b":
            pc = []
            for j, n in enumerate(("wqT", "wkT", "wvT", "wgT", "woT")):
                pc.append(C.P.dma(
                    "gpsimd", [], w_a[n + "c"],
                    w_a[n].rearrange("(c p) f -> p c f", p=128),
                    sem_key=f"apc{j % 2}"))
            w_a["pc"] = pc
    store_x(C, xout, x_pd)
    C.P.run(es)
    es.close()
    return nc


def fused_inputs(inputs, xT):
    d = {"xin": xT}
    d.update(_const_host())
    for tag, prefix in (("f0", "L0_ffn"), ("f1", "L1_ffn")):
        wh = _ffn_host(prefix, inputs)
        d.update({f"{tag}_{k}": wh[k] for k in
                  ("rwT", "rb_rep", "selneg", "selgrp", "w1", "b1r", "w2", "b2")})
    wh = _mamba_host("L0_attn", inputs)
    d.update({f"dm0_{k}": wh[k] for k in wh})
    wh = _attn_host("L1_attn", inputs)
    d.update({f"da1_{k}": wh[k] for k in wh})
    return d


_PROG_CACHE = {}


def _get_prog(name, builder):
    if name not in _PROG_CACHE:
        _PROG_CACHE[name] = builder()
    return _PROG_CACHE[name]


def kernel(**inputs):
    x = inputs["L0_attn_x"]
    prog = _get_prog("fused", build_fused_program)
    in_maps = [fused_inputs(inputs, np.ascontiguousarray(x[b].T))
               for b in range(8)]
    res = run_bass_kernel_spmd(prog, in_maps, core_ids=list(range(8)))
    return np.stack([np.asarray(res.results[b]["xout"]).T
                     for b in range(8)]).astype(np.float32)

DI = 2048
NET = DI // 128
DS = 64
QC = 128
NCH = T // QC
QS = 16
TH = 1024


def alloc_mamba_views(C):
    a = ArenaAlloc(C)
    m_xc_all = a.b16(128, NET * (4 + TH))
    C.m_xc = [m_xc_all[:, i * (4 + TH):(i + 1) * (4 + TH)] for i in range(NET)]
    silz1 = a.b16(128, NET * 128)
    C.m_silz = [silz1, silz1]
    C.m_hbf = a.b16(64, DI)
    dt2 = a.b16(64, 2 * QC)
    C.m_dt = [dt2[:, 0:QC], dt2[:, QC:2 * QC]]
    c2 = a.b16(64, 2 * QC)
    C.m_C = [c2[:, 0:QC], c2[:, QC:2 * QC]]
    bd2 = a.b16(64, 2 * QC)
    C.m_Bd = [bd2[:, 0:QC], bd2[:, QC:2 * QC]]
    C.m_woT = a.b16(128, NET * 1024)
    C.m_wxT = a.b16(128, NET * 132)
    C.m_win = a.b16(128, 2 * NDC * 128)
    C.m_cw = a.f32(128, NET * 4)
    C.m_cb = a.f32(128, NET)
    C.m_D = a.f32(128, NET)
    C.m_Alog = a.f32(64, 1)
    C.m_negA = C.m_Alog
    C.m_trimask = a.b16(64, 256)
    C.m_xtail = a.b16(128, NET * 4)
    C.m_convt = C.tmp512b
    C.m_zst = [C.xsq, a.b16(128, TS)]
    C.m_e2 = a.b16(64, 8 * 256)
    def dbl_f(parts, cols):
        v = a.f32(parts, 2 * cols)
        return [v[:, 0:cols], v[:, cols:2 * cols]]

    def dbl_b(parts, cols):
        v = a.b16(parts, 2 * cols)
        return [v[:, 0:cols], v[:, cols:2 * cols]]
    C.m_cum = dbl_f(64, QC)
    C.m_t64a = dbl_f(64, QC)
    C.m_t64b = dbl_b(64, QC)
    C.m_Bst = dbl_b(64, QC)
    C.m_Cst = dbl_b(64, QC)
    bhts = a.b16(128, 64)
    C.m_BhT = [bhts, bhts]
    C.m_Chl = dbl_b(64, 112)
    C.m_Bhr = dbl_b(64, 112)
    C.m_expdb = dbl_f(64, 8)
    C.m_de = dbl_f(64, 1)
    C.m_gt = a.b16(128, QC)
    C.m_band = C.m_win[64:65, 0:8 * 256]
    C.m_xcT = a.b16(128, NET * 128)
    C.m_ych = C.m_xcT
    C.m_yt = a.b16(128, 128)
    nc = C.nc
    kind = "ExternalOutput" if getattr(C, "debug", False) else "Internal"
    C.m_zscr = nc.dram_tensor("m_zscr", [DI, T], BF16, kind=kind).ap()
    C.m_bandscr = nc.dram_tensor("m_bandscr", [8 * 256], BF16,
                                 kind="Internal").ap()
    if getattr(C, "debug", False):
        C.m_dbg_gt = nc.dram_tensor("m_dbg_gt", [NCH, QC, QC], BF16,
                                    kind="ExternalOutput").ap()
        C.m_dbg_cum = nc.dram_tensor("m_dbg_cum", [NCH, 64, QC], F32,
                                     kind="ExternalOutput").ap()
    C.m_scr2 = nc.dram_tensor("m_scr2", [3, 64, T], BF16, kind=kind).ap()


def declare_mamba_dram(nc, tag):
    w = {}
    w["winT"] = nc.dram_tensor(f"{tag}_winT", [DM, 2 * DI], F32,
                               kind="ExternalInput").ap()
    w["wxT"] = nc.dram_tensor(f"{tag}_wxT", [DI, 129], F32,
                              kind="ExternalInput").ap()
    w["woT"] = nc.dram_tensor(f"{tag}_woT", [DI, DM], F32,
                              kind="ExternalInput").ap()
    w["cw"] = nc.dram_tensor(f"{tag}_cw", [128, NET * 4], F32,
                             kind="ExternalInput").ap()
    w["cb"] = nc.dram_tensor(f"{tag}_cb", [128, NET], F32,
                             kind="ExternalInput").ap()
    w["D"] = nc.dram_tensor(f"{tag}_D", [128, NET], F32,
                            kind="ExternalInput").ap()
    w["Alog"] = nc.dram_tensor(f"{tag}_Alog", [64, 1], F32,
                               kind="ExternalInput").ap()
    w["trimask"] = nc.dram_tensor(f"{tag}_trimask", [64, 256], BF16,
                                  kind="ExternalInput").ap()
    w["winTc"] = nc.dram_tensor(f"{tag}_winTc", [128, NDC, 2 * DI], BF16,
                                kind="Internal").ap()
    w["wxTc"] = nc.dram_tensor(f"{tag}_wxTc", [128, 16, 129], BF16,
                               kind="Internal").ap()
    w["woTc"] = nc.dram_tensor(f"{tag}_woTc", [128, 16, DM], BF16,
                               kind="Internal").ap()
    return w


def _mamba_host(prefix, inputs):
    w_in = inputs[f"{prefix}_W_in"]
    conv_w = inputs[f"{prefix}_conv_w"]
    conv_b = inputs[f"{prefix}_conv_b"]
    w_x = inputs[f"{prefix}_W_x"]
    a_log = inputs[f"{prefix}_A_log"]
    d_vec = inputs[f"{prefix}_D"]
    w_out = inputs[f"{prefix}_W_out"]
    tp, tt = np.meshgrid(np.arange(QS), np.arange(QS), indexing="ij")
    tm = np.where(tp <= tt, 0.0, NEG).astype(np.float32).reshape(-1)
    return {
        "winT": np.ascontiguousarray(w_in.T),
        "wxT": np.ascontiguousarray(w_x.T),
        "woT": np.ascontiguousarray(w_out.T),
        "cw": np.ascontiguousarray(
            conv_w.reshape(NET, 128, 4).transpose(1, 0, 2)
            .reshape(128, NET * 4)),
        "cb": np.ascontiguousarray(conv_b.reshape(NET, 128).T),
        "D": np.ascontiguousarray(d_vec.reshape(NET, 128).T),
        "Alog": np.ascontiguousarray(a_log.reshape(64, 1)),
        "trimask": np.ascontiguousarray(
            np.broadcast_to(tm, (64, 256)).astype(BF)),
    }


def build_mamba(C, xh_ready, x_free, w, tag, cached=False):
    nc, P = C.nc, C.P
    key = f"mamw_{tag}"
    gkey = f"mams_{tag}"
    zkey = f"mamz_{tag}"
    arena_war = C.war.get("arena")
    pc = w.get("pc", []) if cached else []

    ld = {}
    if cached:
        ld["wxT"] = P.dma("sync", [C.war.get("m_wxT"), arena_war] + pc,
                          C.m_wxT[:, :].rearrange("p (c f) -> p c f",
                                                  f=132)[:, :, 0:129],
                          w["wxTc"], sem_key=key + "x")
        ld["woT"] = P.dma("scalar", [C.war.get("m_woT"), arena_war] + pc,
                          C.m_woT[:, :].rearrange("p (c f) -> p c f", f=1024),
                          w["woTc"], sem_key=key + "o")
    else:
        ld["wxT"] = P.dma("gpsimd", [C.war.get("m_wxT"), arena_war],
                          C.m_wxT[:, :].rearrange("p (c f) -> p c f",
                                                  f=132)[:, :, 0:129],
                          w["wxT"].rearrange("(c p) f -> p c f", p=128),
                          sem_key=key + "x")
        ld["woT"] = P.dma("gpsimd", [C.war.get("m_woT"), arena_war],
                          C.m_woT[:, :].rearrange("p (c f) -> p c f", f=1024),
                          w["woT"].rearrange("(c p) f -> p c f", p=128),
                          sem_key=key + "o")
    for name in ("cw", "cb", "D", "Alog", "trimask"):
        ld[name] = P.dma("sync", [C.war.get("m_" + name), arena_war],
                         getattr(C, "m_" + name)[:, :], w[name], sem_key=key)

    n_negA = P.op("scalar", [ld["Alog"], C.war.get("m_negA")],
                  lambda q: nc.scalar.activation(C.m_negA[:, :],
                                                 C.m_Alog[:, :], AF.Exp))
    n_negA = P.op("vector", [n_negA],
                  lambda q: nc.vector.tensor_scalar(
                      out=C.m_negA[:, :], in0=C.m_negA[:, :], scalar1=-1.0,
                      scalar2=None, op0=ALU.mult))

    n_hb0 = P.op("vector", [C.war.get("m_hbf"), arena_war],
                 lambda q: nc.vector.memset(C.m_hbf[:, :], 0.0))
    st = Ctx()
    st.hbf_node = n_hb0
    st.psw = [C.war.get(f"ps{i}") for i in range(8)]
    st.x_nodes = [list(xf) if isinstance(xf, list) else ([xf] if xf else [])
                  for xf in x_free]
    st.w = w
    st.key = key
    st.xh_ready = xh_ready
    st.silz_ld = None
    st.silz_readers = []
    st.zscr_stores = []

    win_war = [C.war.get("m_win"), arena_war]
    xc_war = [[C.war.get(f"m_xc{i}"), arena_war] for i in range(NET)]
    xtail_war = C.war.get("m_xtail")
    convt_war = C.war.get("tmp512b")
    zst_war = [C.war.get("xsq"), C.war.get("m_zst1")]
    e2_war = C.war.get("m_e2")

    for H in range(2):
        hsl = slice(H * TH, (H + 1) * TH)
        bc_mm = [None, None]
        dt_mm = [None, None]
        ld_idx = 0
        deferred_bc = None
        for et in range(NET):
            slot_x = ld_idx % 2
            ld_idx += 1
            slot_z = ld_idx % 2
            ld_idx += 1
            if cached:
                n_wx = P.dma("gpsimd", [win_war[slot_x]] + pc,
                             C.m_win[:, :].rearrange("p (s c e) -> p s c e",
                                                     s=2, c=NDC)[:, slot_x],
                             w["winTc"][:, :, et * 128:(et + 1) * 128],
                             sem_key=key + str(slot_x))
                n_wz = P.dma("gpsimd", [win_war[slot_z]] + pc,
                             C.m_win[:, :].rearrange("p (s c e) -> p s c e",
                                                     s=2, c=NDC)[:, slot_z],
                             w["winTc"][:, :, DI + et * 128:DI + (et + 1) * 128],
                             sem_key=key + str(slot_z))
            else:
                n_wx = P.dma("gpsimd", [win_war[slot_x]],
                             C.m_win[:, :].rearrange("p (s c e) -> p s c e",
                                                     s=2, c=NDC)[:, slot_x],
                             w["winT"][:, et * 128:(et + 1) * 128]
                             .rearrange("(c p) e -> p c e", p=128),
                             sem_key=key + str(slot_x))
                n_wz = P.dma("gpsimd", [win_war[slot_z]],
                             C.m_win[:, :].rearrange("p (s c e) -> p s c e",
                                                     s=2, c=NDC)[:, slot_z],
                             w["winT"][:, DI + et * 128:DI + (et + 1) * 128]
                             .rearrange("(c p) e -> p c e", p=128),
                             sem_key=key + str(slot_z))

            xin_done = {}
            prev_ev = None
            for tsh in range(2):
                tsl = slice(H * TH + tsh * TS, H * TH + (tsh + 1) * TS)
                lsl = slice(4 + tsh * TS, 4 + (tsh + 1) * TS)
                psA = C.ps[tsh]

                def emit_win(q, slot=slot_x, tsl=tsl, psA=psA):
                    last = None
                    for dc in range(NDC):
                        last = nc.tensor.matmul(
                            psA[:, :],
                            C.m_win[:, (slot * NDC + dc) * 128:
                                    (slot * NDC + dc + 1) * 128],
                            C.xh[dc][:, tsl],
                            start=(dc == 0), stop=(dc == NDC - 1))
                    return last
                n_mm = P.op("tensor", [n_wx, st.psw[tsh]] + xh_ready, emit_win)
                n_ev = P.op("scalar", [n_mm, prev_ev] + xc_war[et],
                            lambda q, et=et, lsl=lsl, psA=psA:
                            nc.scalar.activation(C.m_xc[et][:, lsl],
                                                 psA[:, :], AF.Copy))
                prev_ev = n_ev
                st.psw[tsh] = n_ev
                xin_done[tsh] = n_ev
            win_x_last = n_mm

            for tsh in range(2):
                tsl = slice(H * TH + tsh * TS, H * TH + (tsh + 1) * TS)
                psZ = C.ps[6 + tsh]

                def emit_z(q, slot=slot_z, tsl=tsl, psZ=psZ):
                    last = None
                    for dc in range(NDC):
                        last = nc.tensor.matmul(
                            psZ[:, :],
                            C.m_win[:, (slot * NDC + dc) * 128:
                                    (slot * NDC + dc + 1) * 128],
                            C.xh[dc][:, tsl],
                            start=(dc == 0), stop=(dc == NDC - 1))
                    return last
                n_zmm = P.op("tensor", [n_wz, st.psw[6 + tsh]] + xh_ready,
                             emit_z)
                zs = (et * 2 + tsh) % 2
                n_zev = P.op("scalar", [n_zmm, zst_war[zs]],
                             lambda q, zs=zs, psZ=psZ: nc.scalar.activation(
                                 C.m_zst[zs][:, :], psZ[:, :], AF.Silu))
                st.psw[6 + tsh] = n_zev
                n_zst = P.dma("sync" if zs == 0 else "scalar",
                              [n_zev],
                              C.m_zscr[et * 128:(et + 1) * 128, tsl],
                              C.m_zst[zs][:, :],
                              sem_key=zkey + ("s" if zs == 0 else "c"))
                zst_war[zs] = n_zst
                st.zscr_stores.append(n_zst)
            win_war[slot_z] = n_zmm
            win_war[slot_x] = win_x_last

            if H == 0:
                n_pad = P.op("vector", xc_war[et],
                             lambda q, et=et: nc.vector.memset(
                                 C.m_xc[et][:, 0:4], 0.0))
                n_tail = P.op("vector", [xin_done[1], xtail_war],
                              lambda q, et=et: nc.vector.tensor_copy(
                                  C.m_xtail[:, et * 4 + 1:(et + 1) * 4],
                                  C.m_xc[et][:, 4 + TH - 3:4 + TH]))
                xtail_war = n_tail
            else:
                n_pad = P.op("vector", [xtail_war] + xc_war[et],
                             lambda q, et=et: nc.vector.tensor_copy(
                                 C.m_xc[et][:, 1:4],
                                 C.m_xtail[:, et * 4 + 1:(et + 1) * 4]))
                n_tail = None

            deps_conv = [xin_done[0], xin_done[1], n_pad, ld["cw"], ld["cb"]]
            if n_tail is not None:
                deps_conv.append(n_tail)
            n_back = None
            for tsh in (1, 0):
                lo = 4 + tsh * TS
                n_c = P.op("vector", deps_conv + [convt_war],
                           lambda q, et=et, lo=lo: nc.vector.tensor_scalar(
                               out=C.m_convt[:, :],
                               in0=C.m_xc[et][:, lo - 3:lo - 3 + TS],
                               scalar1=C.m_cw[:, et * 4:et * 4 + 1],
                               scalar2=None, op0=ALU.mult))
                for j in range(1, 4):
                    n_c = P.op("vector", [n_c],
                               lambda q, et=et, lo=lo, j=j:
                               nc.vector.scalar_tensor_tensor(
                                   out=C.m_convt[:, :],
                                   in0=C.m_xc[et][:, lo - 3 + j:lo - 3 + j + TS],
                                   scalar=C.m_cw[:, et * 4 + j:et * 4 + j + 1],
                                   in1=C.m_convt[:, :],
                                   op0=ALU.mult, op1=ALU.add))
                n_s = P.op("scalar", [n_c, n_back],
                           lambda q, et=et, lo=lo: nc.scalar.activation(
                               C.m_xc[et][:, lo:lo + TS],
                               C.m_convt[:, :], AF.Silu,
                               bias=C.m_cb[:, et:et + 1]))
                convt_war = n_s
                n_back = n_s
            xc_war[et] = [n_back]

            def emit_bc(et, n_back):
                for tsh in range(2):
                    lsl = slice(4 + tsh * TS, 4 + (tsh + 1) * TS)
                    start = (et == 0)
                    stop = (et == NET - 1)
                    dep0 = [st.psw[2 + tsh], st.psw[4 + tsh], ld["wxT"]] \
                        if et == 0 else []
                    n_bcm = P.op("tensor", [n_back] + dep0,
                                 lambda q, et=et, lsl=lsl, tsh=tsh,
                                 start=start, stop=stop: nc.tensor.matmul(
                                     C.ps[2 + tsh][:, :],
                                     C.m_wxT[:, et * 132:et * 132 + 128],
                                     C.m_xc[et][:, lsl], start=start,
                                     stop=stop))
                    n_dtm = P.op("tensor", [n_back] + dep0,
                                 lambda q, et=et, lsl=lsl, tsh=tsh,
                                 start=start, stop=stop: nc.tensor.matmul(
                                     C.ps[4 + tsh][0:1, :],
                                     C.m_wxT[:, et * 132 + 128:et * 132 + 129],
                                     C.m_xc[et][:, lsl], start=start,
                                     stop=stop))
                    if stop:
                        bc_mm[tsh] = n_bcm
                        dt_mm[tsh] = n_dtm
            if deferred_bc is not None:
                emit_bc(*deferred_bc)
            deferred_bc = (et, n_back)
            if et == NET - 1:
                emit_bc(*deferred_bc)
                deferred_bc = None

        e2r0 = C.m_e2[:, 0:TH]
        e2r1 = C.m_e2[:, TH:2 * TH]
        dt_dmas = []
        for tsh in range(2):
            tsl = slice(H * TH + tsh * TS, H * TH + (tsh + 1) * TS)
            n_r = P.op("scalar", [dt_mm[tsh], C.war.get("rowA")],
                       lambda q, tsh=tsh: nc.scalar.activation(
                           C.rowA[:, :], C.ps[4 + tsh][0:1, :], AF.Copy))
            C.war["rowA"] = n_r
            n_d = P.dma("gpsimd", [n_r, C.war.get("row_scr2")],
                        C.row_scr[2:3, tsl], C.rowA[:, :], sem_key=gkey)
            C.war["rowA"] = n_d
            st.psw[4 + tsh] = n_r
            dt_dmas.append(n_d)
        n_bcdt = P.dma("gpsimd", dt_dmas + [e2_war],
                       e2r0, C.row_scr[2:3, hsl].broadcast_to([64, TH]),
                       sem_key=gkey)
        C.war["row_scr2"] = n_bcdt
        n_sp1 = P.op("scalar", [n_bcdt],
                     lambda q: nc.scalar.activation(e2r1, e2r0, AF.Abs))
        n_sp2 = P.op("scalar", [n_sp1],
                     lambda q: nc.scalar.activation(e2r1, e2r1, AF.Exp,
                                                    scale=-1.0))
        n_sp3 = P.op("scalar", [n_sp2] + C.const_nodes,
                     lambda q: nc.scalar.activation(e2r1, e2r1, AF.Ln,
                                                    bias=1.0))
        n_sp4 = P.op("vector", [n_bcdt, n_sp1],
                     lambda q: nc.vector.tensor_scalar(
                         out=e2r0, in0=e2r0, scalar1=0.0, scalar2=None,
                         op0=ALU.max))
        n_dtf = P.op("vector", [n_sp3, n_sp4],
                     lambda q: nc.vector.tensor_tensor(
                         out=e2r0, in0=e2r0, in1=e2r1, op=ALU.add))
        n_dtp = P.dma("gpsimd", [n_dtf, C.war.get("scr2_dt")],
                      C.m_scr2[0, 0:1, hsl], e2r0[0:1, :], sem_key=gkey)
        C.war["scr2_dt"] = n_dtp
        st_dt = n_dtp

        bdc_stores = []
        for tsh in range(2):
            tsl = slice(tsh * TS, (tsh + 1) * TS)
            gsl = slice(H * TH + tsh * TS, H * TH + (tsh + 1) * TS)
            n_bd = P.op("vector", [bc_mm[tsh], n_dtf, n_dtp],
                        lambda q, tsh=tsh, tsl=tsl: nc.vector.tensor_tensor(
                            out=e2r1[:, tsl], in0=C.ps[2 + tsh][0:64, :],
                            in1=e2r0[:, tsl], op=ALU.mult))
            n_sbd = P.dma("sync", [n_bd, C.war.get("scr2_bd")],
                          C.m_scr2[1, :, gsl], e2r1[:, tsl], sem_key=gkey + "s")
            C.war["scr2_bd"] = n_sbd
            n_cc = P.op("vector", [bc_mm[tsh], n_sbd],
                        lambda q, tsh=tsh, tsl=tsl: nc.vector.tensor_copy(
                            e2r1[:, tsl], C.ps[2 + tsh][64:128, :]))
            n_scc = P.dma("sync", [n_cc, C.war.get("scr2_c")],
                          C.m_scr2[2, :, gsl], e2r1[:, tsl], sem_key=gkey + "s")
            C.war["scr2_c"] = n_scc
            st.psw[2 + tsh] = [n_bd, n_cc]
            bdc_stores += [n_sbd, n_scc]
        e2_war = [n_scc, n_dtp]

        st.scr_ready = bdc_stores + [st_dt]
        st.xc_nodes = [xc_war[i][0] for i in range(NET)]
        st.xc_readers = [[] for _ in range(NET)]
        st.band_war = [win_war[0], win_war[1]]
        st.band_nodes = []
        st.stairB_war = [win_war[0], win_war[1]]

        if H == 0:
            st.tile_readers = [[], []]
        pend = _chunk_loads(C, P, st, H * 8, 0, tag)
        zld = _silz_load(C, P, st, H * 8, tag)
        pr = _chunk_prep(C, P, nc, st, H * 8, 0, n_negA, ld, tag, pend, 0)
        for cc8 in range(8):
            c = H * 8 + cc8
            cur_pr = pr
            if cc8 < 7:
                pend = _chunk_loads(C, P, st, c + 1, (cc8 + 1) % 2, tag)
                pr = _chunk_prep(C, P, nc, st, c + 1, cc8 + 1, n_negA, ld,
                                 tag, pend, (cc8 + 1) % 2)
            e2_war, zld = _scan_chunk(C, P, nc, st, c, cc8, n_negA, ld, tag,
                                      e2_war, cur_pr, cc8 % 2, zld, H)
        for i in range(NET):
            xc_war[i] = xc_war[i] + st.xc_readers[i]
        win_war = [[win_war[0]] + st.band_nodes + st.stairB_war,
                   [win_war[1]] + st.band_nodes + st.stairB_war]

    C.war["m_win"] = win_war
    C.war["m_xtail"] = xtail_war
    C.war["tmp512b"] = convt_war
    C.war["m_hbf"] = st.hbf_node
    C.war["m_e2"] = e2_war
    C.war["xsq"] = zst_war[0]
    C.war["m_zst1"] = zst_war[1]
    for i in range(NET):
        C.war[f"m_xc{i}"] = xc_war[i]
    for i in range(8):
        C.war[f"ps{i}"] = st.psw[i]
    allc = [n for dc in range(NDC) for n in st.x_nodes[dc]]
    C.war["arena"] = allc + st.silz_readers
    C.war["zscr"] = st.last_zld
    return [st.x_nodes[dc] for dc in range(NDC)]


def _chunk_loads(C, P, st, c, par, tag):
    csl = slice(c * QC, (c + 1) * QC)
    gkey = f"mams_{tag}"
    zkey = f"mamz_{tag}"
    rdrs = st.tile_readers[par]
    n_dtl = P.dma("sync", st.scr_ready + rdrs,
                  C.m_dt[par][:, :],
                  C.m_scr2[0, 0:1, csl].broadcast_to([64, QC]), sem_key=gkey + "s")
    n_bdl = P.dma("sync", st.scr_ready + rdrs,
                  C.m_Bd[par][:, :], C.m_scr2[1, :, csl], sem_key=gkey + "s")
    n_ccl = P.dma("scalar", st.scr_ready + rdrs,
                  C.m_C[par][:, :], C.m_scr2[2, :, csl], sem_key=gkey + "c")
    st.tile_readers[par] = []
    return (n_dtl, n_bdl, n_ccl)


def _silz_load(C, P, st, c, tag):
    csl = slice(c * QC, (c + 1) * QC)
    zkey = f"mamz_{tag}"
    n_zld = P.dma("gpsimd", st.silz_readers + st.zscr_stores,
                  C.m_silz[0][:, :].rearrange("p (c t) -> p c t", t=128),
                  C.m_zscr[:, csl].rearrange("(c p) t -> p c t", p=128),
                  sem_key=zkey)
    st.zscr_stores = []
    st.silz_readers = []
    st.last_zld = n_zld
    return n_zld


def _chunk_prep(C, P, nc, st, c, cc8, n_negA, ld, tag, lds, par):
    war = C.war
    n_dtl, n_bdl, n_ccl = lds
    m_dt, m_C, m_Bd = C.m_dt[par], C.m_C[par], C.m_Bd[par]
    m_cum = C.m_cum[par]
    m_t64a = C.m_t64a[par]
    m_ach = m_t64a
    m_t64b = C.m_t64b[par]
    m_Bst = C.m_Bst[par]
    m_Cst = C.m_Cst[par]
    m_Chl = C.m_Chl[par]
    m_Bhr = C.m_Bhr[par]
    m_expdb = C.m_expdb[par]
    m_de = C.m_de[par]
    pk = str(par)
    pr = Ctx()

    n_a = P.op("vector", [n_dtl, n_negA, war.get("m_t64a" + pk)],
               lambda q: nc.vector.tensor_scalar(
                   out=m_ach[:, :], in0=m_dt[:, :],
                   scalar1=C.m_negA[:, 0:1], scalar2=None, op0=ALU.mult))
    n_cum = P.op("vector", [n_a, war.get("m_cum" + pk)],
                 lambda q: nc.vector.tensor_tensor_scan(
                     out=m_cum[:, :],
                     data0=C.ones128[0:64, 0:1].broadcast_to([64, QC]),
                     data1=m_ach[:, :], initial=0.0,
                     op0=ALU.mult, op1=ALU.add))
    n_de = P.op("scalar", [n_cum] + [war.get("m_de" + pk) or []],
                lambda q: nc.scalar.activation(m_de[:, :],
                                               m_cum[:, 127:128], AF.Exp))
    n_e1 = P.op("vector", [n_cum, war.get("m_t64b" + pk)],
                lambda q: nc.vector.tensor_scalar(
                    out=m_t64b[:, :], in0=m_cum[:, :],
                    scalar1=m_cum[:, 127:128], scalar2=None,
                    op0=ALU.subtract))
    n_e2x = P.op("scalar", [n_e1],
                 lambda q: nc.scalar.activation(m_t64b[:, :], m_t64b[:, :],
                                                AF.Exp, scale=-1.0))
    n_bst = P.op("vector", [n_e2x, n_bdl, war.get("m_Bst" + pk)],
                 lambda q: nc.vector.tensor_tensor(
                     out=m_Bst[:, :], in0=m_Bd[:, :],
                     in1=m_t64b[:, :], op=ALU.mult))
    n_ec = P.op("scalar", [n_cum, n_bst],
                lambda q: nc.scalar.activation(m_t64b[:, :], m_cum[:, :],
                                               AF.Exp))
    n_cst = P.op("vector", [n_ec, n_ccl, war.get("m_Cst" + pk)],
                 lambda q: nc.vector.tensor_tensor(
                     out=m_Cst[:, :], in0=m_C[:, :],
                     in1=m_t64b[:, :], op=ALU.mult))
    war["m_t64b" + pk] = n_cst

    bnd_ap = m_cum[:, 15:127:16]
    n_chl1 = P.op("vector", [n_cum, n_a],
                  lambda q: nc.vector.tensor_tensor(
                      out=m_t64a[:, 0:112].rearrange("p (j t) -> p j t", t=16),
                      in0=m_cum[:, 16:128].rearrange("p (j t) -> p j t", t=16),
                      in1=bnd_ap[:, :, None].broadcast_to([64, 7, 16]),
                      op=ALU.subtract))
    n_chl2 = P.op("scalar", [n_chl1, war.get("m_Chl" + pk)],
                  lambda q: nc.scalar.activation(
                      m_Chl[:, :], m_t64a[:, 0:112], AF.Exp))
    n_chl = P.op("vector", [n_chl2, n_ccl],
                 lambda q: nc.vector.tensor_tensor(
                     out=m_Chl[:, :], in0=m_Chl[:, :],
                     in1=m_C[:, 16:128], op=ALU.mult))
    war["m_t64a" + pk] = n_chl2
    pr.n_a, pr.n_cum, pr.n_de, pr.n_bst = n_a, n_cum, n_de, n_bst
    pr.n_cst, pr.n_chl = n_cst, n_chl
    pr.n_dtl, pr.n_bdl, pr.n_ccl = n_dtl, n_bdl, n_ccl
    return pr


def _scan_chunk(C, P, nc, st, c, cc8, n_negA, ld, tag, e2_war, pr, par,
                n_zld, H):
    csl = slice(c * QC, (c + 1) * QC)
    loc = cc8 * QC
    xcsl = slice(4 + loc, 4 + loc + QC)
    skey = f"mamb{cc8 % 2}_{tag}"
    war = C.war
    m_dt, m_C, m_Bd, m_silz = (C.m_dt[par], C.m_C[par], C.m_Bd[par],
                               C.m_silz[0])
    m_cum = C.m_cum[par]
    m_t64b = C.m_t64b[par]
    m_Bst = C.m_Bst[par]
    m_Cst = C.m_Cst[par]
    m_BhT = C.m_BhT[par]
    m_Chl = C.m_Chl[par]
    m_Bhr = C.m_Bhr[par]
    m_expdb = C.m_expdb[par]
    m_de = C.m_de[par]
    pk = str(par)
    n_cum, n_bst, n_cst, n_de = pr.n_cum, pr.n_bst, pr.n_cst, pr.n_de
    n_chl = pr.n_chl
    n_bdl, n_ccl = pr.n_bdl, pr.n_ccl

    n_trB = P.op("tensor", [n_bst, st.psw[4]],
                 lambda q: nc.tensor.transpose(
                     C.ps[4].bitcast(BF16)[:, 0:64], m_Bst[:, :],
                     C.ident[0:64, 0:64]))
    n_evB = P.op("vector", [n_trB, war.get("m_BhT" + pk)],
                 lambda q: nc.vector.tensor_copy(
                     m_BhT[:, :], C.ps[4].bitcast(BF16)[:, 0:64]))
    st.psw[4] = n_evB
    war["m_BhT" + pk] = n_evB
    xct_nodes = []
    xcT_war = war.get("m_xcT")
    for et in range(NET):
        b = 4 + (et + 1) % 2
        n_tr = P.op("tensor",
                    [st.xc_nodes[et], st.psw[b]] +
                    ([] if et else [xcT_war]),
                    lambda q, et=et, b=b: nc.tensor.transpose(
                        C.ps[b].bitcast(BF16)[:, 0:128],
                        C.m_xc[et][:, xcsl], C.ident[:, :]))
        n_evt = P.op("vector",
                     [n_tr] + ([] if et else st.scr_ready),
                     lambda q, et=et, b=b: nc.vector.tensor_copy(
                         C.m_xcT[:, et * 128:(et + 1) * 128],
                         C.ps[b].bitcast(BF16)[:, 0:128]))
        st.psw[b] = n_evt
        st.xc_readers[et].append(n_tr)
        xct_nodes.append(n_evt)

    dh_nodes = []
    for i in range(4):
        n_dh = P.op("tensor", [n_evB] + xct_nodes[4 * i:4 * i + 4] +
                    [st.psw[i]],
                    lambda q, i=i: nc.tensor.matmul(
                        C.ps[i][0:64, :], m_BhT[:, :],
                        C.m_xcT[:, i * 512:(i + 1) * 512],
                        start=True, stop=True))
        dh_nodes.append(n_dh)

    stairB = C.m_win[0:64, 0:896]
    offs = [0]
    for i in range(1, 8):
        offs.append(offs[-1] + 16 * i)
    sb_prev = [n_cum, pr.n_bdl] + st.stairB_war
    sb_nodes = []
    for i in range(1, 8):
        sb_nodes.append(P.op("vector", sb_prev,
                        lambda q, i=i: nc.vector.tensor_scalar(
                            out=stairB[:, offs[i - 1]:offs[i]],
                            in0=m_cum[:, 0:16 * i],
                            scalar1=m_cum[:, 16 * i - 1:16 * i],
                            scalar2=None, op0=ALU.subtract)))
        sb_prev = []
    n_sbx = P.op("scalar", sb_nodes,
                 lambda q: nc.scalar.activation(
                     stairB[:, 0:896], stairB[:, 0:896], AF.Exp, scale=-1.0))
    sbb_nodes = []
    for i in range(1, 8):
        sbb_nodes.append(P.op("vector", [n_sbx],
                         lambda q, i=i: nc.vector.tensor_tensor(
                             out=stairB[:, offs[i - 1]:offs[i]],
                             in0=stairB[:, offs[i - 1]:offs[i]],
                             in1=m_Bd[:, 0:16 * i], op=ALU.mult)))
    stair_nodes = []
    for i in range(1, 8):
        n_stm = P.op("tensor", [sbb_nodes[i - 1], n_chl, st.psw[7]],
                     lambda q, i=i: nc.tensor.matmul(
                         C.ps[7][0:16 * i, 16 * i:16 * (i + 1)],
                         stairB[:, offs[i - 1]:offs[i]],
                         m_Chl[:, (i - 1) * 16:i * 16],
                         start=True, stop=True))
        stair_nodes.append(n_stm)
    st.stairB_war = stair_nodes

    n_gt0 = P.op("vector", [war.get("m_gt")],
                 lambda q: nc.vector.memset(C.m_gt[:, :], 0.0))
    gt_evs = []
    for i in range(1, 8):
        n_gev = P.op("vector", [stair_nodes[-1], n_gt0],
                     lambda q, i=i: nc.vector.tensor_copy(
                         C.m_gt[0:16 * i, 16 * i:16 * (i + 1)],
                         C.ps[7][0:16 * i, 16 * i:16 * (i + 1)]))
        gt_evs.append(n_gev)
    st.psw[7] = gt_evs

    n_prev = [n_cum, e2_war]
    for s8 in range(8):
        ss = slice(s8 * 16, (s8 + 1) * 16)
        n_prev = [P.op("vector", n_prev,
                       lambda q, s8=s8, ss=ss: nc.vector.scalar_tensor_tensor(
                           out=C.m_e2[:, s8 * 256:(s8 + 1) * 256]
                           .rearrange("p (a b) -> p a b", b=16),
                           in0=m_cum[:, ss][:, :, None]
                           .broadcast_to([64, 16, 16]),
                           scalar=-1.0,
                           in1=m_cum[:, ss][:, None, :]
                           .broadcast_to([64, 16, 16]),
                           op0=ALU.mult, op1=ALU.add))]
    n_b2 = P.op("vector", n_prev + [ld["trimask"]],
                lambda q: nc.vector.scalar_tensor_tensor(
                    out=C.m_e2[:, :].rearrange("p (s ab) -> p s ab", ab=256),
                    in0=C.m_trimask[:, :][:, None, :]
                    .broadcast_to([64, 8, 256]),
                    scalar=1.0,
                    in1=C.m_e2[:, :].rearrange("p (s ab) -> p s ab", ab=256),
                    op0=ALU.mult, op1=ALU.add))
    n_b3 = P.op("scalar", [n_b2],
                lambda q: nc.scalar.activation(C.m_e2[:, :], C.m_e2[:, :],
                                               AF.Exp))
    n_prev = [n_b3, n_ccl]
    for s8 in range(8):
        ss = slice(s8 * 16, (s8 + 1) * 16)
        n_prev = [P.op("vector", n_prev,
                       lambda q, s8=s8, ss=ss: nc.vector.tensor_tensor(
                           out=C.m_e2[:, s8 * 256:(s8 + 1) * 256]
                           .rearrange("p (a b) -> p a b", b=16),
                           in0=C.m_e2[:, s8 * 256:(s8 + 1) * 256]
                           .rearrange("p (a b) -> p a b", b=16),
                           in1=m_C[:, ss][:, None, :]
                           .broadcast_to([64, 16, 16]),
                           op=ALU.mult))]
    n_b5 = P.op("vector", n_prev + [n_bdl],
                lambda q: nc.vector.tensor_tensor(
                    out=C.m_e2[:, :].rearrange("p (sa b) -> p sa b", b=16),
                    in0=C.m_e2[:, :].rearrange("p (sa b) -> p sa b", b=16),
                    in1=m_Bd[:, :][:, :, None].broadcast_to([64, 128, 16]),
                    op=ALU.mult))

    band_ev = []
    red_mms = []
    prev_red = st.psw[4]
    for k in range(4):
        n_red = P.op("tensor", [n_b5, prev_red],
                     lambda q, k=k: nc.tensor.matmul(
                         C.ps[4][0:1, 0:512], C.ones128b[0:64, 0:1],
                         C.m_e2[:, k * 512:(k + 1) * 512],
                         start=True, stop=True))
        n_bev = P.op("vector",
                     [n_red] + (([war.get("m_band")] + st.band_war)
                                if k == 0 else []),
                     lambda q, k=k: nc.vector.tensor_copy(
                         C.m_band[:, k * 512:(k + 1) * 512],
                         C.ps[4][0:1, 0:512]))
        prev_red = n_bev
        band_ev.append(n_bev)
        red_mms.append(n_red)
    st.psw[4] = band_ev[-1]
    war["m_band"] = band_ev[-1]

    n_bd = P.dma("sync", band_ev + [war.get("m_bandscr")],
                 C.m_bandscr[:], C.m_band[0:1, :], sem_key=skey)
    scat_nodes = []
    for i in range(8):
        eng = "sync" if i % 2 == 0 else "scalar"
        n_sc = P.dma(eng, [n_bd, n_gt0],
                     C.m_gt[i * 16:(i + 1) * 16, i * 16:(i + 1) * 16],
                     C.m_bandscr[i * 256:(i + 1) * 256]
                     .rearrange("(a b) -> a b", b=16),
                     sem_key=skey + "abcd"[i % 4])
        scat_nodes.append(n_sc)
    war["m_bandscr"] = scat_nodes
    st.band_nodes += scat_nodes
    gt_ready = gt_evs + scat_nodes

    y_mms = []
    yev_nodes = []
    yt_war = war.get("m_yt")
    for et in range(NET):
        b = 5 + et % 2

        def emit_y(q, et=et, b=b):
            nc.tensor.matmul(C.ps[b][:, 0:128],
                             C.m_hbf[:, et * 128:(et + 1) * 128],
                             m_Cst[:, :], start=True, stop=False)
            return nc.tensor.matmul(C.ps[b][:, 0:128],
                                    C.m_xcT[:, et * 128:(et + 1) * 128],
                                    C.m_gt[:, :], start=False, stop=True)
        n_ymm = P.op("tensor",
                     [n_cst, st.hbf_node, xct_nodes[et], st.psw[b]] + gt_ready,
                     emit_y)
        y_mms.append(n_ymm)
        n_y1 = P.op("vector", [n_ymm, ld["D"], yt_war],
                    lambda q, et=et, b=b: nc.vector.scalar_tensor_tensor(
                        out=C.m_yt[:, :], in0=C.m_xc[et][:, xcsl],
                        scalar=C.m_D[:, et:et + 1], in1=C.ps[b][:, 0:128],
                        op0=ALU.mult, op1=ALU.add))
        st.psw[b] = n_y1
        n_y2 = P.op("vector", [n_y1, n_zld],
                    lambda q, et=et: nc.vector.tensor_tensor(
                        out=C.m_ych[:, et * 128:(et + 1) * 128],
                        in0=C.m_yt[:, :],
                        in1=m_silz[:, et * 128:(et + 1) * 128],
                        op=ALU.mult))
        yt_war = n_y2
        yev_nodes.append(n_y2)
        st.xc_readers[et].append(n_y1)
    st.silz_war = yev_nodes
    war["m_yt"] = yt_war
    war["m_gt"] = y_mms
    war["m_Cst" + pk] = y_mms

    h_upd = []
    for i in range(4):
        n_hu = P.op("vector", [dh_nodes[i], n_de, st.hbf_node] + y_mms,
                    lambda q, i=i: nc.vector.scalar_tensor_tensor(
                        out=C.m_hbf[:, i * 512:(i + 1) * 512],
                        in0=C.m_hbf[:, i * 512:(i + 1) * 512],
                        scalar=m_de[:, 0:1],
                        in1=C.ps[i][0:64, :], op0=ALU.mult, op1=ALU.add))
        h_upd.append(n_hu)
    war["m_de" + pk] = h_upd
    st.hbf_node = h_upd

    wo_evs = []
    wo_mms = []
    for dm in range(NDC):
        b = dm % 4
        reg = (dm // 4) * 128

        def emit_wo(q, dm=dm, b=b, reg=reg):
            last = None
            for ec in range(NET):
                last = nc.tensor.matmul(
                    C.ps[b][:, reg:reg + 128],
                    C.m_woT[:, ec * 1024 + dm * 128:ec * 1024 + (dm + 1) * 128],
                    C.m_ych[:, ec * 128:(ec + 1) * 128],
                    start=(ec == 0), stop=(ec == NET - 1))
            return last
        dep_ps = h_upd[b] if dm < 4 else wo_evs[dm - 4]
        n_wo = P.op("tensor", yev_nodes + [ld["woT"], dep_ps], emit_wo)
        wo_mms.append(n_wo)
        n_xa = P.op("vector", [n_wo] + st.x_nodes[dm],
                    lambda q, dm=dm, b=b, reg=reg: nc.vector.tensor_tensor(
                        out=C.x[dm][:, csl], in0=C.x[dm][:, csl],
                        in1=C.ps[b][:, reg:reg + 128], op=ALU.add))
        wo_evs.append(n_xa)
        st.x_nodes[dm] = [n_xa]
    for i in range(4):
        st.psw[i] = [wo_evs[i], wo_evs[i + 4]]
    war["m_xcT"] = wo_mms
    st.tile_readers[par] += [pr.n_a, n_bst, n_b5, n_cst, n_chl] + sbb_nodes
    st.silz_readers += yev_nodes
    next_zld = n_zld
    if cc8 < 7:
        next_zld = _silz_load(C, P, st, c + 1, tag)
    return red_mms, next_zld


def build_mamba_sublayer_program(debug=False):
    nc = bass.Bass("TRN2", target_bir_lowering=False)
    es = contextlib.ExitStack()
    xin = nc.dram_tensor("xin", [DM, T], F32, kind="ExternalInput").ap()
    xout = nc.dram_tensor("xout", [DM, T], F32, kind="ExternalOutput").ap()
    hconst = declare_const_dram(nc)
    w = declare_mamba_dram(nc, "dm0")
    C = alloc_common(nc, es)
    C.debug = debug
    alloc_mamba_views(C)
    load_consts(C, hconst)
    x_ready = load_x(C, xin)
    xh_nodes, _, _ = build_ln(C, x_ready, [], "ln0")
    x_upd = build_mamba(C, xh_nodes, x_ready, w, "m")
    store_x(C, xout, x_upd)
    C.P.run(es)
    es.close()
    return nc


def mamba_sublayer_inputs(prefix, inputs, xT):
    d = {"xin": xT}
    d.update(_const_host())
    wh = _mamba_host(prefix, inputs)
    return d | {f"dm0_{k}": wh[k] for k in wh}

def alloc_attn_views(C):
    a = ArenaAlloc(C)
    C.at_qk = a.b16(128, 8 * T)
    C.at_wvg = [C.at_qk[:, 0:8 * 1024],
                C.at_qk[:, 8 * 1024:16 * 1024]]
    C.at_woT = a.b16(128, 8 * 1024)
    C.at_tab = a.b16(128, 2 * T)
    C.at_wbuf = a.b16(128, 2 * 1024)
    C.at_sblk = [a.b16(128, TS), a.b16(128, TS)]
    C.at_vbuf = a.b16(128, 2048)
    C.at_vbuf2 = a.b16(128, 2048)
    C.at_osb = a.b16(128, 4 * 1024)
    C.at_qraw = C.at_osb[:, 0:2048]
    C.at_silg = a.b16(128, 1024)
    C.at_gtmp = a.b16(128, 1024)
    C.at_ogT = a.b16(128, 8 * 128)
    C.at_gnw = a.b16(128, 1024)
    C.at_gnb = a.b16(128, 1024)
    C.at_cmask = a.b16(128, 128)
    C.at_rbig = a.b16(128, 128)
    C.at_mean = a.f32(128, 8)
    C.at_var = a.f32(128, 8)
    C.at_rstd = a.f32(128, 8)


def declare_attn_dram(nc, tag):
    w = {}
    for n, cols in (("wqT", 512), ("wkT", 512), ("wvT", 1024), ("wgT", 1024),
                    ("woT", 1024)):
        w[n] = nc.dram_tensor(f"{tag}_{n}", [DM, cols], F32,
                              kind="ExternalInput").ap()
    w["gnw"] = nc.dram_tensor(f"{tag}_gnw", [128, 1024], F32,
                              kind="ExternalInput").ap()
    w["gnb"] = nc.dram_tensor(f"{tag}_gnb", [128, 1024], F32,
                              kind="ExternalInput").ap()
    w["qtab"] = nc.dram_tensor(f"{tag}_qtab", [512, 2 * T], BF16,
                               kind="ExternalInput").ap()
    w["ktab"] = nc.dram_tensor(f"{tag}_ktab", [512, 2 * T], BF16,
                               kind="ExternalInput").ap()
    w["rbig"] = nc.dram_tensor(f"{tag}_rbig", [128, 128], BF16,
                               kind="ExternalInput").ap()
    w["cmask"] = nc.dram_tensor(f"{tag}_cmask", [128, 128], BF16,
                                kind="ExternalInput").ap()
    for n, cols in (("wqT", 512), ("wkT", 512), ("wvT", 1024), ("wgT", 1024),
                    ("woT", 1024)):
        w[n + "c"] = nc.dram_tensor(f"{tag}_{n}c", [128, NDC, cols], BF16,
                                    kind="Internal").ap()
    return w


def _attn_host(prefix, inputs):
    wq = inputs[f"{prefix}_Wq"]
    wk = inputs[f"{prefix}_Wk"]
    wv = inputs[f"{prefix}_Wv"]
    wg = inputs[f"{prefix}_Wg"]
    wo = inputs[f"{prefix}_Wo"]
    gnw = inputs[f"{prefix}_gn_weight"]
    gnb = inputs[f"{prefix}_gn_bias"]
    H, D = 8, 64
    inv_freq = 1.0 / 10000.0 ** (np.arange(0, D, 2, dtype=np.float64) / D)
    t = np.arange(T, dtype=np.float64)
    freqs = t[:, None] * inv_freq[None, :]
    emb = np.concatenate([freqs, freqs], axis=-1)
    cos = np.cos(emb)
    sin = np.sin(emb)
    gamma = 1.0 - 2.0 ** (-5.0 - np.arange(H, dtype=np.float64))
    s = np.log2(gamma)
    qsc = np.exp2(t[:, None] * s[None, :]) / 8.0
    ksc = np.exp2(-t[:, None] * s[None, :])
    qtab = np.zeros((512, 2 * T), dtype=BF)
    ktab = np.zeros((512, 2 * T), dtype=BF)
    for h in range(H):
        r0 = (h % 2) * 64 + (h // 2) * 128
        qtab[r0:r0 + 64, 0:T] = (cos * qsc[:, h:h + 1]).T.astype(BF)
        qtab[r0:r0 + 64, T:2 * T] = (sin * qsc[:, h:h + 1]).T.astype(BF)
        ktab[r0:r0 + 64, 0:T] = (cos * ksc[:, h:h + 1]).T.astype(BF)
        ktab[r0:r0 + 64, T:2 * T] = (sin * ksc[:, h:h + 1]).T.astype(BF)
    rb = np.zeros((128, 128), dtype=np.float64)
    for blk in range(2):
        o = blk * 64
        for m in range(32):
            rb[o + m, o + 32 + m] = -1.0
            rb[o + 32 + m, o + m] = 1.0
    rbigT = rb.T
    cm = (np.arange(128)[:, None] <= np.arange(128)[None, :]).astype(np.float64)
    return {
        "wqT": np.ascontiguousarray(wq.T), "wkT": np.ascontiguousarray(wk.T),
        "wvT": np.ascontiguousarray(wv.T), "wgT": np.ascontiguousarray(wg.T),
        "woT": np.ascontiguousarray(wo.T),
        "gnw": np.ascontiguousarray(np.broadcast_to(gnw[None, :], (128, 1024))),
        "gnb": np.ascontiguousarray(np.broadcast_to(gnb[None, :], (128, 1024))),
        "qtab": qtab, "ktab": ktab,
        "rbig": rbigT.astype(BF), "cmask": cm.astype(BF),
    }


def build_attn(C, xh_ready, x_free, w, tag, cached=False):
    nc, P = C.nc, C.P
    key = f"attw_{tag}"
    vkey = f"attv_{tag}"
    arena_war = C.war.get("arena")
    pc = w.get("pc", []) if cached else []
    vscr = C.m_zscr

    ld = {}
    ld["woT"] = P.dma("gpsimd", [C.war.get("at_woT"), arena_war] + pc,
                      C.at_woT[:, :].rearrange("p (c f) -> p c f", f=1024),
                      w["woTc"] if cached else
                      w["woT"].rearrange("(c p) f -> p c f", p=128),
                      sem_key=key + "o")
    ld["gnw"] = P.dma("gpsimd", [C.war.get("at_gnw"), arena_war],
                      C.at_gnw[:, :], w["gnw"], sem_key=key + "n")
    ld["gnb"] = P.dma("gpsimd", [C.war.get("at_gnb"), arena_war],
                      C.at_gnb[:, :], w["gnb"], sem_key=key + "n")
    ld["rbig"] = P.dma("sync", [C.war.get("at_rbig"), arena_war],
                       C.at_rbig[:, :], w["rbig"], sem_key=key)
    ld["cmask"] = P.dma("sync", [C.war.get("at_cmask"), arena_war],
                        C.at_cmask[:, :], w["cmask"], sem_key=key)

    ld_wv = P.dma("gpsimd", [C.war.get("at_qk"), arena_war] + pc,
                  C.at_wvg[0].rearrange("p (c f) -> p c f", f=1024),
                  w["wvTc"] if cached else
                  w["wvT"].rearrange("(c p) f -> p c f", p=128),
                  sem_key=key + "v")
    ld_wg = P.dma("gpsimd", [C.war.get("at_qk"), arena_war] + pc,
                  C.at_wvg[1].rearrange("p (c f) -> p c f", f=1024),
                  w["wgTc"] if cached else
                  w["wgT"].rearrange("(c p) f -> p c f", p=128),
                  sem_key=key + "g")
    vg_mms = []
    stage_war = [C.war.get("at_vbuf"), arena_war]
    vstore = []
    psw = [C.war.get(f"ps{i}") for i in range(8)]
    for tt in range(NTT):
        ttl = slice(tt * 128, (tt + 1) * 128)
        evs = []
        for half in range(2):
            hs = slice(half * TS, (half + 1) * TS)

            def emit_v(q, ttl=ttl, hs=hs, half=half):
                last = None
                for dc in range(NDC):
                    last = nc.tensor.matmul(
                        C.ps[half][:, :], C.xh[dc][:, ttl],
                        C.at_wvg[0][:, :].rearrange("p (c f) -> p c f",
                                                    f=1024)[:, dc, hs],
                        start=(dc == 0), stop=(dc == NDC - 1))
                return last
            n_vm = P.op("tensor", [ld_wv, psw[half]] + xh_ready, emit_v)

            def emit_g(q, ttl=ttl, hs=hs, half=half):
                last = None
                for dc in range(NDC):
                    last = nc.tensor.matmul(
                        C.ps[2 + half][:, :], C.xh[dc][:, ttl],
                        C.at_wvg[1][:, :].rearrange("p (c f) -> p c f",
                                                    f=1024)[:, dc, hs],
                        start=(dc == 0), stop=(dc == NDC - 1))
                return last
            n_gm = P.op("tensor", [ld_wg, psw[2 + half]] + xh_ready, emit_g)
            n_ve = P.op("scalar", [n_vm] + stage_war,
                        lambda q, hs=hs, half=half: nc.scalar.activation(
                            C.at_vbuf[:, hs], C.ps[half][:, :], AF.Copy))
            n_ge = P.op("scalar", [n_gm] + stage_war,
                        lambda q, hs=hs, half=half: nc.scalar.activation(
                            C.at_vbuf[:, 1024 + hs.start:1024 + hs.stop],
                            C.ps[2 + half][:, :], AF.Silu))
            psw[half] = n_ve
            psw[2 + half] = n_ge
            evs += [n_ve, n_ge]
            vg_mms += [n_vm, n_gm]
        n_st = P.dma("sync", evs + [C.war.get("zscr")], vscr[ttl, :],
                     C.at_vbuf[:, :], sem_key=vkey + "s")
        stage_war = [n_st]
        vstore.append(n_st)

    qk_nodes = []
    tab_war = [C.war.get("at_tab"), arena_war]
    wbuf_war = [C.war.get("at_wbuf"), arena_war]
    qraw_war = [arena_war] + vg_mms
    tmp2k_war = [stage_war[0]]
    qk_war = [ld_wv, ld_wg] + vg_mms
    for ft in range(8):
        isq = ft < 4
        fc = ft % 4
        slot = ft % 2
        if cached:
            wsrcc = w["wqTc"] if isq else w["wkTc"]
            n_wb = P.dma("gpsimd", [wbuf_war[slot]] + pc,
                         C.at_wbuf[:, slot * 1024:(slot + 1) * 1024]
                         .rearrange("p (c f) -> p c f", f=128),
                         wsrcc[:, :, fc * 128:(fc + 1) * 128],
                         sem_key=key + str(slot))
        else:
            wsrc = w["wqT"] if isq else w["wkT"]
            n_wb = P.dma("gpsimd", [wbuf_war[slot]],
                         C.at_wbuf[:, slot * 1024:(slot + 1) * 1024]
                         .rearrange("p (c f) -> p c f", f=128),
                         wsrc[:, fc * 128:(fc + 1) * 128]
                         .rearrange("(c p) f -> p c f", p=128),
                         sem_key=key + str(slot))
        n_tb = P.dma("sync", tab_war,
                     C.at_tab[:, :],
                     (w["qtab"] if isq else w["ktab"])[fc * 128:(fc + 1) * 128, :],
                     sem_key=key)
        raw_evs = []
        rot_mms = []
        for ts4 in range(4):
            tsl = slice(ts4 * TS, (ts4 + 1) * TS)
            b = ts4 % 2

            def emit_p(q, slot=slot, tsl=tsl, b=b):
                last = None
                for dc in range(NDC):
                    last = nc.tensor.matmul(
                        C.ps[b][:, :],
                        C.at_wbuf[:, slot * 1024 + dc * 128:
                                  slot * 1024 + (dc + 1) * 128],
                        C.xh[dc][:, tsl], start=(dc == 0), stop=(dc == NDC - 1))
                return last
            n_pm = P.op("tensor", [n_wb, psw[b]] + xh_ready, emit_p)
            n_pe = P.op("scalar", [n_pm] + (qraw_war if ts4 == 0 else []),
                        lambda q, tsl=tsl, b=b: nc.scalar.activation(
                            C.at_qraw[:, tsl], C.ps[b][:, :], AF.Copy))
            psw[b] = n_pe
            raw_evs.append(n_pe)
            n_rm = P.op("tensor", [n_pe, ld["rbig"], psw[4 + b]],
                        lambda q, tsl=tsl, b=b: nc.tensor.matmul(
                            C.ps[4 + b][:, :],
                            C.at_rbig[:, :], C.at_qraw[:, tsl],
                            start=True, stop=True))
            rot_mms.append(n_rm)
            n_re = P.op("vector", [n_rm, n_tb] + (tmp2k_war if ts4 == 0 else []),
                        lambda q, tsl=tsl, b=b: nc.vector.tensor_tensor(
                            out=C.at_vbuf[:, tsl],
                            in0=C.ps[4 + b][:, :],
                            in1=C.at_tab[:, T + tsl.start:T + tsl.stop],
                            op=ALU.mult))
            psw[4 + b] = n_re
            rot_mms[-1] = n_re
        n_t1 = P.op("vector", raw_evs + [n_tb] + qk_war,
                    lambda q, ft=ft: nc.vector.tensor_tensor(
                        out=C.at_qk[:, ft * T:(ft + 1) * T],
                        in0=C.at_qraw[:, :], in1=C.at_tab[:, 0:T],
                        op=ALU.mult))
        qraw_war = [n_t1]
        n_t2 = P.op("vector", [n_t1] + rot_mms,
                    lambda q, ft=ft: nc.vector.tensor_tensor(
                        out=C.at_qk[:, ft * T:(ft + 1) * T],
                        in0=C.at_qk[:, ft * T:(ft + 1) * T],
                        in1=C.at_vbuf[:, :], op=ALU.add))
        tmp2k_war = [n_t2]
        tab_war = [n_t1] + rot_mms
        wbuf_war[slot] = n_pm
        qk_nodes.append(n_t2)
        qk_war = []

    x_nodes = [list(xf) if isinstance(xf, list) else ([xf] if xf else [])
               for xf in x_free]
    vbuf_war = tmp2k_war + [arena_war]
    silg_war = [C.war.get("at_silg"), arena_war]
    osb_war = qraw_war
    ogT_war = [C.war.get("at_ogT"), arena_war]
    gtmp_war = [C.war.get("at_gtmp"), arena_war]
    sblk_war = [[arena_war], [arena_war]]
    mean_war = [C.war.get("at_mean"), arena_war]

    vbuf_wars = [vbuf_war, [w for w in vbuf_war]]
    vbufs = [C.at_vbuf, C.at_vbuf2]
    pending_vl = None

    def issue_vl(qg_, h_, par_):
        n_kc_ = 4 * qg_ + 4
        n = P.dma("gpsimd", vbuf_wars[par_] + vstore,
                  vbufs[par_][:, 0:n_kc_ * 128]
                  .rearrange("p (kc f) -> p kc f", f=128),
                  vscr[0:n_kc_ * 128, h_ * 128:(h_ + 1) * 128]
                  .rearrange("(kc p) f -> p kc f", p=128),
                  sem_key=vkey)
        vbuf_wars[par_] = []
        return n
    pending_vl = issue_vl(0, 0, 0)
    for qg in range(4):
        n_kc = 4 * qg + 4
        o_evs = []
        for h in range(8):
            par = (qg * 8 + h) % 2
            at_vbuf = vbufs[par]
            qt = C.at_qk[:, (h // 2) * T:(h // 2 + 1) * T]
            kt = C.at_qk[:, (4 + h // 2) * T:(4 + h // 2 + 1) * T]
            r0 = (h % 2) * 64
            n_vl = pending_vl
            if not (qg == 3 and h == 7):
                nh = qg * 8 + h + 1
                pending_vl = issue_vl(nh // 8, nh % 8, (par + 1) % 2)
            o_mms_all = []
            o_mm_last = [None] * 4
            for kc in range(n_kc):
                b = kc % 2
                n_sm = P.op("tensor", [qk_nodes[h // 2], qk_nodes[4 + h // 2],
                                       psw[b]] + sblk_war[b],
                            lambda q, kc=kc, b=b, qt=qt, kt=kt, r0=r0, qg=qg:
                            nc.tensor.matmul(
                                C.ps[b][:, :],
                                kt[r0:r0 + 64, kc * 128:(kc + 1) * 128],
                                qt[r0:r0 + 64, qg * TS:(qg + 1) * TS],
                                start=True, stop=True))
                n_se = P.op("scalar", [n_sm] + sblk_war[b],
                            lambda q, b=b: nc.scalar.activation(
                                C.at_sblk[b][:, :], C.ps[b][:, :], AF.Copy))
                psw[b] = n_se
                jd = kc - 4 * qg
                if 0 <= jd <= 3:
                    n_se = P.op("vector", [n_se, ld["cmask"]],
                                lambda q, b=b, jd=jd:
                                nc.vector.tensor_tensor(
                                    out=C.at_sblk[b][:, jd * 128:(jd + 1) * 128],
                                    in0=C.at_sblk[b][:, jd * 128:(jd + 1) * 128],
                                    in1=C.at_cmask[:, :], op=ALU.mult))
                o_deps = [n_se, n_vl]
                sb_readers = []
                for j in range(4):
                    if 4 * qg + j < kc:
                        continue
                    n_om = P.op("tensor",
                                o_deps + [o_mm_last[j] or psw[4 + j]],
                                lambda q, kc=kc, b=b, j=j, at_vbuf=at_vbuf:
                                nc.tensor.matmul(
                                    C.ps[4 + j][:, 0:128],
                                    C.at_sblk[b][:, j * 128:(j + 1) * 128],
                                    at_vbuf[:, kc * 128:(kc + 1) * 128],
                                    start=(kc == 0), stop=(kc == 4 * qg + j)))
                    o_mm_last[j] = n_om
                    sb_readers.append(n_om)
                    o_mms_all.append(n_om)
                sblk_war[b] = sb_readers
            for j in range(4):
                n_oe = P.op("scalar",
                            [o_mm_last[j]] + (osb_war if (h == 0 and j == 0)
                                              else []),
                            lambda q, h=h, j=j: nc.scalar.activation(
                                C.at_osb[:, j * 1024 + h * 128:
                                         j * 1024 + (h + 1) * 128],
                                C.ps[4 + j][:, 0:128], AF.Copy))
                psw[4 + j] = n_oe
                o_evs.append(n_oe)
            vbuf_wars[par] = o_mms_all
        osb_war = []
        osb_readers = []

        for j in range(4):
            tt = qg * 4 + j
            ttl = slice(tt * 128, (tt + 1) * 128)
            osl = C.at_osb[:, j * 1024:(j + 1) * 1024]
            n_sg = P.dma("gpsimd", silg_war + vstore, C.at_silg[:, :],
                         vscr[ttl, 1024:2048], sem_key=vkey)
            n_mn = P.op("vector", o_evs + mean_war,
                        lambda q, osl=osl: nc.vector.tensor_reduce(
                            out=C.at_mean[:, :],
                            in_=osl.rearrange("p (h v) -> p h v", v=128),
                            axis=AX.X, op=ALU.add))
            n_sq = P.op("scalar", o_evs + gtmp_war,
                        lambda q, osl=osl: nc.scalar.activation(
                            C.at_gtmp[:, :], osl, AF.Square))
            n_vr = P.op("vector", [n_sq, C.war.get("at_var")],
                        lambda q: nc.vector.tensor_reduce(
                            out=C.at_var[:, :],
                            in_=C.at_gtmp[:, :].rearrange(
                                "p (h v) -> p h v", v=128),
                            axis=AX.X, op=ALU.add))
            n_mn2 = P.op("vector", [n_mn],
                         lambda q: nc.vector.tensor_scalar(
                             out=C.at_mean[:, :], in0=C.at_mean[:, :],
                             scalar1=1.0 / 128.0, scalar2=None, op0=ALU.mult))
            n_vr2 = P.op("vector", [n_vr, n_mn2, C.war.get("at_rstd")],
                         lambda q: nc.vector.tensor_tensor(
                             out=C.at_rstd[:, :], in0=C.at_mean[:, :],
                             in1=C.at_mean[:, :], op=ALU.mult))
            n_vr3 = P.op("vector", [n_vr2],
                         lambda q: nc.vector.scalar_tensor_tensor(
                             out=C.at_var[:, :], in0=C.at_var[:, :],
                             scalar=1.0 / 128.0, in1=C.at_rstd[:, :],
                             op0=ALU.mult, op1=ALU.subtract))
            n_r1 = P.op("scalar", [n_vr3] + C.const_nodes,
                        lambda q: nc.scalar.activation(
                            C.at_var[:, :], C.at_var[:, :], AF.Ln, bias=1e-5))
            n_r2 = P.op("scalar", [n_r1],
                        lambda q: nc.scalar.activation(
                            C.at_rstd[:, :], C.at_var[:, :], AF.Exp,
                            scale=-0.5))
            mean_war = [n_vr2]
            C.war["at_var"] = n_r2
            C.war["at_rstd"] = n_r2
            osb_readers_j = []
            n_n1 = P.op("vector", [n_mn2, n_sq] + o_evs,
                        lambda q, osl=osl: nc.vector.tensor_tensor(
                            out=C.at_gtmp[:, :].rearrange(
                                "p (h v) -> p h v", v=128),
                            in0=osl.rearrange("p (h v) -> p h v", v=128),
                            in1=C.at_mean[:, :, None]
                            .broadcast_to([128, 8, 128]),
                            op=ALU.subtract))
            n_n2 = P.op("vector", [n_n1, n_r2],
                        lambda q: nc.vector.tensor_tensor(
                            out=C.at_gtmp[:, :].rearrange(
                                "p (h v) -> p h v", v=128),
                            in0=C.at_gtmp[:, :].rearrange(
                                "p (h v) -> p h v", v=128),
                            in1=C.at_rstd[:, :, None]
                            .broadcast_to([128, 8, 128]),
                            op=ALU.mult))
            n_n3 = P.op("vector", [n_n2, ld["gnw"]],
                        lambda q: nc.vector.tensor_tensor(
                            out=C.at_gtmp[:, :], in0=C.at_gtmp[:, :],
                            in1=C.at_gnw[:, :], op=ALU.mult))
            n_n4 = P.op("vector", [n_n3, ld["gnb"]],
                        lambda q: nc.vector.tensor_tensor(
                            out=C.at_gtmp[:, :], in0=C.at_gtmp[:, :],
                            in1=C.at_gnb[:, :], op=ALU.add))
            n_n5 = P.op("vector", [n_n4, n_sg],
                        lambda q: nc.vector.tensor_tensor(
                            out=C.at_gtmp[:, :], in0=C.at_gtmp[:, :],
                            in1=C.at_silg[:, :], op=ALU.mult))
            silg_war = [n_n5]
            gtmp_war = [n_n5]
            osb_readers.append(n_n1)
            tr_evs = []
            for hc in range(8):
                b = hc % 2
                n_tr = P.op("tensor", [n_n5, psw[b]],
                            lambda q, hc=hc, b=b: nc.tensor.transpose(
                                C.ps[b].bitcast(BF16)[:, 0:128],
                                C.at_gtmp[:, hc * 128:(hc + 1) * 128],
                                C.ident[:, :]))
                n_te = P.op("vector", [n_tr] + (ogT_war if hc == 0 else []),
                            lambda q, hc=hc, b=b: nc.vector.tensor_copy(
                                C.at_ogT[:, hc * 128:(hc + 1) * 128],
                                C.ps[b].bitcast(BF16)[:, 0:128]))
                psw[b] = n_te
                tr_evs.append(n_te)
            gtmp_war = tr_evs
            ogT_war = []
            wo_evs = []
            for dm in range(NDC):
                b = 2 + dm % 2

                def emit_wo(q, dm=dm, b=b):
                    last = None
                    for hc in range(8):
                        last = nc.tensor.matmul(
                            C.ps[b][:, 0:128],
                            C.at_woT[:, hc * 1024 + dm * 128:
                                     hc * 1024 + (dm + 1) * 128],
                            C.at_ogT[:, hc * 128:(hc + 1) * 128],
                            start=(hc == 0), stop=(hc == 7))
                    return last
                n_wm = P.op("tensor",
                            tr_evs + [ld["woT"],
                                      wo_evs[dm - 2] if dm >= 2 else psw[b]],
                            emit_wo)
                n_xa = P.op("vector", [n_wm] + x_nodes[dm],
                            lambda q, dm=dm, b=b, ttl=ttl:
                            nc.vector.tensor_tensor(
                                out=C.x[dm][:, ttl], in0=C.x[dm][:, ttl],
                                in1=C.ps[b][:, 0:128], op=ALU.add))
                wo_evs.append(n_xa)
                x_nodes[dm] = [n_xa]
            ogT_war = [w for w in wo_evs]
            for dm in range(NDC):
                b = 2 + dm % 2
                psw[b] = wo_evs[dm]
        osb_war = osb_readers

    for i in range(8):
        C.war[f"ps{i}"] = psw[i]
    allc = [n for dc in range(NDC) for n in x_nodes[dc]]
    C.war["arena"] = allc
    return [x_nodes[dc] for dc in range(NDC)]



def build_attn_sublayer_program(debug=False):
    nc = bass.Bass("TRN2", target_bir_lowering=False)
    es = contextlib.ExitStack()
    xin = nc.dram_tensor("xin", [DM, T], F32, kind="ExternalInput").ap()
    xout = nc.dram_tensor("xout", [DM, T], F32, kind="ExternalOutput").ap()
    hconst = declare_const_dram(nc)
    w = declare_attn_dram(nc, "da1")
    C = alloc_common(nc, es)
    C.debug = debug
    alloc_attn_views(C)
    zkind = "ExternalOutput" if debug else "Internal"
    C.m_zscr = nc.dram_tensor("m_zscr", [DI, T], BF16, kind=zkind).ap()
    load_consts(C, hconst)
    x_ready = load_x(C, xin)
    xh_nodes, _, _ = build_ln(C, x_ready, [], "ln0")
    x_upd = build_attn(C, xh_nodes, x_ready, w, "a1")
    if debug:
        dbg_qk = nc.dram_tensor("dbg_qk", [128, 8 * T], BF16,
                                kind="ExternalOutput").ap()
        C.P.dma("gpsimd", [C.war.get("arena")], dbg_qk, C.at_qk[:, :],
                sem_key="dbgq")
        dbg_osb = nc.dram_tensor("dbg_osb", [128, 4 * 1024], BF16,
                                 kind="ExternalOutput").ap()
        C.P.dma("gpsimd", [C.war.get("arena")], dbg_osb, C.at_osb[:, :],
                sem_key="dbgq")
    store_x(C, xout, x_upd)
    C.P.run(es)
    es.close()
    return nc


def attn_sublayer_inputs(prefix, inputs, xT):
    d = {"xin": xT}
    d.update(_const_host())
    wh = _attn_host(prefix, inputs)
    return d | {f"da1_{k}": wh[k] for k in wh}
